# version 07
# baseline (speedup 1.0000x reference)
import numpy as np
import ml_dtypes
import concourse.bass as bass
import concourse.mybir as mybir
from concourse.bass_utils import run_bass_kernel_spmd
from contextlib import ExitStack

F32 = mybir.dt.float32
BF16 = mybir.dt.bfloat16
F16 = mybir.dt.float16
I32 = mybir.dt.int32
AF = mybir.ActivationFunctionType
ALU = mybir.AluOpType
AX = mybir.AxisListType

B, T, D = 16, 2048, 2048
NCORES = 8
BPC = B // NCORES
HEADS, HD = 16, 128
KVC, QC = 512, 1024
FFH = 5120
I1 = 5632
NT = T // 128
NDC = D // 128

ALL_PHASES = ("ln0", "tstat", "ffn0h", "ffn0p", "attn", "comb0",
              "kvr", "wkv", "wo", "dq", "gu", "down")


class S:
    def __init__(self, ctx, nc, name):
        self.h = ctx.enter_context(nc.semaphore(name))
        self.n = 0

    def tick(self, k=1):
        self.n += k
        return self.n


class Env:
    pass


class Chained:

    def __init__(self, eng, s):
        self._eng = eng
        self._s = s

    def wait_ge(self, h, n):
        self._eng.wait_ge(h, n)

    def __getattr__(self, name):
        f = getattr(self._eng, name)

        def wrapped(*a, **kw):
            self._eng.wait_ge(self._s.h, self._s.n)
            instr = f(*a, **kw)
            instr.then_inc(self._s.h, 1)
            self._s.tick()
            return instr
        return wrapped


def bar_of(e):
    return [(s, s.n) for s in (e.s_pe, e.s_dve, e.s_act, e.s_gp)
            + tuple(e.LX) + tuple(e.LW) + tuple(e.STT) + tuple(e.STX)]


def wait_bar(eng, bar):
    for s, n in bar:
        if n:
            eng.wait_ge(s.h, n)



def emit_ln(e, block, src_x_ap, dst_T_ap, dst_x_ap=None, s0g0=None, tag="ln"):
    nc = e.nc
    comb = s0g0 is not None
    ntiles = BPC * NT
    LD = 3 if comb else 1
    DVE = 41 if comb else 23
    PE = 32 if comb else 16
    ACT = 2
    STD = 17 if comb else 16
    dve0, pe0, act0 = e.s_dve.n, e.s_pe.n, e.s_act.n
    lx0 = [e.LX[0].n, e.LX[1].n]
    st0 = [e.STT[0].n, e.STT[1].n]

    def ld(i):      return lx0[i % 2] + 16 * LD * (i // 2 + 1)
    def dve(i, k):  return dve0 + DVE * i + k
    def pe(i, k):   return pe0 + PE * i + k
    def act(i, k):  return act0 + ACT * i + k
    def st(i):      return st0[i % 2] + 16 * STD * (i // 2 + 1)

    bar = bar_of(e)

    with ExitStack() as ctx:
        def sb(name, shape, dt=F32):
            return ctx.enter_context(nc.sbuf_tensor(tag + name, shape, dt))
        xt = [sb(f"x{i}", [128, D]) for i in range(2)]
        xn = [sb(f"xn{i}", [128, D], BF16) for i in range(2)]
        stage = [sb(f"stg{i}", [128, NDC, 128], BF16) for i in range(2)]
        junk = sb("junk", [128, D], BF16)
        red = [sb(f"red{i}", [128, 8]) for i in range(2)]
        if comb:
            s_ = [sb(f"s{i}", [128, NDC, 128]) for i in range(2)]
            g_ = [sb(f"g{i}", [128, NDC, 128]) for i in range(2)]
            sgT = [sb(f"sgT{i}", [128, NDC, 128]) for i in range(2)]
            sgtok = [sb(f"sgtok{i}", [128, D]) for i in range(2)]
            psf = ctx.enter_context(nc.psum_tensor(tag + "psf", [128, 2048], F32))
        pst = [ctx.enter_context(nc.psum_tensor(tag + f"pst{i}", [128, 2048], BF16))
               for i in range(2)]

        @block.sync
        def _(sync):
            wait_bar(sync, bar)
            for i in range(ntiles):
                b, tt = divmod(i, NT)
                if i >= 2:
                    sync.wait_ge(e.s_dve.h, dve(i - 2, DVE))
                    sync.wait_ge(e.s_act.h, act(i - 2, ACT))
                    if comb:
                        sync.wait_ge(e.STT[i % 2].h, st(i - 2))
                q = e.LX[i % 2].h
                sync.dma_start(xt[i % 2][:],
                               src_x_ap[b, tt * 128:(tt + 1) * 128, :]).then_inc(q, 16)
                if comb:
                    sync.dma_start(s_[i % 2][:],
                                   s0g0[0][b, :, :, tt * 128:(tt + 1) * 128]
                                   .transpose([1, 0, 2])).then_inc(q, 16)
                    sync.dma_start(g_[i % 2][:],
                                   s0g0[1][b, :, :, tt * 128:(tt + 1) * 128]
                                   .transpose([1, 0, 2])).then_inc(q, 16)
        e.LX[0].tick(16 * LD * ((ntiles + 1) // 2))
        e.LX[1].tick(16 * LD * (ntiles // 2))

        @block.tensor
        def _(tensor):
            wait_bar(tensor, bar)
            for i in range(ntiles):
                p = i % 2
                if comb:
                    tensor.wait_ge(e.s_dve.h, dve(i, 1))
                    if i >= 1:
                        tensor.wait_ge(e.s_dve.h, dve(i - 1, 17))
                    for c in range(NDC):
                        tensor.transpose(psf[:, c * 128:(c + 1) * 128],
                                         sgT[p][:, c, :], e.idf[:]
                                         ).then_inc(e.s_pe.h, 1)
                tensor.wait_ge(e.s_dve.h, dve(i, DVE - 16))
                if i >= 2:
                    tensor.wait_ge(e.s_dve.h, dve(i - 2, DVE))
                for c in range(NDC):
                    tensor.transpose(pst[p][:, c * 128:(c + 1) * 128],
                                     xn[p][:, c * 128:(c + 1) * 128],
                                     e.idb[:]).then_inc(e.s_pe.h, 1)
        e.s_pe.tick(PE * ntiles)

        @block.scalar
        def _(scalar):
            scalar = Chained(scalar, e.s_act)
            wait_bar(scalar, bar)
            for i in range(ntiles):
                p = i % 2
                scalar.wait_ge(e.LX[p].h, ld(i))
                if i >= 2:
                    scalar.wait_ge(e.s_dve.h, dve(i - 2, DVE - 16))
                if comb:
                    scalar.wait_ge(e.s_dve.h, dve(i, 18))
                scalar.activation(junk[:], xt[p][:], AF.Square,
                                  accum_out=red[p][:, 1:2])
                scalar.wait_ge(e.s_dve.h, dve(i, DVE - 18))
                scalar.activation(red[p][:, 5:6], red[p][:, 3:4], AF.Sqrt
                                  )

        @block.vector
        def _(vector):
            vector = Chained(vector, e.s_dve)
            wait_bar(vector, bar)
            for i in range(ntiles):
                p = i % 2
                k = 0
                vector.wait_ge(e.LX[p].h, ld(i))
                if comb:
                    vector.tensor_tensor(sgT[p][:], s_[p][:], g_[p][:],
                                         op=ALU.mult)
                    k += 1
                    vector.wait_ge(e.s_pe.h, pe(i, 16))
                    for c in range(NDC):
                        vector.tensor_copy(sgtok[p][:, c * 128:(c + 1) * 128],
                                           psf[:, c * 128:(c + 1) * 128]
                                           )
                        k += 1
                    vector.wait_ge(e.s_dve.h, dve(i, k))
                    vector.tensor_tensor(xt[p][:], xt[p][:], sgtok[p][:],
                                         op=ALU.add)
                    k += 1
                    vector.wait_ge(e.s_dve.h, dve(i, k))
                vector.reduce_sum(red[p][:, 0:1], xt[p][:], axis=AX.X
                                  )
                k += 1
                vector.wait_ge(e.s_dve.h, dve(i, k))
                vector.tensor_scalar(red[p][:, 2:3], red[p][:, 0:1], -1.0 / D, None,
                                     op0=ALU.mult)
                k += 1
                vector.wait_ge(e.s_act.h, act(i, 1))
                vector.tensor_scalar(red[p][:, 3:4], red[p][:, 1:2], 1.0 / D, 1e-5,
                                     op0=ALU.mult, op1=ALU.add)
                k += 1
                vector.wait_ge(e.s_dve.h, dve(i, k))
                vector.tensor_tensor(red[p][:, 4:5], red[p][:, 2:3], red[p][:, 2:3],
                                     op=ALU.mult)
                k += 1
                vector.wait_ge(e.s_dve.h, dve(i, k))
                vector.tensor_tensor(red[p][:, 3:4], red[p][:, 3:4], red[p][:, 4:5],
                                     op=ALU.subtract)
                k += 1
                vector.wait_ge(e.s_act.h, act(i, 2))
                vector.reciprocal(red[p][:, 6:7], red[p][:, 5:6])
                k += 1
                vector.wait_ge(e.s_dve.h, dve(i, k))
                vector.tensor_scalar(xn[p][:], xt[p][:], red[p][:, 2:3], red[p][:, 6:7],
                                     op0=ALU.add, op1=ALU.mult)
                k += 1
                vector.wait_ge(e.s_pe.h, pe(i, PE))
                if i >= 2:
                    vector.wait_ge(e.STT[p].h, st(i - 2))
                for c in range(NDC):
                    vector.tensor_copy(stage[p][:, c, :],
                                       pst[p][:, c * 128:(c + 1) * 128]
                                       )
                    k += 1
                assert k == DVE, (k, DVE)

        @block.gpsimd
        def _(gpsimd):
            wait_bar(gpsimd, bar)
            for i in range(ntiles):
                b, tt = divmod(i, NT)
                p = i % 2
                gpsimd.wait_ge(e.s_dve.h, dve(i, DVE))
                q = e.STT[p].h
                if comb:
                    gpsimd.dma_start(dst_x_ap[b, tt * 128:(tt + 1) * 128, :],
                                     xt[p][:]).then_inc(q, 16)
                for c in range(NDC):
                    gpsimd.dma_start(dst_T_ap[b, c, :, tt * 128:(tt + 1) * 128],
                                     stage[p][:, c, :]).then_inc(q, 16)
        e.STT[0].tick(16 * STD * ((ntiles + 1) // 2))
        e.STT[1].tick(16 * STD * (ntiles // 2))



def emit_tstat(e, block):
    nc = e.nc
    bar = bar_of(e)
    dve0, pe0, act0 = e.s_dve.n, e.s_pe.n, e.s_act.n
    lw0 = [e.LW[0].n, e.LW[1].n]
    NTI = 40 + 16

    def lw(i):  return lw0[i % 2] + 16 * (i // 2 + 1)

    with ExitStack() as ctx:
        def sb(name, shape, dt=F32):
            return ctx.enter_context(nc.sbuf_tensor("ts" + name, shape, dt))
        wt = [sb(f"w{i}", [128, FFH]) for i in range(2)]
        junk = sb("junk", [128, FFH], BF16)
        acc = sb("acc", [128, 56])
        tot = sb("tot", [128, 2])
        sc = sb("sc", [1, 8])
        ps1 = ctx.enter_context(nc.psum_tensor("tsps1", [1, 8], F32))
        ps2 = ctx.enter_context(nc.psum_tensor("tsps2", [128, 8], F32))

        @block.gpsimd
        def _(gp):
            wait_bar(gp, bar)
            for i in range(NTI):
                if i >= 2:
                    gp.wait_ge(e.s_act.h, act0 + i - 1)
                if i < 40:
                    src = e.w_fc[i * 128:(i + 1) * 128, :]
                else:
                    src = e.w_proj[(i - 40) * 128:(i - 39) * 128, :]
                gp.dma_start(wt[i % 2][:, :src.shape[1]], src
                             ).then_inc(e.LW[i % 2].h, 16)
        e.LW[0].tick(16 * ((NTI + 1) // 2))
        e.LW[1].tick(16 * (NTI // 2))

        @block.scalar
        def _(scalar):
            scalar = Chained(scalar, e.s_act)
            wait_bar(scalar, bar)
            for i in range(NTI):
                w = 2048 if i < 40 else FFH
                scalar.wait_ge(e.LW[i % 2].h, lw(i))
                scalar.activation(junk[:, :w], wt[i % 2][:, :w], AF.Abs,
                                  accum_out=acc[:, i:i + 1])

        @block.vector
        def _(vector):
            vector = Chained(vector, e.s_dve)
            wait_bar(vector, bar)
            vector.wait_ge(e.s_act.h, act0 + NTI)
            vector.reduce_sum(tot[:, 0:1], acc[:, 0:40], axis=AX.X)
            vector.reduce_sum(tot[:, 1:2], acc[:, 40:56], axis=AX.X)
            vector.wait_ge(e.s_pe.h, pe0 + 1)
            vector.tensor_scalar(sc[0:1, 0:1], ps1[0:1, 0:1], 1.0 / (FFH * D), 1e-5,
                                 op0=ALU.mult, op1=ALU.max)
            vector.tensor_scalar(sc[0:1, 2:3], ps1[0:1, 1:2], 1.0 / (FFH * D), 1e-5,
                                 op0=ALU.mult, op1=ALU.max)
            vector.wait_ge(e.s_dve.h, dve0 + 4)
            vector.tensor_scalar(sc[0:1, 1:2], sc[0:1, 0:1], 0.7, None,
                                 op0=ALU.mult)
            vector.tensor_scalar(sc[0:1, 3:4], sc[0:1, 2:3], 0.7, None,
                                 op0=ALU.mult)
            vector.tensor_tensor(sc[0:1, 5:6], sc[0:1, 0:1], sc[0:1, 0:1],
                                 op=ALU.mult)
            vector.wait_ge(e.s_dve.h, dve0 + 7)
            vector.tensor_tensor(sc[0:1, 4:5], sc[0:1, 5:6], sc[0:1, 2:3],
                                 op=ALU.mult)
            vector.wait_ge(e.s_pe.h, pe0 + 2)
            vector.tensor_copy(e.thrfc[:], ps2[:, 1:2])
            vector.tensor_copy(e.thrpj[:], ps2[:, 3:4])
            vector.tensor_copy(e.amtot[:], ps2[:, 4:5])
            vector.wait_ge(e.s_dve.h, dve0 + 11)
            vector.tensor_scalar(e.nthrfc[:], e.thrfc[:], -1.0, None,
                                 op0=ALU.mult)
            vector.tensor_scalar(e.nthrpj[:], e.thrpj[:], -1.0, None,
                                 op0=ALU.mult)

        @block.tensor
        def _(tensor):
            wait_bar(tensor, bar)
            tensor.wait_ge(e.s_dve.h, dve0 + 2)
            tensor.matmul(ps1[0:1, 0:2], e.onescol[:], tot[:, 0:2],
                          start=True, stop=True).then_inc(e.s_pe.h, 1)
            tensor.wait_ge(e.s_dve.h, dve0 + 8)
            tensor.matmul(ps2[:, 0:6], e.onesrow[:], sc[0:1, 0:6],
                          start=True, stop=True).then_inc(e.s_pe.h, 1)
        e.s_pe.tick(2)

        @block.gpsimd
        def _(gp):
            gp.wait_ge(e.s_dve.h, dve0 + 13)
            gp.dma_start(e.tern[0:1, 0:6], sc[0:1, 0:6]).then_inc(e.STT[0].h, 16)
        e.STT[0].tick(16)



def emit_ffn0h(e, block):
    nc = e.nc
    bar = bar_of(e)
    NO = FFH // 128
    NSUB = BPC * 4
    DVE = 4 + NSUB
    PE = NDC + NSUB * NDC
    ACT = NSUB
    dve0, pe0, act0 = e.s_dve.n, e.s_pe.n, e.s_act.n
    lx0 = e.LX[0].n
    lw0 = [e.LW[0].n, e.LW[1].n]
    st0 = [e.STX[0].n, e.STX[1].n]

    def lw(i):      return lw0[i % 2] + 16 * (i // 2 + 1)
    def dve(i, k):  return dve0 + DVE * i + k
    def pe(i, k):   return pe0 + PE * i + k
    def act(i, k):  return act0 + ACT * i + k
    def st(gi):     return st0[gi % 2] + 16 * (gi // 2 + 1)

    with ExitStack() as ctx:
        def sb(name, shape, dt=F32):
            return ctx.enter_context(nc.sbuf_tensor("f0h" + name, shape, dt))
        xnT_sb = sb("xnT", [128, BPC * NDC, T], BF16)
        wt = [sb(f"w{i}", [128, D]) for i in range(2)]
        m1 = sb("m1", [128, D])
        m2 = sb("m2", [128, D])
        q = sb("q", [128, D], BF16)
        fcT = sb("fcT", [128, NDC, 128], BF16)
        relu = [sb(f"relu{i}", [128, 512]) for i in range(2)]
        ht = [sb(f"ht{i}", [128, 512], BF16) for i in range(2)]
        psq = ctx.enter_context(nc.psum_tensor("f0hpsq", [128, 2048], BF16))
        psm = [ctx.enter_context(nc.psum_tensor(f"f0hpsm{i}", [128, 512], F32))
               for i in range(2)]

        @block.sync
        def _(sync):
            wait_bar(sync, bar)
            for b in range(BPC):
                for c in range(NDC):
                    sync.dma_start(xnT_sb[:, b * NDC + c, :], e.xnT[b, c, :, :]
                                   ).then_inc(e.LX[0].h, 16)
            for i in range(NO):
                for s_ in range(NSUB):
                    b, ts = divmod(s_, 4)
                    gi = i * NSUB + s_
                    sync.wait_ge(e.s_act.h, act(i, s_ + 1))
                    sync.dma_start(e.hmid[i, :, b * T + ts * 512:b * T + (ts + 1) * 512],
                                   ht[gi % 2][:]).then_inc(e.STX[gi % 2].h, 16)
        e.LX[0].tick(16 * BPC * NDC)
        e.STX[0].tick(16 * (NO * NSUB // 2))
        e.STX[1].tick(16 * (NO * NSUB // 2))

        @block.gpsimd
        def _(gp):
            wait_bar(gp, bar)
            for i in range(NO):
                if i >= 2:
                    gp.wait_ge(e.s_dve.h, dve(i - 2, 3))
                gp.dma_start(wt[i % 2][:], e.w_fc[i * 128:(i + 1) * 128, :]
                             ).then_inc(e.LW[i % 2].h, 16)
        e.LW[0].tick(16 * ((NO + 1) // 2))
        e.LW[1].tick(16 * (NO // 2))

        @block.vector
        def _(vector):
            vector = Chained(vector, e.s_dve)
            wait_bar(vector, bar)
            for i in range(NO):
                p = i % 2
                vector.wait_ge(e.LW[p].h, lw(i))
                k = 0
                vector.tensor_scalar(m1[:], wt[p][:], e.thrfc[:], None,
                                     op0=ALU.is_gt); k += 1
                vector.tensor_scalar(m2[:], wt[p][:], e.nthrfc[:], None,
                                     op0=ALU.is_lt); k += 1
                vector.wait_ge(e.s_dve.h, dve(i, k))
                vector.tensor_tensor(q[:], m1[:], m2[:], op=ALU.subtract
                                     ); k += 1
                vector.wait_ge(e.s_pe.h, pe(i, NDC))
                vector.tensor_copy(fcT[:], psq[:].rearrange("p (c n) -> p c n", n=128)
                                  ); k += 1
                for s_ in range(NSUB):
                    g = i * NSUB + s_
                    vector.wait_ge(e.s_pe.h, pe(i, NDC + (s_ + 1) * NDC))
                    if g >= 2:
                        vector.wait_ge(e.s_act.h, act0 + g - 1)
                    vector.tensor_scalar(relu[s_ % 2][:], psm[s_ % 2][:], 0.0, None,
                                         op0=ALU.max); k += 1

        @block.tensor
        def _(tensor):
            wait_bar(tensor, bar)
            tensor.wait_ge(e.LX[0].h, lx0 + 16 * BPC * NDC)
            for i in range(NO):
                tensor.wait_ge(e.s_dve.h, dve(i, 3))
                if i >= 1:
                    tensor.wait_ge(e.s_dve.h, dve(i - 1, 4))
                for c in range(NDC):
                    tensor.transpose(psq[:, c * 128:(c + 1) * 128],
                                     q[:, c * 128:(c + 1) * 128], e.idb[:]
                                     ).then_inc(e.s_pe.h, 1)
                tensor.wait_ge(e.s_dve.h, dve(i, 4))
                for s_ in range(NSUB):
                    b, ts = divmod(s_, 4)
                    g = i * NSUB + s_
                    if g >= 2:
                        pi, ps = divmod(g - 2, NSUB)
                        tensor.wait_ge(e.s_dve.h, dve(pi, 5 + ps))
                    for c in range(NDC):
                        tensor.matmul(psm[s_ % 2][:], fcT[:, c, :],
                                      xnT_sb[:, b * NDC + c, ts * 512:(ts + 1) * 512],
                                      start=(c == 0), stop=(c == NDC - 1)
                                      ).then_inc(e.s_pe.h, 1)
        e.s_pe.tick(PE * NO)

        @block.scalar
        def _(scalar):
            scalar = Chained(scalar, e.s_act)
            wait_bar(scalar, bar)
            for i in range(NO):
                for s_ in range(NSUB):
                    gi = i * NSUB + s_
                    scalar.wait_ge(e.s_dve.h, dve(i, 5 + s_))
                    if gi >= 2:
                        scalar.wait_ge(e.STX[gi % 2].h, st(gi - 2))
                    scalar.activation(ht[gi % 2][:], relu[s_ % 2][:], AF.Square
                                      )



def emit_ffn0p(e, block):
    nc = e.nc
    NOC = FFH // 128
    NDH = 8
    TS = 256
    NSUB = T // TS
    NU = BPC * NSUB
    bar = bar_of(e)
    dve0, pe0, act0 = e.s_dve.n, e.s_pe.n, e.s_act.n
    lw0 = e.LW[0].n
    lx0 = [e.LX[0].n, e.LX[1].n]
    st0 = [e.STT[0].n, e.STT[1].n]
    PREP_ACT, PREP_DVE, PREP_PE = 2, 5, 40
    MM_PE = NDH * NOC
    H_ACT = PREP_ACT * NDH + NU * NDH
    H_DVE = PREP_DVE * NDH
    H_PE = PREP_PE * NDH + NU * MM_PE

    def p_act(h, i, k): return act0 + H_ACT * h + PREP_ACT * i + k
    def p_dve(h, i, k): return dve0 + H_DVE * h + PREP_DVE * i + k
    def p_pe(h, i, k):  return pe0 + H_PE * h + PREP_PE * i + k
    def m_pe(h, u, j, k):
        return pe0 + H_PE * h + PREP_PE * NDH + MM_PE * u + NOC * j + k
    def m_act(h, u, j):
        return act0 + H_ACT * h + PREP_ACT * NDH + NDH * u + j + 1
    def m_ld(h, u):
        g = h * NU + u
        return lx0[g % 2] + 16 * NOC * (g // 2 + 1)
    def m_st(h, u, j):
        gi = h * NU * NDH + u * NDH + j
        return st0[gi % 2] + 16 * (gi // 2 + 1)

    with ExitStack() as ctx:
        def sb(name, shape, dt=F32):
            return ctx.enter_context(nc.sbuf_tensor("f0p" + name, shape, dt))
        wt = sb("w", [128, FFH])
        a = sb("a", [128, FFH])
        mbf = sb("mbf", [128, FFH], BF16)
        q = sb("q", [128, FFH], BF16)
        projT = sb("projT", [128, NOC, NDH, 128], BF16)
        h_sb = [sb(f"h{i}", [128, NOC, TS], BF16) for i in range(2)]
        g_t = [sb(f"g{i}", [128, TS]) for i in range(2)]
        psq = ctx.enter_context(nc.psum_tensor("f0ppsq", [128, 2048], BF16))
        psm = [ctx.enter_context(nc.psum_tensor(f"f0ppsm{i}", [128, TS], F32))
               for i in range(2)]

        @block.gpsimd
        def _(gp):
            wait_bar(gp, bar)
            for h in range(2):
                for i in range(NDH):
                    gi = h * NDH + i
                    if gi:
                        ph, pi = divmod(gi - 1, NDH)
                        gp.wait_ge(e.s_act.h, p_act(ph, pi, 2))
                    dd0 = gi * 128
                    gp.dma_start(wt[:], e.w_proj[dd0:dd0 + 128, :]
                                 ).then_inc(e.LW[0].h, 16)
                for u in range(NU):
                    b, ts = divmod(u, NSUB)
                    for j in range(NDH):
                        gi = h * NU * NDH + u * NDH + j
                        gp.wait_ge(e.s_act.h, m_act(h, u, j))
                        gp.dma_start(
                            e.g0[b, h * NDH + j, :, ts * TS:(ts + 1) * TS],
                            g_t[(u * NDH + j) % 2][:]).then_inc(e.STT[gi % 2].h, 16)
        e.LW[0].tick(16 * 2 * NDH)
        e.STT[0].tick(16 * NU * NDH)
        e.STT[1].tick(16 * NU * NDH)

        @block.scalar
        def _(scalar):
            scalar = Chained(scalar, e.s_act)
            wait_bar(scalar, bar)
            for h in range(2):
                for i in range(NDH):
                    gi = h * NDH + i
                    scalar.wait_ge(e.LW[0].h, lw0 + 16 * (gi + 1))
                    if gi:
                        ph, pi = divmod(gi - 1, NDH)
                        scalar.wait_ge(e.s_dve.h, p_dve(ph, pi, 2))
                        scalar.wait_ge(e.s_pe.h, p_pe(ph, pi, PREP_PE))
                    scalar.activation(a[:], wt[:], AF.Abs)
                    scalar.activation(q[:], wt[:], AF.Sign)
                for u in range(NU):
                    for j in range(NDH):
                        gi = u * NDH + j
                        gfull = h * NU * NDH + gi
                        scalar.wait_ge(e.s_pe.h, m_pe(h, u, j, NOC))
                        if gfull >= 2:
                            phh, rem = divmod(gfull - 2, NU * NDH)
                            pu, pj = divmod(rem, NDH)
                            scalar.wait_ge(e.STT[gfull % 2].h, m_st(phh, pu, pj))
                        scalar.activation(g_t[gi % 2][:], psm[j % 2][:], AF.Silu,
                                          scale=e.amtot[:, :])

        @block.vector
        def _(vector):
            vector = Chained(vector, e.s_dve)
            wait_bar(vector, bar)
            for h in range(2):
                for i in range(NDH):
                    k = 0
                    vector.wait_ge(e.s_act.h, p_act(h, i, 2))
                    vector.tensor_scalar(mbf[:], a[:], e.thrpj[:], None,
                                         op0=ALU.is_gt); k += 1
                    vector.wait_ge(e.s_dve.h, p_dve(h, i, k))
                    vector.tensor_tensor(q[:], q[:], mbf[:], op=ALU.mult
                                         ); k += 1
                    for r, (o0, no) in enumerate([(0, 16), (16, 16), (32, 8)]):
                        vector.wait_ge(e.s_pe.h, p_pe(h, i, o0 + no))
                        vector.tensor_copy(
                            projT[:, o0:o0 + no, i, :],
                            psq[:, :no * 128].rearrange("p (c n) -> p c n", n=128)
                        ); k += 1

        @block.tensor
        def _(tensor):
            wait_bar(tensor, bar)
            for h in range(2):
                for i in range(NDH):
                    if h:
                        tensor.wait_ge(e.s_pe.h, pe0 + H_PE)
                    tensor.wait_ge(e.s_dve.h, p_dve(h, i, 2))
                    for r, (o0, no) in enumerate([(0, 16), (16, 16), (32, 8)]):
                        if r:
                            tensor.wait_ge(e.s_dve.h, p_dve(h, i, 2 + r))
                        elif h or i:
                            ph, pi = divmod(h * NDH + i - 1, NDH)
                            tensor.wait_ge(e.s_dve.h, p_dve(ph, pi, 5))
                        for c in range(no):
                            tensor.transpose(psq[:, c * 128:(c + 1) * 128],
                                             q[:, (o0 + c) * 128:(o0 + c + 1) * 128],
                                             e.idb[:]).then_inc(e.s_pe.h, 1)
                tensor.wait_ge(e.s_dve.h, p_dve(h, NDH - 1, 5))
                for u in range(NU):
                    g = h * NU + u
                    tensor.wait_ge(e.LX[g % 2].h, m_ld(h, u))
                    for j in range(NDH):
                        gi = u * NDH + j
                        if gi >= 2:
                            pu, pj = divmod(gi - 2, NDH)
                            tensor.wait_ge(e.s_act.h, m_act(h, pu, pj))
                        for c in range(NOC):
                            tensor.matmul(psm[j % 2][:], projT[:, c, j, :],
                                          h_sb[u % 2][:, c, :],
                                          start=(c == 0), stop=(c == NOC - 1)
                                          ).then_inc(e.s_pe.h, 1)
        e.s_pe.tick(2 * H_PE)

        @block.sync
        def _(sync):
            wait_bar(sync, bar)
            for h in range(2):
                for u in range(NU):
                    b, ts = divmod(u, NSUB)
                    g = h * NU + u
                    if g >= 2:
                        ph, pu = divmod(g - 2, NU)
                        sync.wait_ge(e.s_pe.h, m_pe(ph, pu, NDH - 1, NOC))
                    for c in range(NOC):
                        sync.dma_start(
                            h_sb[u % 2][:, c, :],
                            e.hmid[c, :, b * T + ts * TS:b * T + (ts + 1) * TS]
                        ).then_inc(e.LX[g % 2].h, 16)
        e.LX[0].tick(16 * NOC * NU)
        e.LX[1].tick(16 * NOC * NU)



def emit_attn_batch(e, block, b):
    nc = e.nc

    with ExitStack() as bctx:
        def bsb(name, shape, dt=F32):
            return bctx.enter_context(nc.sbuf_tensor(f"at{b}{name}", shape, dt))
        abctx = bctx.enter_context(ExitStack())
        def absb(name, shape, dt=F32):
            return abctx.enter_context(nc.sbuf_tensor(f"at{b}{name}", shape, dt))
        kvlat = absb("kvlat", [128, 4, T], BF16)
        qlat = absb("qlat", [128, 8, T], BF16)
        krT = absb("krT", [128, T], BF16)
        qrT = absb("qrT", [128, T], BF16)
        psA = [bctx.enter_context(nc.psum_tensor(f"at{b}psA{i}", [128, 512], F32))
               for i in range(2)]
        psS = bctx.enter_context(nc.psum_tensor(f"at{b}psS", [128, 2048], F32))
        psT = bctx.enter_context(nc.psum_tensor(f"at{b}psT", [128, 2048], BF16))

        with ExitStack() as actx:
            def asb(name, shape, dt=F32):
                return actx.enter_context(nc.sbuf_tensor(f"at{b}a{name}", shape, dt))
            xnT_sb = asb("xnT", [128, NDC, T], BF16)
            wtmp = [asb(f"wt{i}", [128, 1024]) for i in range(2)]
            a1ctx = actx.enter_context(ExitStack())
            def a1sb(name, shape, dt=F32):
                return a1ctx.enter_context(nc.sbuf_tensor(f"at{b}a{name}", shape, dt))
            cosb = a1sb("cos", [128, T])
            sinb = a1sb("sin", [128, T])
            krw = a1sb("krw", [128, NDC, 128], BF16)
            qrw = a1sb("qrw", [128, NDC, 128], BF16)
            krf = a1sb("krf", [128, T])
            qrf = a1sb("qrf", [128, T])
            krs = a1sb("krs", [128, T])

            bar = bar_of(e)
            dve0, pe0 = e.s_dve.n, e.s_pe.n
            lx0 = e.LX[0].n
            lw0 = [e.LW[0].n, e.LW[1].n]

            @block.sync
            def _(sync):
                wait_bar(sync, bar)
                sync.dma_start(cosb[:], e.cos2_d[:]).then_inc(e.LX[0].h, 16)
                sync.dma_start(sinb[:], e.sin2_d[:]).then_inc(e.LX[0].h, 16)
                for c in range(NDC):
                    sync.dma_start(xnT_sb[:, c, :], e.xnT[b, c, :, :]
                                   ).then_inc(e.LX[0].h, 16)
            e.LX[0].tick(16 * (2 + NDC))

            @block.gpsimd
            def _(gp):
                wait_bar(gp, bar)
                for l in range(32):
                    w = e.w_kr if l < 16 else e.w_qr
                    c = l % 16
                    if l >= 2:
                        gp.wait_ge(e.s_dve.h, dve0 + l - 1)
                    gp.dma_start(wtmp[l % 2][:, 0:128], w[c * 128:(c + 1) * 128, :]
                                 ).then_inc(e.LW[l % 2].h, 16)
            e.LW[0].tick(16 * 16)
            e.LW[1].tick(16 * 16)

            @block.vector
            def _(vector):
                vector = Chained(vector, e.s_dve)
                wait_bar(vector, bar)
                for l in range(32):
                    vector.wait_ge(e.LW[l % 2].h, lw0[l % 2] + 16 * (l // 2 + 1))
                    dst = krw if l < 16 else qrw
                    vector.tensor_copy(dst[:, l % 16, :], wtmp[l % 2][:, 0:128])
                for g in range(8):
                    ts, j = divmod(g, 2)
                    vector.wait_ge(e.s_pe.h, pe0 + 16 * (g + 1))
                    dst = krf if j == 0 else qrf
                    vector.tensor_copy(dst[:, ts * 512:(ts + 1) * 512],
                                       psA[g % 2][:])
                vector.wait_ge(e.LX[0].h, lx0 + 16 * (2 + NDC))
                vector.stream_shuffle(krs[:], krf[:],
                                      [i ^ 1 for i in range(32)])
                vector.tensor_tensor(krf[:], krf[:], cosb[:], op=ALU.mult)
                vector.tensor_tensor(krs[:], krs[:], sinb[:], op=ALU.mult)
                vector.tensor_tensor(krT[:], krf[:], krs[:], op=ALU.add)
                vector.stream_shuffle(krs[:], qrf[:],
                                      [i ^ 1 for i in range(32)])
                vector.tensor_tensor(qrf[:], qrf[:], cosb[:], op=ALU.mult)
                vector.tensor_tensor(krs[:], krs[:], sinb[:], op=ALU.mult)
                vector.tensor_tensor(qrT[:], qrf[:], krs[:], op=ALU.add)

            @block.tensor
            def _(tensor):
                wait_bar(tensor, bar)
                tensor.wait_ge(e.LX[0].h, lx0 + 16 * (2 + NDC))
                for g in range(8):
                    ts, j = divmod(g, 2)
                    tensor.wait_ge(e.s_dve.h, dve0 + (16 if j == 0 else 32))
                    if g >= 2:
                        tensor.wait_ge(e.s_dve.h, dve0 + 32 + g - 1)
                    w = krw if j == 0 else qrw
                    for c in range(NDC):
                        tensor.matmul(psA[g % 2][:], w[:, c, :],
                                      xnT_sb[:, c, ts * 512:(ts + 1) * 512],
                                      start=(c == 0), stop=(c == NDC - 1)
                                      ).then_inc(e.s_pe.h, 1)
            e.s_pe.tick(16 * 8)

            a1ctx.close()
            kvdb = asb("kvdb", [128, NDC, 512], BF16)
            qdb = asb("qdb", [128, NDC, 1024], BF16)
            dve0, pe0 = e.s_dve.n, e.s_pe.n
            lw0 = [e.LW[0].n, e.LW[1].n]

            @block.gpsimd
            def _(gp):
                for l in range(32):
                    gp.wait_ge(e.s_dve.h, dve0 + (l - 1 if l >= 2 else 0))
                    if l < 16:
                        src = e.w_kvd[l * 128:(l + 1) * 128, :]
                        gp.dma_start(wtmp[l % 2][:, 0:512], src
                                     ).then_inc(e.LW[l % 2].h, 16)
                    else:
                        c = l - 16
                        src = e.w_qd[c * 128:(c + 1) * 128, :]
                        gp.dma_start(wtmp[l % 2][:, 0:1024], src
                                     ).then_inc(e.LW[l % 2].h, 16)
            e.LW[0].tick(16 * 16)
            e.LW[1].tick(16 * 16)

            @block.vector
            def _(vector):
                vector = Chained(vector, e.s_dve)
                for l in range(32):
                    vector.wait_ge(e.LW[l % 2].h, lw0[l % 2] + 16 * (l // 2 + 1))
                    if l < 16:
                        vector.tensor_copy(kvdb[:, l, :], wtmp[l % 2][:, 0:512])
                    else:
                        vector.tensor_copy(qdb[:, l - 16, :], wtmp[l % 2][:, 0:1024])
                for g in range(48):
                    ts, j = divmod(g, 12)
                    vector.wait_ge(e.s_pe.h, pe0 + 16 * (g + 1))
                    if j < 4:
                        vector.tensor_copy(kvlat[:, j, ts * 512:(ts + 1) * 512],
                                           psA[g % 2][:])
                    else:
                        vector.tensor_copy(qlat[:, j - 4, ts * 512:(ts + 1) * 512],
                                           psA[g % 2][:])

            @block.tensor
            def _(tensor):
                tensor.wait_ge(e.s_dve.h, dve0)
                for g in range(48):
                    ts, j = divmod(g, 12)
                    tensor.wait_ge(e.s_dve.h, dve0 + (16 if j < 4 else 32))
                    if g >= 2:
                        tensor.wait_ge(e.s_dve.h, dve0 + 32 + g - 1)
                    for c in range(NDC):
                        if j < 4:
                            lhs = kvdb[:, c, j * 128:(j + 1) * 128]
                        else:
                            lhs = qdb[:, c, (j - 4) * 128:(j - 3) * 128]
                        tensor.matmul(psA[g % 2][:], lhs,
                                      xnT_sb[:, c, ts * 512:(ts + 1) * 512],
                                      start=(c == 0), stop=(c == NDC - 1)
                                      ).then_inc(e.s_pe.h, 1)
            e.s_pe.tick(16 * 48)

        sto0 = [e.STT[0].n, e.STT[1].n]
        with ExitStack() as hctx:
            def hsb(name, shape, dt=F32):
                return hctx.enter_context(nc.sbuf_tensor(f"at{b}h{name}", shape, dt))
            wtm2 = [hsb(f"wt{i}", [128, 128]) for i in range(2)]
            upk = hsb("upk", [128, 4, 128], BF16)
            upv = hsb("upv", [128, 4, 128], BF16)
            quw = hsb("quw", [128, 8, 128], BF16)
            kcT = hsb("kcT", [128, T], BF16)
            vtok = hsb("vtok", [128, NDC, 128], BF16)
            qcT = hsb("qcT", [128, T], BF16)
            ef = [hsb(f"ef{i}", [128, T]) for i in range(2)]
            attnT = hsb("attnT", [128, NDC, 512], BF16)
            nmx = hsb("nmx", [128, 4])
            sums = hsb("sums", [128, 2, 4])
            stg_o = [hsb(f"stgo{i}", [128, 512], BF16) for i in range(2)]
            prep_mm = [4] * 4 + [4] * 16 + [8] * 4
            ends = []
            acc = 0
            for m in prep_mm:
                acc += m
                ends.append(acc)
            PREP = acc
            pe_sA = {}
            pe_sB = {}
            pe_tr = {}
            pe_out = {}
            p = PREP
            for gq in range(16):
                p += 4
                pe_sA[gq] = p
                p += 4
                pe_sB[gq] = p
                if gq >= 1:
                    p += 16
                    pe_tr[gq - 1] = p
                if gq >= 4 and gq % 4 == 0:
                    p += 16
                    pe_out[gq // 4 - 1] = p
            p += 16
            pe_tr[15] = p
            p += 16
            pe_out[3] = p
            PEH = p
            d_nma = {}
            d_nm16 = {}
            d_rs = {}
            d_ab = {}
            p = 40
            for gq in range(16):
                p += 1
                d_nma[gq] = p
                p += 3
                d_nm16[gq] = p
                if gq >= 1:
                    p += 2
                    d_rs[gq - 1] = p - 1
                    p += 1
                    d_ab[gq - 1] = p
            p += 3
            d_rs[15] = p - 2
            d_ab[15] = p
            DVEH = p
            a_ea = {}
            a_eb = {}
            a_ev = {}
            a_oe = {}
            p = 0
            for gq in range(16):
                if gq >= 1 and gq % 4 == 0:
                    p += 1
                    a_ev[gq - 1] = p
                    p += 1
                    a_oe[gq // 4 - 1] = p
                elif gq >= 1:
                    p += 1
                    a_ev[gq - 1] = p
                p += 1
                a_ea[gq] = p
                p += 1
                a_eb[gq] = p
            p += 1
            a_ev[15] = p
            p += 1
            a_oe[3] = p
            ACTH = p

            for h in range(HEADS):
                bar = bar_of(e)
                dve0, pe0, act0 = e.s_dve.n, e.s_pe.n, e.s_act.n
                lw0 = [e.LW[0].n, e.LW[1].n]

                @block.gpsimd
                def _(gp):
                    for l in range(16):
                        gp.wait_ge(e.s_dve.h, dve0 + (l - 1 if l >= 2 else 0))
                        if l < 4:
                            src = e.w_upk[l * 128:(l + 1) * 128,
                                          h * 128:(h + 1) * 128]
                        elif l < 8:
                            src = e.w_upv[(l - 4) * 128:(l - 3) * 128,
                                          h * 128:(h + 1) * 128]
                        else:
                            src = e.w_qu[(l - 8) * 128:(l - 7) * 128,
                                         h * 128:(h + 1) * 128]
                        gp.dma_start(wtm2[l % 2][:], src).then_inc(e.LW[l % 2].h, 16)
                    for qg in range(4):
                        oi = h * 4 + qg
                        gp.wait_ge(e.s_act.h, act0 + a_oe[qg])
                        gp.dma_start(e.outTd[b, h, :, qg * 512:(qg + 1) * 512],
                                     stg_o[oi % 2][:]).then_inc(e.STT[oi % 2].h, 16)
                e.LW[0].tick(16 * 8)
                e.LW[1].tick(16 * 8)
                e.STT[0].tick(16 * 2)
                e.STT[1].tick(16 * 2)

                @block.vector
                def _(vector):
                    vector = Chained(vector, e.s_dve)
                    vector.wait_ge(e.s_pe.h, pe0)
                    for l in range(16):
                        vector.wait_ge(e.LW[l % 2].h, lw0[l % 2] + 16 * (l // 2 + 1))
                        if l < 4:
                            vector.tensor_copy(upk[:, l, :], wtm2[l % 2][:])
                        elif l < 8:
                            vector.tensor_copy(upv[:, l - 4, :], wtm2[l % 2][:])
                        else:
                            vector.tensor_copy(quw[:, l - 8, :], wtm2[l % 2][:])
                    for g in range(24):
                        vector.wait_ge(e.s_pe.h, pe0 + ends[g])
                        if g < 4:
                            vector.tensor_copy(kcT[:, g * 512:(g + 1) * 512],
                                               psA[g % 2][:])
                        elif g < 20:
                            tt = g - 4
                            vector.tensor_copy(vtok[:, tt, :],
                                               psA[g % 2][:, 0:128])
                        else:
                            ts = g - 20
                            vector.tensor_copy(qcT[:, ts * 512:(ts + 1) * 512],
                                               psA[g % 2][:])

                    def lagged(gq):
                        pp_ = gq % 2
                        vector.wait_ge(e.s_act.h, act0 + a_eb[gq])
                        vector.reduce_sum(sums[:, pp_, 2:3], sums[:, pp_, 0:2],
                                          axis=AX.X)
                        vector.reciprocal(sums[:, pp_, 3:4], sums[:, pp_, 2:3])
                        if gq >= 1:
                            vector.wait_ge(e.s_pe.h, pe0 + pe_tr[gq - 1])
                        vector.tensor_scalar(attn_bf(e)[:], ef[pp_][:],
                                             sums[:, pp_, 3:4], None,
                                             op0=ALU.mult)

                    for gq in range(16):
                        qg, qt = divmod(gq, 4)
                        base_pe = pe0 + pe_sA[gq]
                        vector.wait_ge(e.s_pe.h, pe0 + pe_sA[gq])
                        vector.reduce_max(nmx[:, 0:1], psS[:, 0:1024], axis=AX.X,
                                          negate=True)
                        vector.wait_ge(e.s_pe.h, pe0 + pe_sB[gq])
                        vector.reduce_max(nmx[:, 1:2], psS[:, 1024:2048],
                                          axis=AX.X, negate=True)
                        vector.tensor_reduce(nmx[:, 2:3], nmx[:, 0:2],
                                             axis=AX.X, op=ALU.min)
                        if gq >= 1:
                            vector.wait_ge(e.s_act.h, act0 + a_eb[gq - 1])
                        vector.tensor_scalar(nmx[:, 3:4], nmx[:, 2:3],
                                             1.0 / 16.0, None, op0=ALU.mult)
                        if gq >= 1:
                            lagged(gq - 1)
                    lagged(15)

                @block.scalar
                def _(scalar):
                    scalar = Chained(scalar, e.s_act)

                    def evac(gq):
                        scalar.wait_ge(e.s_pe.h, pe0 + pe_tr[gq])
                        scalar.copy(attnT[:, :, (gq % 4) * 128:(gq % 4 + 1) * 128],
                                    psT[:].rearrange("p (c n) -> p c n", n=128))

                    def outev(qg):
                        oi = h * 4 + qg
                        scalar.wait_ge(e.s_pe.h, pe0 + pe_out[qg])
                        if oi >= 2:
                            scalar.wait_ge(e.STT[oi % 2].h,
                                           sto0[oi % 2] + 16 * (oi // 2))
                        scalar.copy(stg_o[oi % 2][:], psA[qg % 2][:])

                    for gq in range(16):
                        pp_ = gq % 2
                        if gq >= 1 and gq % 4 == 0:
                            evac(gq - 1)
                            outev(gq // 4 - 1)
                        elif gq >= 1:
                            evac(gq - 1)
                        if gq >= 2:
                            scalar.wait_ge(e.s_dve.h, dve0 + d_ab[gq - 2])
                        scalar.wait_ge(e.s_dve.h, dve0 + d_nm16[gq])
                        scalar.wait_ge(e.s_pe.h, pe0 + pe_sA[gq])
                        scalar.activation(ef[pp_][:, 0:1024], psS[:, 0:1024],
                                          AF.Exp, bias=nmx[:, 3:4],
                                          scale=1.0 / 16.0,
                                          accum_out=sums[:, pp_, 0:1])
                        scalar.wait_ge(e.s_pe.h, pe0 + pe_sB[gq])
                        scalar.activation(ef[pp_][:, 1024:2048], psS[:, 1024:2048],
                                          AF.Exp, bias=nmx[:, 3:4],
                                          scale=1.0 / 16.0,
                                          accum_out=sums[:, pp_, 1:2])
                    evac(15)
                    outev(3)

                @block.tensor
                def _(tensor):
                    tensor.wait_ge(e.s_dve.h, dve0)
                    tensor.wait_ge(e.s_act.h, act0)
                    for g in range(24):
                        if g < 4:
                            tensor.wait_ge(e.s_dve.h, dve0 + 4)
                        elif g < 20:
                            tensor.wait_ge(e.s_dve.h, dve0 + 8)
                        else:
                            tensor.wait_ge(e.s_dve.h, dve0 + 16)
                        if g >= 2:
                            tensor.wait_ge(e.s_dve.h, dve0 + 16 + g - 1)
                        if g < 4:
                            ts = g
                            for kc in range(4):
                                tensor.matmul(psA[g % 2][:], upk[:, kc, :],
                                              kvlat[:, kc, ts * 512:(ts + 1) * 512],
                                              start=(kc == 0), stop=(kc == 3)
                                              ).then_inc(e.s_pe.h, 1)
                        elif g < 20:
                            tt = g - 4
                            for kc in range(4):
                                tensor.matmul(psA[g % 2][:, 0:128],
                                              kvlat[:, kc, tt * 128:(tt + 1) * 128],
                                              upv[:, kc, :],
                                              start=(kc == 0), stop=(kc == 3)
                                              ).then_inc(e.s_pe.h, 1)
                        else:
                            ts = g - 20
                            for qc in range(8):
                                tensor.matmul(psA[g % 2][:], quw[:, qc, :],
                                              qlat[:, qc, ts * 512:(ts + 1) * 512],
                                              start=(qc == 0), stop=(qc == 7)
                                              ).then_inc(e.s_pe.h, 1)
                    tensor.wait_ge(e.s_dve.h, dve0 + 40)

                    def transposes(gq):
                        tensor.wait_ge(e.s_dve.h, dve0 + d_ab[gq])
                        if gq >= 1:
                            tensor.wait_ge(e.s_act.h, act0 + a_ev[gq - 1])
                        for c in range(NDC):
                            tensor.transpose(psT[:, c * 128:(c + 1) * 128],
                                             attn_bf(e)[:, c * 128:(c + 1) * 128],
                                             e.idb[:]).then_inc(e.s_pe.h, 1)

                    def outmm(qg):
                        tensor.wait_ge(e.s_act.h, act0 + a_ev[4 * qg + 3])
                        if qg >= 2:
                            tensor.wait_ge(e.s_act.h, act0 + a_oe[qg - 2])
                        for c in range(NDC):
                            tensor.matmul(psA[qg % 2][:], vtok[:, c, :],
                                          attnT[:, c, :],
                                          start=(c == 0), stop=(c == NDC - 1)
                                          ).then_inc(e.s_pe.h, 1)

                    for gq in range(16):
                        qg, qt = divmod(gq, 4)
                        q0 = qg * 512 + qt * 128
                        for half2 in range(2):
                            if gq >= 1:
                                tensor.wait_ge(e.s_dve.h,
                                               dve0 + d_nma[gq - 1] + half2)
                                tensor.wait_ge(e.s_act.h,
                                               act0 + (a_ea if half2 == 0
                                                       else a_eb)[gq - 1])
                            for ks in (0, 1):
                                kk0 = half2 * 1024 + ks * 512
                                tensor.matmul(psS[:, kk0:kk0 + 512],
                                              qcT[:, q0:q0 + 128],
                                              kcT[:, kk0:kk0 + 512],
                                              start=True, stop=False
                                              ).then_inc(e.s_pe.h, 1)
                                tensor.matmul(psS[:, kk0:kk0 + 512],
                                              qrT[:, q0:q0 + 128],
                                              krT[:, kk0:kk0 + 512],
                                              start=False, stop=True
                                              ).then_inc(e.s_pe.h, 1)
                        if gq >= 1:
                            transposes(gq - 1)
                        if gq >= 4 and gq % 4 == 0:
                            outmm(gq // 4 - 1)
                    transposes(15)
                    outmm(3)
                e.s_pe.tick(PEH)

        abctx.close()
        with ExitStack() as cctx:
            def csb(name, shape, dt=F32):
                return cctx.enter_context(nc.sbuf_tensor(f"at{b}c{name}", shape, dt))
            wtm3 = [csb(f"wt{i}", [128, D]) for i in range(2)]
            woutb = csb("wout", [128, NDC, D], BF16)
            outT = csb("outT", [128, HEADS, T], BF16)
            stg = [csb(f"stg{i}", [128, 512]) for i in range(2)]
            bar = bar_of(e)
            dve0, pe0 = e.s_dve.n, e.s_pe.n
            lw0 = [e.LW[0].n, e.LW[1].n]
            st0 = [e.STT[0].n, e.STT[1].n]
            lx0c = e.LX[0].n

            @block.sync
            def _(sync):
                for s_h, n in bar:
                    if n:
                        sync.wait_ge(s_h.h, n)
                for hh in range(HEADS):
                    sync.dma_start(outT[:, hh, :], e.outTd[b, hh, :, :]
                                   ).then_inc(e.LX[0].h, 16)
            e.LX[0].tick(16 * HEADS)

            @block.gpsimd
            def _(gp):
                for l in range(16):
                    gp.wait_ge(e.s_dve.h, dve0 + (l - 1 if l >= 2 else 0))
                    gp.dma_start(wtm3[l % 2][:], e.w_out[l * 128:(l + 1) * 128, :]
                                 ).then_inc(e.LW[l % 2].h, 16)
                for u in range(64):
                    ts, ddt = divmod(u, NDC)
                    gp.wait_ge(e.s_dve.h, dve0 + 16 + u + 1)
                    gp.dma_start(e.s0[b, ddt, :, ts * 512:(ts + 1) * 512],
                                 stg[u % 2][:]).then_inc(e.STT[u % 2].h, 16)
            e.LW[0].tick(16 * 8)
            e.LW[1].tick(16 * 8)
            e.STT[0].tick(16 * 32)
            e.STT[1].tick(16 * 32)

            @block.vector
            def _(vector):
                vector = Chained(vector, e.s_dve)
                for l in range(16):
                    vector.wait_ge(e.LW[l % 2].h, lw0[l % 2] + 16 * (l // 2 + 1))
                    vector.tensor_copy(woutb[:, l, :], wtm3[l % 2][:])
                for u in range(64):
                    vector.wait_ge(e.s_pe.h, pe0 + 16 * (u + 1))
                    if u >= 2:
                        vector.wait_ge(e.STT[u % 2].h, st0[u % 2] + 16 * (u // 2))
                    vector.tensor_copy(stg[u % 2][:], psA[u % 2][:])

            @block.tensor
            def _(tensor):
                tensor.wait_ge(e.LX[0].h, lx0c + 16 * HEADS)
                for u in range(64):
                    ts, ddt = divmod(u, NDC)
                    tensor.wait_ge(e.s_dve.h, dve0 + 16)
                    if u >= 2:
                        tensor.wait_ge(e.s_dve.h, dve0 + 16 + u - 1)
                    for c in range(NDC):
                        tensor.matmul(psA[u % 2][:],
                                      woutb[:, c, ddt * 128:(ddt + 1) * 128],
                                      outT[:, c, ts * 512:(ts + 1) * 512],
                                      start=(c == 0), stop=(c == NDC - 1)
                                      ).then_inc(e.s_pe.h, 1)
            e.s_pe.tick(16 * 64)


def attn_bf(e):
    return e._attn_bf


def emit_attn(e, block):
    e._attn_bf_guard = e.top_ctx.enter_context(
        e.nc.sbuf_tensor("attnbf", [128, T], BF16))
    e._attn_bf = e._attn_bf_guard
    for b in range(BPC):
        emit_attn_batch(e, block, b)



def emit_l1prep(e, block):
    nc = e.nc
    bar = bar_of(e)
    dve0, act0 = e.s_dve.n, e.s_act.n
    lx0 = e.LX[0].n

    @block.sync
    def _(sync):
        wait_bar(sync, bar)
        for i, v in enumerate(("bk", "bv", "br", "bo", "tmk", "tmv", "tmr",
                               "td", "tf")):
            sync.dma_start(e.vecs[:, i, :], getattr(e, v)[:]).then_inc(e.LX[0].h, 16)
    e.LX[0].tick(16 * 9)

    @block.scalar
    def _(scalar):
        scalar = Chained(scalar, e.s_act)
        wait_bar(scalar, bar)
        scalar.wait_ge(e.LX[0].h, lx0 + 16 * 9)
        scalar.activation(e.dvecs[:, 0, :], e.vecs[:, 7, :], AF.Exp)

    @block.vector
    def _(vector):
        vector = Chained(vector, e.s_dve)
        wait_bar(vector, bar)
        vector.wait_ge(e.LX[0].h, lx0 + 16 * 9)
        vector.wait_ge(e.s_act.h, act0 + 1)
        vector.tensor_scalar(e.dvecs[:, 0, :], e.dvecs[:, 0, :], -1.0, None,
                             op0=ALU.mult)
        vector.tensor_scalar(e.dvecs[:, 1, :], e.vecs[:, 8, :], -1.0, None,
                             op0=ALU.mult)
        for j in range(3):
            vector.tensor_scalar(e.dvecs[:, 2 + j, :], e.vecs[:, 4 + j, :],
                                 -1.0, 1.0, op0=ALU.mult, op1=ALU.add)
        vector.memset(e.dvecs[:, 5, :], 0.0)



def emit_kvr(e, block):
    nc = e.nc
    with ExitStack() as ctx:
        def sb(name, shape, dt=F32):
            return ctx.enter_context(nc.sbuf_tensor("kvr" + name, shape, dt))
        wtmp = [sb(f"wt{i}", [128, D]) for i in range(2)]
        wbf = sb("wbf", [128, D], BF16)
        WT = sb("WT", [128, NDC, NDC, 128], BF16)
        xp = sb("xp", [128, NDC, T + 1], BF16)
        xk = sb("xk", [128, NDC, 512], BF16)
        stg = [sb(f"stg{i}", [128, 512]) for i in range(2)]
        psq = ctx.enter_context(nc.psum_tensor("kvrpsq", [128, 2048], BF16))
        psm = [ctx.enter_context(nc.psum_tensor(f"kvrpsm{i}", [128, 512], F32))
               for i in range(2)]

        for wi, (wsrc, dstT, bcol, tmcol) in enumerate(
                [(e.Wk, e.kT, 0, 2), (e.Wv, e.vT, 1, 3), (e.Wr, e.rT, 2, 4)]):
            bar = bar_of(e)
            dve0, pe0 = e.s_dve.n, e.s_pe.n
            lw0 = [e.LW[0].n, e.LW[1].n]

            @block.gpsimd
            def _(gp):
                wait_bar(gp, bar)
                for l in range(NDC):
                    gp.wait_ge(e.s_dve.h, dve0 + (2 * (l - 2) + 2 if l >= 2 else 0))
                    gp.dma_start(wtmp[l % 2][:], wsrc[l * 128:(l + 1) * 128, :]
                                 ).then_inc(e.LW[l % 2].h, 16)
            e.LW[0].tick(16 * 8)
            e.LW[1].tick(16 * 8)

            @block.vector
            def _(vector):
                vector = Chained(vector, e.s_dve)
                wait_bar(vector, bar)
                for l in range(NDC):
                    vector.wait_ge(e.LW[l % 2].h, lw0[l % 2] + 16 * (l // 2 + 1))
                    vector.wait_ge(e.s_pe.h, pe0 + 16 * l)
                    vector.tensor_copy(wbf[:], wtmp[l % 2][:])
                    vector.wait_ge(e.s_pe.h, pe0 + 16 * (l + 1))
                    vector.tensor_copy(WT[:, :, l, :],
                                       psq[:].rearrange("p (c n) -> p c n", n=128))

            @block.tensor
            def _(tensor):
                wait_bar(tensor, bar)
                for l in range(NDC):
                    tensor.wait_ge(e.s_dve.h, dve0 + 2 * l + 1)
                    for c in range(NDC):
                        tensor.transpose(psq[:, c * 128:(c + 1) * 128],
                                         wbf[:, c * 128:(c + 1) * 128],
                                         e.idb[:]).then_inc(e.s_pe.h, 1)
            e.s_pe.tick(16 * NDC)

            for b in range(BPC):
                bar = bar_of(e)
                dve0, pe0 = e.s_dve.n, e.s_pe.n
                lx0 = e.LX[0].n
                st0 = [e.STT[0].n, e.STT[1].n]

                @block.sync
                def _(sync):
                    for s_h, n in bar:
                        if n:
                            sync.wait_ge(s_h.h, n)
                    for c in range(NDC):
                        sync.dma_start(xp[:, c, 1:T + 1], e.xn1T[b, c, :, :]
                                       ).then_inc(e.LX[0].h, 16)
                e.LX[0].tick(16 * NDC)

                LRP = 2 * NDC
                TSD = LRP + NDC

                @block.vector
                def _(vector):
                    vector = Chained(vector, e.s_dve)
                    for s_h, n in bar:
                        if n:
                            vector.wait_ge(s_h.h, n)
                    vector.wait_ge(e.LX[0].h, lx0 + 16 * NDC)
                    vector.memset(xp[:, :, 0:1], 0.0)
                    for ts in range(4):
                        if ts >= 1:
                            vector.wait_ge(e.s_pe.h, pe0 + 16 * NDC * ts)
                        for c in range(NDC):
                            vector.tensor_scalar(
                                xk[:, c, :], xp[:, c, ts * 512 + 1:ts * 512 + 513],
                                e.vecs[:, 4 + wi, c:c + 1], None, op0=ALU.mult)
                            vector.scalar_tensor_tensor(
                                xk[:, c, :], xp[:, c, ts * 512:ts * 512 + 512],
                                e.dvecs[:, 2 + wi, c:c + 1], xk[:, c, :],
                                op0=ALU.mult, op1=ALU.add)
                        for ot in range(NDC):
                            g = ts * NDC + ot
                            vector.wait_ge(e.s_pe.h, pe0 + 16 * (g + 1))
                            if g >= 2:
                                vector.wait_ge(e.STT[g % 2].h,
                                               st0[g % 2] + 16 * (g // 2))
                            vector.tensor_scalar(stg[g % 2][:], psm[g % 2][:],
                                                 e.vecs[:, bcol, ot:ot + 1], None,
                                                 op0=ALU.add)

                @block.tensor
                def _(tensor):
                    for ts in range(4):
                        tensor.wait_ge(e.s_dve.h, dve0 + 1 + TSD * ts + LRP)
                        for ot in range(NDC):
                            g = ts * NDC + ot
                            if g >= 2:
                                tensor.wait_ge(e.s_dve.h,
                                               dve0 + 1 + TSD * ((g - 2) // NDC)
                                               + LRP + ((g - 2) % NDC) + 1)
                            for c in range(NDC):
                                tensor.matmul(psm[g % 2][:],
                                              WT[:, c, ot, :], xk[:, c, :],
                                              start=(c == 0), stop=(c == NDC - 1)
                                              ).then_inc(e.s_pe.h, 1)
                e.s_pe.tick(16 * 64)

                @block.gpsimd
                def _(gp):
                    for ts in range(4):
                        for ot in range(NDC):
                            g = ts * NDC + ot
                            gp.wait_ge(e.s_dve.h, dve0 + 1 + TSD * ts + LRP + ot + 1)
                            gp.dma_start(dstT[b, ot, :, ts * 512:(ts + 1) * 512],
                                         stg[g % 2][:]).then_inc(e.STT[g % 2].h, 16)
                e.STT[0].tick(16 * 32)
                e.STT[1].tick(16 * 32)



def emit_wkv(e, block):
    nc = e.nc
    CH = 256
    NCHK = T // CH
    HA = NDC // 2
    with ExitStack() as ctx:
        def sb(name, shape, dt=F32):
            return ctx.enter_context(nc.sbuf_tensor("wkv" + name, shape, dt))
        kc = sb("kc", [128, NDC, CH])
        vc = sb("vc", [128, NDC, CH])
        rc = sb("rc", [128, NDC, CH])
        t1 = sb("t1", [128, NDC, CH])
        t2 = sb("t2", [128, NDC, CH])
        t3 = sb("t3", [128, NDC, CH])
        t4 = sb("t4", [128, NDC, CH])
        pp = sb("pp", [128, NDC, CH + 1])
        aa = sb("aa", [128, NDC, CH + 1])
        bb = sb("bb", [128, NDC, CH + 1])
        rw = [sb(f"rw{i}", [128, NDC, CH], BF16) for i in range(2)]
        decB = sb("decB", [128, NDC // 2, CH])
        nbB = sb("nbB", [128, NDC // 2, CH])
        zeroB = sb("zeroB", [128, NDC // 2, CH])

        DVE_CHUNK = 73
        GP_CHUNK = 13
        ACT_CHUNK = 5

        def half(buf, h):
            return buf[:, h * HA:(h + 1) * HA, :]

        for b in range(BPC):
            bar = bar_of(e)
            dve0, gp0, act0 = e.s_dve.n, e.s_gp.n, e.s_act.n
            lx0 = [e.LX[0].n, e.LX[1].n]
            st0 = [e.STT[0].n, e.STT[1].n]

            def dvec(ci, k):  return dve0 + 9 + DVE_CHUNK * ci + k
            def gpc(ci, k):   return gp0 + GP_CHUNK * ci + k
            def actc(ci, k):  return act0 + ACT_CHUNK * ci + k

            @block.sync
            def _(sync):
                for s_h, n in bar:
                    if n:
                        sync.wait_ge(s_h.h, n)
                for ci in range(NCHK):
                    if ci >= 1:
                        sync.wait_ge(e.s_dve.h, dvec(ci - 1, DVE_CHUNK))
                        sync.wait_ge(e.s_gp.h, gpc(ci - 1, GP_CHUNK))
                        sync.wait_ge(e.s_act.h, actc(ci - 1, ACT_CHUNK))
                    q = e.LX[ci % 2].h
                    for c in range(NDC):
                        sync.dma_start(kc[:, c, :],
                                       e.kT[b, c, :, ci * CH:(ci + 1) * CH]
                                       ).then_inc(q, 16)
                    for c in range(NDC):
                        sync.dma_start(vc[:, c, :],
                                       e.vT[b, c, :, ci * CH:(ci + 1) * CH]
                                       ).then_inc(q, 16)
                    for c in range(NDC):
                        sync.dma_start(rc[:, c, :],
                                       e.rT[b, c, :, ci * CH:(ci + 1) * CH]
                                       ).then_inc(q, 16)
            e.LX[0].tick(16 * 3 * NDC * (NCHK // 2))
            e.LX[1].tick(16 * 3 * NDC * (NCHK // 2))

            def emit_half_init(eng, h):
                eng.memset(pp[:, h * HA:(h + 1) * HA, 0:1], -1e30)
                eng.memset(aa[:, h * HA:(h + 1) * HA, 0:1], 0.0)
                eng.memset(bb[:, h * HA:(h + 1) * HA, 0:1], 0.0)

            def emit_dve_chunk(vector, ci):
                h = 0
                ppP = pp[:, 0:HA, 0:CH]
                ppC = pp[:, 0:HA, 1:CH + 1]
                aaP = aa[:, 0:HA, 0:CH]
                bbP = bb[:, 0:HA, 0:CH]
                kch, vch, rch = half(kc, 0), half(vc, 0), half(rc, 0)
                t1h, t2h, t3h, t4h = half(t1, 0), half(t2, 0), half(t3, 0), half(t4, 0)
                k = 0
                vector.wait_ge(e.LX[ci % 2].h,
                               lx0[ci % 2] + 16 * 3 * NDC * (ci // 2 + 1))
                for c in range(NDC):
                    vector.tensor_tensor_scan(
                        pp[:, c, 1:CH + 1],
                        e.dvecs[:, 0, c:c + 1].broadcast_to([128, CH]),
                        kc[:, c, :], pp[:, c, 0:1],
                        op0=ALU.add, op1=ALU.max)
                    k += 1
                vector.tensor_tensor(t1h, ppP, ppC, op=ALU.subtract); k += 1
                vector.tensor_tensor(
                    t1h, t1h, e.dvecs[:, 0, 0:HA].unsqueeze(2)
                    .broadcast_to([128, HA, CH]), op=ALU.add); k += 1
                vector.wait_ge(e.s_act.h, actc(ci, 1))
                vector.tensor_tensor(t1h, kch, ppC, op=ALU.subtract); k += 1
                vector.wait_ge(e.s_act.h, actc(ci, 2))
                vector.tensor_tensor(t4h, t3h, vch, op=ALU.mult); k += 1
                for c in range(HA):
                    vector.tensor_tensor_scan(
                        aa[:, c, 1:CH + 1], t2[:, c, :], t4[:, c, :],
                        aa[:, c, 0:1], op0=ALU.mult, op1=ALU.add)
                    k += 1
                vector.wait_ge(e.s_gp.h, gpc(ci, 4))
                for c in range(HA, NDC):
                    vector.tensor_tensor_scan(
                        aa[:, c, 1:CH + 1], t2[:, c, :], t4[:, c, :],
                        aa[:, c, 0:1], op0=ALU.mult, op1=ALU.add)
                    k += 1
                for c in range(NDC):
                    vector.tensor_tensor_scan(
                        bb[:, c, 1:CH + 1], t2[:, c, :], t3[:, c, :],
                        bb[:, c, 0:1], op0=ALU.mult, op1=ALU.add)
                    k += 1
                vector.tensor_tensor(t1h, ppP, kch, op=ALU.subtract); k += 1
                vector.tensor_tensor(
                    t1h, t1h, e.dvecs[:, 1, 0:HA].unsqueeze(2)
                    .broadcast_to([128, HA, CH]), op=ALU.add); k += 1
                vector.tensor_scalar(t4h, t1h, 0.0, None, op0=ALU.min); k += 1
                vector.tensor_scalar(t1h, t1h, 0.0, None, op0=ALU.max); k += 1
                vector.wait_ge(e.s_gp.h, gpc(ci, 6))
                vector.tensor_scalar(half(t4, 1), half(t1, 1), 0.0, None,
                                     op0=ALU.min); k += 1
                vector.tensor_scalar(half(t1, 1), half(t1, 1), 0.0, None,
                                     op0=ALU.max); k += 1
                vector.wait_ge(e.s_act.h, actc(ci, 4))
                vector.tensor_tensor(t3h, t4h, aaP, op=ALU.mult); k += 1
                vector.tensor_tensor(vch, t1h, vch, op=ALU.mult); k += 1
                vector.tensor_tensor(t3h, t3h, vch, op=ALU.add); k += 1
                vector.tensor_tensor(t4h, t4h, bbP, op=ALU.mult); k += 1
                vector.tensor_tensor(t4h, t4h, t1h, op=ALU.add); k += 1
                vector.reciprocal(t4h, t4h); k += 1
                vector.wait_ge(e.s_gp.h, gpc(ci, 11))
                vector.reciprocal(half(t4, 1), half(t4, 1)); k += 1
                vector.tensor_tensor(t3h, t3h, t4h, op=ALU.mult); k += 1
                vector.wait_ge(e.s_act.h, actc(ci, 5))
                if ci >= 2:
                    oi = b * NCHK + ci
                    vector.wait_ge(e.STT[oi % 2].h,
                                   st0[oi % 2] + 16 * NDC * (ci // 2))
                vector.tensor_tensor(half(rw[ci % 2], 0), t3h, rch,
                                     op=ALU.mult); k += 1
                vector.tensor_copy(pp[:, 0:HA, 0:1], pp[:, 0:HA, CH:CH + 1]); k += 1
                vector.tensor_copy(aa[:, 0:HA, 0:1], aa[:, 0:HA, CH:CH + 1]); k += 1
                vector.tensor_copy(bb[:, 0:HA, 0:1], bb[:, 0:HA, CH:CH + 1]); k += 1
                vector.wait_ge(e.s_gp.h, gpc(ci, GP_CHUNK))
                vector.tensor_copy(pp[:, HA:NDC, 0:1], pp[:, HA:NDC, CH:CH + 1]); k += 1
                vector.tensor_copy(aa[:, HA:NDC, 0:1], aa[:, HA:NDC, CH:CH + 1]); k += 1
                vector.tensor_copy(bb[:, HA:NDC, 0:1], bb[:, HA:NDC, CH:CH + 1]); k += 1
                assert k == DVE_CHUNK, (k, DVE_CHUNK)

            def emit_gp_chunk(gpx, ci):
                ppP = pp[:, HA:NDC, 0:CH]
                ppC = pp[:, HA:NDC, 1:CH + 1]
                aaP = aa[:, HA:NDC, 0:CH]
                bbP = bb[:, HA:NDC, 0:CH]
                kch, vch, rch = half(kc, 1), half(vc, 1), half(rc, 1)
                t1h, t2h, t3h, t4h = half(t1, 1), half(t2, 1), half(t3, 1), half(t4, 1)
                zb = zeroB[:]
                k = 0
                gpx.wait_ge(e.LX[ci % 2].h,
                            lx0[ci % 2] + 16 * 3 * NDC * (ci // 2 + 1))
                gpx.wait_ge(e.s_dve.h, dvec(ci, 16))
                gpx.tensor_tensor(t1h, ppP, ppC, op=ALU.subtract); k += 1
                gpx.tensor_tensor(t1h, t1h, decB[:], op=ALU.add); k += 1
                gpx.wait_ge(e.s_act.h, actc(ci, 1))
                gpx.tensor_tensor(t1h, kch, ppC, op=ALU.subtract); k += 1
                gpx.wait_ge(e.s_act.h, actc(ci, 2))
                gpx.tensor_tensor(t4h, t3h, vch, op=ALU.mult); k += 1
                gpx.wait_ge(e.s_dve.h, dvec(ci, 36))
                gpx.tensor_tensor(t1h, ppP, kch, op=ALU.subtract); k += 1
                gpx.tensor_tensor(t1h, t1h, nbB[:], op=ALU.add); k += 1
                gpx.wait_ge(e.s_act.h, actc(ci, 4))
                gpx.wait_ge(e.s_dve.h, dvec(ci, 58))
                gpx.tensor_tensor(t3h, t4h, aaP, op=ALU.mult); k += 1
                gpx.tensor_tensor(vch, t1h, vch, op=ALU.mult); k += 1
                gpx.tensor_tensor(t3h, t3h, vch, op=ALU.add); k += 1
                gpx.tensor_tensor(t4h, t4h, bbP, op=ALU.mult); k += 1
                gpx.tensor_tensor(t4h, t4h, t1h, op=ALU.add); k += 1
                gpx.wait_ge(e.s_dve.h, dvec(ci, 65))
                gpx.tensor_tensor(t3h, t3h, t4h, op=ALU.mult); k += 1
                gpx.wait_ge(e.s_act.h, actc(ci, 5))
                if ci >= 2:
                    oi = b * NCHK + ci
                    gpx.wait_ge(e.STT[oi % 2].h,
                                st0[oi % 2] + 16 * NDC * (ci // 2))
                gpx.tensor_tensor(half(rw[ci % 2], 1), t3h, rch,
                                  op=ALU.mult); k += 1
                assert k == GP_CHUNK, (k, GP_CHUNK)

            @block.vector
            def _(vector):
                vector = Chained(vector, e.s_dve)
                for s_h, n in bar:
                    if n:
                        vector.wait_ge(s_h.h, n)
                emit_half_init(vector, 0)
                emit_half_init(vector, 1)
                vector.tensor_copy(decB[:], e.dvecs[:, 0, HA:NDC].unsqueeze(2)
                                   .broadcast_to([128, HA, CH]))
                vector.tensor_copy(nbB[:], e.dvecs[:, 1, HA:NDC].unsqueeze(2)
                                   .broadcast_to([128, HA, CH]))
                vector.memset(zeroB[:], 0.0)
                for ci in range(NCHK):
                    emit_dve_chunk(vector, ci)

            @block.gpsimd
            def _(gp):
                gpx = Chained(gp, e.s_gp)
                for s_h, n in bar:
                    if n:
                        gp.wait_ge(s_h.h, n)
                for ci in range(NCHK):
                    emit_gp_chunk(gpx, ci)
                    oi = b * NCHK + ci
                    gp.wait_ge(e.s_gp.h, gpc(ci, GP_CHUNK))
                    gp.wait_ge(e.s_dve.h, dvec(ci, DVE_CHUNK - 6))
                    for c in range(NDC):
                        gp.dma_start(e.rwkvT[b, c, :, ci * CH:(ci + 1) * CH],
                                     rw[ci % 2][:, c, :]).then_inc(e.STT[oi % 2].h, 16)

            e.STT[0].tick(16 * NDC * (NCHK // 2))
            e.STT[1].tick(16 * NDC * (NCHK // 2))

            @block.scalar
            def _(scalar):
                scalar = Chained(scalar, e.s_act)
                for s_h, n in bar:
                    if n:
                        scalar.wait_ge(s_h.h, n)
                for ci in range(NCHK):
                    scalar.wait_ge(e.s_dve.h, dvec(ci, 18))
                    scalar.wait_ge(e.s_gp.h, gpc(ci, 2))
                    scalar.activation(t2[:], t1[:], AF.Exp)
                    scalar.wait_ge(e.s_dve.h, dvec(ci, 19))
                    scalar.wait_ge(e.s_gp.h, gpc(ci, 3))
                    scalar.activation(t3[:], t1[:], AF.Exp)
                    scalar.wait_ge(e.s_dve.h, dvec(ci, 58))
                    scalar.wait_ge(e.s_gp.h, gpc(ci, 6))
                    scalar.activation(t4[:], t4[:], AF.Exp)
                    scalar.activation(t1[:], t1[:], AF.Exp, scale=-1.0)
                    scalar.wait_ge(e.LX[ci % 2].h,
                                   lx0[ci % 2] + 16 * 3 * NDC * (ci // 2 + 1))
                    scalar.activation(rc[:], rc[:], AF.Sigmoid)



def emit_wo(e, block):
    nc = e.nc
    with ExitStack() as ctx:
        def sb(name, shape, dt=F32):
            return ctx.enter_context(nc.sbuf_tensor("wo" + name, shape, dt))
        wtmp = [sb(f"wt{i}", [128, D]) for i in range(2)]
        wbf = sb("wbf", [128, D], BF16)
        WT = sb("WT", [128, NDC, NDC, 128], BF16)
        rws = sb("rws", [128, NDC, T], BF16)
        stg = [sb(f"stg{i}", [128, 512]) for i in range(2)]
        psq = ctx.enter_context(nc.psum_tensor("wopsq", [128, 2048], BF16))
        psm = [ctx.enter_context(nc.psum_tensor(f"wopsm{i}", [128, 512], F32))
               for i in range(2)]

        bar = bar_of(e)
        dve0, pe0 = e.s_dve.n, e.s_pe.n
        lw0 = [e.LW[0].n, e.LW[1].n]

        @block.gpsimd
        def _(gp):
            wait_bar(gp, bar)
            for l in range(NDC):
                gp.wait_ge(e.s_dve.h, dve0 + (2 * (l - 2) + 2 if l >= 2 else 0))
                gp.dma_start(wtmp[l % 2][:], e.Wo[l * 128:(l + 1) * 128, :]
                             ).then_inc(e.LW[l % 2].h, 16)
        e.LW[0].tick(16 * 8)
        e.LW[1].tick(16 * 8)

        @block.vector
        def _(vector):
            vector = Chained(vector, e.s_dve)
            wait_bar(vector, bar)
            for l in range(NDC):
                vector.wait_ge(e.LW[l % 2].h, lw0[l % 2] + 16 * (l // 2 + 1))
                vector.wait_ge(e.s_pe.h, pe0 + 16 * l)
                vector.tensor_copy(wbf[:], wtmp[l % 2][:])
                vector.wait_ge(e.s_pe.h, pe0 + 16 * (l + 1))
                vector.tensor_copy(WT[:, :, l, :],
                                   psq[:].rearrange("p (c n) -> p c n", n=128))

        @block.tensor
        def _(tensor):
            wait_bar(tensor, bar)
            for l in range(NDC):
                tensor.wait_ge(e.s_dve.h, dve0 + 2 * l + 1)
                for c in range(NDC):
                    tensor.transpose(psq[:, c * 128:(c + 1) * 128],
                                     wbf[:, c * 128:(c + 1) * 128],
                                     e.idb[:]).then_inc(e.s_pe.h, 1)
        e.s_pe.tick(16 * NDC)

        for b in range(BPC):
            bar = bar_of(e)
            dve0, pe0 = e.s_dve.n, e.s_pe.n
            lx0 = e.LX[0].n
            st0 = [e.STT[0].n, e.STT[1].n]

            @block.sync
            def _(sync):
                for s_h, n in bar:
                    if n:
                        sync.wait_ge(s_h.h, n)
                for c in range(NDC):
                    sync.dma_start(rws[:, c, :], e.rwkvT[b, c, :, :]
                                   ).then_inc(e.LX[0].h, 16)
            e.LX[0].tick(16 * NDC)

            @block.vector
            def _(vector):
                vector = Chained(vector, e.s_dve)
                for s_h, n in bar:
                    if n:
                        vector.wait_ge(s_h.h, n)
                for g in range(64):
                    ts, ot = divmod(g, NDC)
                    vector.wait_ge(e.s_pe.h, pe0 + 16 * (g + 1))
                    if g >= 2:
                        vector.wait_ge(e.STT[g % 2].h, st0[g % 2] + 16 * (g // 2))
                    vector.tensor_scalar(stg[g % 2][:], psm[g % 2][:],
                                         e.vecs[:, 3, ot:ot + 1], None,
                                         op0=ALU.add)

            @block.tensor
            def _(tensor):
                tensor.wait_ge(e.LX[0].h, lx0 + 16 * NDC)
                for g in range(64):
                    ts, ot = divmod(g, NDC)
                    if g >= 2:
                        tensor.wait_ge(e.s_dve.h, dve0 + g - 1)
                    for c in range(NDC):
                        tensor.matmul(psm[g % 2][:], WT[:, c, ot, :],
                                      rws[:, c, ts * 512:(ts + 1) * 512],
                                      start=(c == 0), stop=(c == NDC - 1)
                                      ).then_inc(e.s_pe.h, 1)
            e.s_pe.tick(16 * 64)

            @block.gpsimd
            def _(gp):
                for g in range(64):
                    ts, ot = divmod(g, NDC)
                    gp.wait_ge(e.s_dve.h, dve0 + g + 1)
                    gp.dma_start(e.s1[b, ot, :, ts * 512:(ts + 1) * 512],
                                 stg[g % 2][:]).then_inc(e.STT[g % 2].h, 16)
            e.STT[0].tick(16 * 32)
            e.STT[1].tick(16 * 32)



def emit_dq(e, block):
    nc = e.nc
    F1 = 0x3F000000
    F2 = 0x3F800000
    NTI = 2 * I1 // 128 + D // 128

    def tinfo(i):
        if i < 88:
            return ("gu", 256, 16)
        return ("dn", 704, 44)

    def rounds(nblk):
        out = []
        o = 0
        while o < nblk:
            out.append((o, min(16, nblk - o)))
            o += 16
        return out

    dve0, pe0 = e.s_dve.n, e.s_pe.n
    lw0 = [e.LW[0].n, e.LW[1].n]
    lx0 = e.LX[0].n
    st0 = [e.STT[0].n, e.STT[1].n]
    dve_n = dve0
    pe_n = pe0
    st_n = [st0[0], st0[1]]
    sched = []
    for i in range(NTI):
        kind, W, nblk = tinfo(i)
        s = {}
        if i == 0:
            dve_n += 2
        dve_n += 88
        s["plane_end"] = dve_n
        pe_n += 1
        s["sc_tr"] = pe_n
        dve_n += 1
        s["scb_end"] = dve_n
        dve_n += nblk
        s["mults_end"] = dve_n
        s["evac_end"] = []
        s["pe_round"] = []
        s["st_round"] = []
        for (o, nb) in rounds(nblk):
            pe_n += nb
            s["pe_round"].append(pe_n)
            dve_n += 1
            s["evac_end"].append(dve_n)
            st_n[i % 2] += 16 * nb
            s["st_round"].append(st_n[i % 2])
        sched.append(s)

    bar = bar_of(e)
    with ExitStack() as ctx:
        def sb(name, shape, dt=F32):
            return ctx.enter_context(nc.sbuf_tensor("dq" + name, shape, dt))
        gusf = sb("gusf", [64, 2 * I1])
        dnsf = sb("dnsf", [64, D])
        gush = sb("gush", [64, 2 * I1], F16)
        dnsh = sb("dnsh", [64, D], F16)
        pk = [sb(f"pk{i}", [128, 704], I32) for i in range(2)]
        nib = sb("nib", [128, 704], I32)
        b3s = sb("b3s", [128, 704], I32)
        rr = sb("rr", [128, 704], I32)
        bits = sb("bits", [128, 704], I32)
        msk = sb("msk", [128, 704], I32)
        s1i = sb("s1i", [128, 704], I32)
        alt = sb("alt", [128, 704])
        val = sb("val", [128, I1])
        scb = sb("scb", [128, 44])
        vbf = sb("vbf", [128, I1], BF16)
        stg = sb("stg", [128, NDC, 128], BF16)
        psq = ctx.enter_context(nc.psum_tensor("dqpsq", [128, 2048], BF16))
        pss = ctx.enter_context(nc.psum_tensor("dqpss", [128, 64], F32))

        @block.sync
        def _(sync):
            wait_bar(sync, bar)
            sync.dma_start(gush[0:16, :], e.gus[:]).then_inc(e.LX[0].h, 16)
            sync.dma_start(dnsh[0:44, :], e.dns[:]).then_inc(e.LX[0].h, 16)
        e.LX[0].tick(16 * 2)

        @block.gpsimd
        def _(gp):
            wait_bar(gp, bar)
            for i in range(NTI):
                kind, W, nblk = tinfo(i)
                if i >= 2:
                    gp.wait_ge(e.s_dve.h, sched[i - 2]["plane_end"])
                if kind == "gu":
                    gp.dma_start(pk[i % 2][:, 0:256],
                                 e.gup[i * 128:(i + 1) * 128, :]
                                 ).then_inc(e.LW[i % 2].h, 16)
                else:
                    j = i - 88
                    gp.dma_start(pk[i % 2][:],
                                 e.dnp[j * 128:(j + 1) * 128, :]
                                 ).then_inc(e.LW[i % 2].h, 16)
                for r, (o, nb) in enumerate(rounds(nblk)):
                    gp.wait_ge(e.s_dve.h, sched[i]["evac_end"][r])
                    for c2 in range(nb):
                        blk = o + c2
                        if kind == "gu":
                            dst = e.wguT[i, :, blk * 128:(blk + 1) * 128]
                        else:
                            dst = e.wdnT[blk, :, (i - 88) * 128:(i - 87) * 128]
                        gp.dma_start(dst, stg[:, c2, :]).then_inc(e.STT[i % 2].h, 16)
        e.STT[0].tick(st_n[0] - st0[0])
        e.STT[1].tick(st_n[1] - st0[1])
        e.LW[0].tick(16 * 44)
        e.LW[1].tick(16 * 44)

        @block.vector
        def _(vector):
            vector = Chained(vector, e.s_dve)
            wait_bar(vector, bar)
            for i in range(NTI):
                kind, W, nblk = tinfo(i)
                if i == 0:
                    vector.wait_ge(e.LX[0].h, lx0 + 32)
                    vector.tensor_copy(gusf[0:16, :], gush[0:16, :])
                    vector.tensor_copy(dnsf[0:44, :], dnsh[0:44, :])
                vector.wait_ge(e.LW[i % 2].h, lw0[i % 2] + 16 * (i // 2 + 1))
                p = pk[i % 2]
                for j in range(8):
                    vector.tensor_scalar(nib[:, :W], p[:, :W], 4 * j, 15,
                                         op0=ALU.logical_shift_right,
                                         op1=ALU.bitwise_and)
                    vector.tensor_scalar(b3s[:, :W], nib[:, :W], 8, 28,
                                         op0=ALU.bitwise_and,
                                         op1=ALU.logical_shift_left)
                    vector.tensor_scalar(rr[:, :W], nib[:, :W], 7, None,
                                         op0=ALU.bitwise_and)
                    vector.tensor_scalar(bits[:, :W], rr[:, :W], 22, None,
                                         op0=ALU.logical_shift_left)
                    vector.tensor_scalar(bits[:, :W], bits[:, :W], F1, None,
                                         op0=ALU.add)
                    vector.tensor_tensor(bits[:, :W], bits[:, :W], b3s[:, :W],
                                         op=ALU.bitwise_or)
                    vector.tensor_scalar(msk[:, :W], rr[:, :W], 2, None,
                                         op0=ALU.is_lt)
                    vector.tensor_scalar(s1i[:, :W], b3s[:, :W], F2, None,
                                         op0=ALU.bitwise_or)
                    vector.tensor_scalar(alt[:, :W], bits[:, :W].bitcast(F32),
                                         2.0, None, op0=ALU.mult)
                    vector.tensor_tensor(alt[:, :W], alt[:, :W],
                                         s1i[:, :W].bitcast(F32), op=ALU.subtract)
                    vector.select(val[:, j:8 * W:8], msk[:, :W], alt[:, :W],
                                  bits[:, :W].bitcast(F32), add_drain=True)
                vector.wait_ge(e.s_pe.h, sched[i]["sc_tr"])
                vector.tensor_copy(scb[:, 0:nblk], pss[:, 0:nblk])
                for g2 in range(nblk):
                    vector.tensor_scalar(vbf[:, g2 * 128:(g2 + 1) * 128],
                                         val[:, g2 * 128:(g2 + 1) * 128],
                                         scb[:, g2:g2 + 1], None, op0=ALU.mult)
                for r, (o, nb) in enumerate(rounds(nblk)):
                    vector.wait_ge(e.s_pe.h, sched[i]["pe_round"][r])
                    if r > 0:
                        vector.wait_ge(e.STT[i % 2].h, sched[i]["st_round"][r - 1])
                    elif i >= 1:
                        vector.wait_ge(e.STT[(i - 1) % 2].h,
                                       sched[i - 1]["st_round"][-1])
                    vector.tensor_copy(stg[:, 0:nb, :],
                                       psq[:, :nb * 128]
                                       .rearrange("p (c n) -> p c n", n=128))

        @block.tensor
        def _(tensor):
            wait_bar(tensor, bar)
            for i in range(NTI):
                kind, W, nblk = tinfo(i)
                tensor.wait_ge(e.s_dve.h,
                               sched[i - 1]["scb_end"] if i else dve0 + 2)
                if kind == "gu":
                    tensor.transpose(pss[:, 0:16],
                                     gusf[0:16, i * 128:(i + 1) * 128],
                                     e.idf[0:16, 0:16]).then_inc(e.s_pe.h, 1)
                else:
                    j = i - 88
                    tensor.transpose(pss[:, 0:44],
                                     dnsf[0:44, j * 128:(j + 1) * 128],
                                     e.idf[0:44, 0:44]).then_inc(e.s_pe.h, 1)
                tensor.wait_ge(e.s_dve.h, sched[i]["mults_end"])
                for r, (o, nb) in enumerate(rounds(nblk)):
                    if r > 0:
                        tensor.wait_ge(e.s_dve.h, sched[i]["evac_end"][r - 1])
                    elif i >= 1:
                        tensor.wait_ge(e.s_dve.h, sched[i - 1]["evac_end"][-1])
                    for c2 in range(nb):
                        tensor.transpose(psq[:, c2 * 128:(c2 + 1) * 128],
                                         vbf[:, (o + c2) * 128:(o + c2 + 1) * 128],
                                         e.idb[:]).then_inc(e.s_pe.h, 1)
        e.s_pe.tick(pe_n - pe0)



def emit_gu(e, block):
    nc = e.nc
    NP = 44
    NU = BPC * 4
    bar = bar_of(e)
    dve0, pe0, act0 = e.s_dve.n, e.s_pe.n, e.s_act.n
    lx0 = [e.LX[0].n, e.LX[1].n]
    st0 = [e.STT[0].n, e.STT[1].n]
    PEP = 32 * NU

    def pe_iu(i, u):   return pe0 + PEP * i + 32 * (u + 1)
    def act_iu(i, u):  return act0 + NU * i + u + 1
    def dve_iu(i, u):  return dve0 + NU * i + u + 1
    def st_iu(i, u):
        gi = i * NU + u
        return st0[gi % 2] + 16 * (gi // 2 + 1)

    def lx_pair(i):
        if i % 2 == 0:
            return e.LX[0], lx0[0] + 16 * BPC * NDC + 16 * 2 * (i // 2 + 1)
        return e.LX[1], lx0[1] + 16 * 2 * (i // 2 + 1)

    with ExitStack() as ctx:
        def sb(name, shape, dt=F32):
            return ctx.enter_context(nc.sbuf_tensor("gu" + name, shape, dt))
        xn1 = sb("xn1", [128, BPC * NDC, T], BF16)
        wg = [sb(f"wg{i}", [128, D], BF16) for i in range(2)]
        wu = [sb(f"wu{i}", [128, D], BF16) for i in range(2)]
        sg = [sb(f"sg{i}", [128, 512]) for i in range(2)]
        h2s = [sb(f"h2s{i}", [128, 512], BF16) for i in range(2)]
        psG = [ctx.enter_context(nc.psum_tensor(f"gupsG{i}", [128, 512], F32))
               for i in range(2)]
        psU = [ctx.enter_context(nc.psum_tensor(f"gupsU{i}", [128, 512], F32))
               for i in range(2)]

        @block.sync
        def _(sync):
            wait_bar(sync, bar)
            for b in range(BPC):
                for c in range(NDC):
                    sync.dma_start(xn1[:, b * NDC + c, :], e.xn1T[b, c, :, :]
                                   ).then_inc(e.LX[0].h, 16)
            for i in range(NP):
                if i >= 2:
                    sync.wait_ge(e.s_pe.h, pe_iu(i - 2, NU - 1))
                q = e.LX[i % 2].h
                sync.dma_start(wg[i % 2][:], e.wguT[i, :, :]).then_inc(q, 16)
                sync.dma_start(wu[i % 2][:], e.wguT[i + 44, :, :]).then_inc(q, 16)
        e.LX[0].tick(16 * BPC * NDC + 16 * 2 * ((NP + 1) // 2))
        e.LX[1].tick(16 * 2 * (NP // 2))

        @block.tensor
        def _(tensor):
            wait_bar(tensor, bar)
            for i in range(NP):
                s_, v_ = lx_pair(i)
                tensor.wait_ge(s_.h, v_)
                for u in range(NU):
                    b, ts = divmod(u, 4)
                    if i * NU + u >= 2:
                        pi, pu = divmod(i * NU + u - 2, NU)
                        tensor.wait_ge(e.s_act.h, act_iu(pi, pu))
                        tensor.wait_ge(e.s_dve.h, dve_iu(pi, pu))
                    for c in range(NDC):
                        tensor.matmul(psG[u % 2][:],
                                      wg[i % 2][:, c * 128:(c + 1) * 128],
                                      xn1[:, b * NDC + c, ts * 512:(ts + 1) * 512],
                                      start=(c == 0), stop=(c == NDC - 1)
                                      ).then_inc(e.s_pe.h, 1)
                    for c in range(NDC):
                        tensor.matmul(psU[u % 2][:],
                                      wu[i % 2][:, c * 128:(c + 1) * 128],
                                      xn1[:, b * NDC + c, ts * 512:(ts + 1) * 512],
                                      start=(c == 0), stop=(c == NDC - 1)
                                      ).then_inc(e.s_pe.h, 1)
        e.s_pe.tick(PEP * NP)

        @block.scalar
        def _(scalar):
            scalar = Chained(scalar, e.s_act)
            wait_bar(scalar, bar)
            for i in range(NP):
                for u in range(NU):
                    scalar.wait_ge(e.s_pe.h, pe0 + PEP * i + 32 * u + 16)
                    if i * NU + u >= 2:
                        pi, pu = divmod(i * NU + u - 2, NU)
                        scalar.wait_ge(e.s_dve.h, dve_iu(pi, pu))
                    scalar.activation(sg[u % 2][:], psG[u % 2][:], AF.Silu)

        @block.vector
        def _(vector):
            vector = Chained(vector, e.s_dve)
            wait_bar(vector, bar)
            for i in range(NP):
                for u in range(NU):
                    gi = i * NU + u
                    vector.wait_ge(e.s_act.h, act_iu(i, u))
                    vector.wait_ge(e.s_pe.h, pe_iu(i, u))
                    if gi >= 2:
                        vector.wait_ge(e.STT[gi % 2].h, st_iu(i, u) - 16)
                    vector.tensor_tensor(h2s[u % 2][:], sg[u % 2][:],
                                         psU[u % 2][:], op=ALU.mult)

        @block.gpsimd
        def _(gp):
            wait_bar(gp, bar)
            for i in range(NP):
                for u in range(NU):
                    b, ts = divmod(u, 4)
                    gi = i * NU + u
                    gp.wait_ge(e.s_dve.h, dve_iu(i, u))
                    gp.dma_start(e.h2[i, :, b * T + ts * 512:b * T + (ts + 1) * 512],
                                 h2s[u % 2][:]).then_inc(e.STT[gi % 2].h, 16)
        e.STT[0].tick(16 * (NP * NU // 2))
        e.STT[1].tick(16 * (NP * NU // 2))



def emit_down(e, block):
    nc = e.nc
    NH2 = 44
    with ExitStack() as ctx:
        def sb(name, shape, dt=F32):
            return ctx.enter_context(nc.sbuf_tensor("dn" + name, shape, dt))
        h2c = sb("h2c", [128, NH2, 512], BF16)
        wq = sb("wq", [128, NH2, 512], BF16)
        ybuf = sb("ybuf", [128, 512])
        ytok = [sb(f"ytok{i}", [128, D]) for i in range(4)]
        junk = sb("junk", [128, D], BF16)
        red = sb("red", [128, 4])
        g1 = sb("g1", [128, D])
        s1c = sb("s1c", [128, NDC, 128])
        s1tok = sb("s1tok", [128, D])
        x1t = sb("x1t", [128, D])
        outt = [sb(f"outt{i}", [128, D]) for i in range(2)]
        psm = [ctx.enter_context(nc.psum_tensor(f"dnpsm{i}", [128, 512], F32))
               for i in range(2)]
        psf = ctx.enter_context(nc.psum_tensor("dnpsf", [128, 512], F32))
        psf2 = ctx.enter_context(nc.psum_tensor("dnpsf2", [128, 2048], F32))

        for b in range(BPC):
            for tsb in range(4):
                gidx = b * 4 + tsb
                bar = bar_of(e)
                dve0, pe0, act0 = e.s_dve.n, e.s_pe.n, e.s_act.n
                lx0 = [e.LX[0].n, e.LX[1].n]
                st0 = [e.STT[0].n, e.STT[1].n]

                def fdve(tt, k):  return dve0 + 80 + 5 * tt + k
                def fact(tt, k):  return act0 + 3 * tt + k
                def fst(tt):
                    return st0[tt % 2] + 16 * (tt // 2 + 1)

                @block.sync
                def _(sync):
                    for s_h, n in bar:
                        if n:
                            sync.wait_ge(s_h.h, n)
                    for c in range(NH2):
                        sync.dma_start(
                            h2c[:, c, :],
                            e.h2[c, :, b * T + tsb * 512:b * T + (tsb + 1) * 512]
                        ).then_inc(e.LX[0].h, 16)
                    for q in range(4):
                        if q > 0:
                            sync.wait_ge(e.s_pe.h, pe0 + 192 * q)
                        for c in range(NH2):
                            sync.dma_start(wq[:, c, :],
                                           e.wdnT[c, :, q * 512:(q + 1) * 512]
                                           ).then_inc(e.LX[1].h, 16)
                    for tt in range(4):
                        t0 = tsb * 512 + tt * 128
                        if tt >= 1:
                            sync.wait_ge(e.s_pe.h, pe0 + 768 + 16 * tt)
                            sync.wait_ge(e.s_dve.h, fdve(tt - 1, 5))
                        sync.dma_start(s1c[:, :, :],
                                       e.s1[b, :, :, t0:t0 + 128]
                                       .transpose([1, 0, 2])).then_inc(e.LX[0].h, 16)
                        sync.dma_start(x1t[:], e.x1[b, t0:t0 + 128, :]
                                       ).then_inc(e.LX[0].h, 16)
                e.LX[0].tick(16 * (NH2 + 8))
                e.LX[1].tick(16 * 4 * NH2)

                @block.tensor
                def _(tensor):
                    for s_h, n in bar:
                        if n:
                            tensor.wait_ge(s_h.h, n)
                    tensor.wait_ge(e.LX[0].h, lx0[0] + 16 * NH2)
                    for q in range(4):
                        tensor.wait_ge(e.LX[1].h, lx0[1] + 16 * NH2 * (q + 1))
                        for dq_ in range(4):
                            g = q * 4 + dq_
                            if g >= 2:
                                tensor.wait_ge(e.s_dve.h, dve0 + 5 * (g - 2) + 1)
                            for c in range(NH2):
                                tensor.matmul(psm[g % 2][:],
                                              wq[:, c, dq_ * 128:(dq_ + 1) * 128],
                                              h2c[:, c, :],
                                              start=(c == 0), stop=(c == NH2 - 1)
                                              ).then_inc(e.s_pe.h, 1)
                            tensor.wait_ge(e.s_dve.h, dve0 + 5 * g + 1)
                            if g >= 1:
                                tensor.wait_ge(e.s_dve.h, dve0 + 5 * g)
                            for tt in range(4):
                                tensor.transpose(psf[:, tt * 128:(tt + 1) * 128],
                                                 ybuf[:, tt * 128:(tt + 1) * 128],
                                                 e.idf[:]).then_inc(e.s_pe.h, 1)
                    for tt in range(4):
                        tensor.wait_ge(e.LX[0].h,
                                       lx0[0] + 16 * (NH2 + 2 * (tt + 1)))
                        if tt >= 1:
                            tensor.wait_ge(e.s_dve.h, fdve(tt - 1, 3))
                        for c in range(NDC):
                            tensor.transpose(psf2[:, c * 128:(c + 1) * 128],
                                             s1c[:, c, :], e.idf[:]
                                             ).then_inc(e.s_pe.h, 1)
                e.s_pe.tick(192 * 4 + 64)

                @block.scalar
                def _(scalar):
                    scalar = Chained(scalar, e.s_act)
                    for s_h, n in bar:
                        if n:
                            scalar.wait_ge(s_h.h, n)
                    for tt in range(4):
                        scalar.wait_ge(e.s_dve.h, dve0 + 80)
                        if tt >= 1:
                            scalar.wait_ge(e.s_dve.h, fdve(tt - 1, 5))
                        scalar.activation(junk[:], ytok[tt][:], AF.Square,
                                          accum_out=red[:, 0:1])
                        scalar.wait_ge(e.s_dve.h, fdve(tt, 1))
                        scalar.activation(red[:, 2:3], red[:, 1:2], AF.Sqrt)
                        scalar.wait_ge(e.s_dve.h, fdve(tt, 2))
                        scalar.activation(g1[:], ytok[tt][:], AF.Silu,
                                          scale=red[:, 3:4])

                @block.vector
                def _(vector):
                    vector = Chained(vector, e.s_dve)
                    for s_h, n in bar:
                        if n:
                            vector.wait_ge(s_h.h, n)
                    for q in range(4):
                        for dq_ in range(4):
                            g = q * 4 + dq_
                            vector.wait_ge(e.s_pe.h, pe0 + 48 * g + 44)
                            vector.tensor_copy(ybuf[:], psm[g % 2][:])
                            vector.wait_ge(e.s_pe.h, pe0 + 48 * (g + 1))
                            for tt in range(4):
                                dc = (q * 4 + dq_) * 128
                                vector.tensor_copy(ytok[tt][:, dc:dc + 128],
                                                   psf[:, tt * 128:(tt + 1) * 128])
                    for tt in range(4):
                        vector.wait_ge(e.s_act.h, fact(tt, 1))
                        vector.tensor_scalar(red[:, 1:2], red[:, 0:1], 1.0 / D,
                                             1e-6, op0=ALU.mult, op1=ALU.add)
                        vector.wait_ge(e.s_act.h, fact(tt, 2))
                        vector.reciprocal(red[:, 3:4], red[:, 2:3])
                        vector.wait_ge(e.s_pe.h, pe0 + 768 + 16 * (tt + 1))
                        vector.tensor_copy(s1tok[:], psf2[:])
                        vector.wait_ge(e.s_act.h, fact(tt, 3))
                        if tt >= 2:
                            vector.wait_ge(e.STT[tt % 2].h, fst(tt - 2))
                        vector.tensor_tensor(outt[tt % 2][:], s1tok[:], g1[:],
                                             op=ALU.mult)
                        vector.tensor_tensor(outt[tt % 2][:], outt[tt % 2][:],
                                             x1t[:], op=ALU.add)

                @block.gpsimd
                def _(gp):
                    for tt in range(4):
                        t0 = tsb * 512 + tt * 128
                        fi = gidx * 4 + tt
                        gp.wait_ge(e.s_dve.h, fdve(tt, 5))
                        gp.dma_start(e.out[b, t0:t0 + 128, :], outt[tt % 2][:]
                                     ).then_inc(e.STT[tt % 2].h, 16)
                e.STT[0].tick(16 * 2)
                e.STT[1].tick(16 * 2)


def build_program(phases=ALL_PHASES, debug_out=()):
    nc = bass.Bass("TRN2", target_bir_lowering=False)
    e = Env()
    e.nc = nc

    def din(name, shape, dt=F32):
        return nc.dram_tensor(name, list(shape), dt, kind="ExternalInput").ap()

    e.x = din("x", [BPC, T, D])
    e.w_kvd = din("w_kvd", [D, KVC])
    e.w_upk = din("w_upk", [KVC, D])
    e.w_upv = din("w_upv", [KVC, D])
    e.w_qd = din("w_qd", [D, QC])
    e.w_qu = din("w_qu", [QC, D])
    e.w_kr = din("w_kr", [D, HD])
    e.w_qr = din("w_qr", [D, HD])
    e.w_out = din("w_out", [D, D])
    e.w_fc = din("w_fc", [FFH, D])
    e.w_proj = din("w_proj", [D, FFH])
    e.Wk = din("Wk", [D, D])
    e.Wv = din("Wv", [D, D])
    e.Wr = din("Wr", [D, D])
    e.Wo = din("Wo", [D, D])
    for v in ("bk", "bv", "br", "bo", "tmk", "tmv", "tmr", "td", "tf"):
        setattr(e, v, din(v, [128, NDC]))
    e.gup = din("gup", [2 * I1, D // 8], I32)
    e.gus = din("gus", [D // 128, 2 * I1], F16)
    e.dnp = din("dnp", [D, I1 // 8], I32)
    e.dns = din("dns", [I1 // 128, D], F16)
    e.idb_d = din("idb", [128, 128], BF16)
    e.idf_d = din("idf", [128, 128], F32)
    e.cos2_d = din("cos2", [128, T])
    e.sin2_d = din("sin2", [128, T])
    e.onescol_d = din("onescol", [128, 1])
    e.onesrow_d = din("onesrow", [1, 128])

    e.out = nc.dram_tensor("out", [BPC, T, D], F32, kind="ExternalOutput").ap()

    def scratch(name, shape, dt=F32):
        if name in debug_out:
            return nc.dram_tensor(name, list(shape), dt, kind="ExternalOutput").ap()
        return nc.dram_tensor(name, list(shape), dt, kind="Internal").ap()

    e.xnT = scratch("xnT", [BPC, NDC, 128, T], BF16)
    e.hmid = scratch("hmid", [FFH // 128, 128, BPC * T], BF16)
    e.g0 = scratch("g0", [BPC, NDC, 128, T])
    e.s0 = scratch("s0", [BPC, NDC, 128, T])
    e.outTd = scratch("outTd", [BPC, HEADS, 128, T], BF16)
    e.x1 = scratch("x1", [BPC, T, D])
    e.xn1T = scratch("xn1T", [BPC, NDC, 128, T], BF16)
    e.kT = scratch("kT", [BPC, NDC, 128, T])
    e.vT = scratch("vT", [BPC, NDC, 128, T])
    e.rT = scratch("rT", [BPC, NDC, 128, T])
    e.rwkvT = scratch("rwkvT", [BPC, NDC, 128, T], BF16)
    e.s1 = scratch("s1", [BPC, NDC, 128, T])
    e.wguT = scratch("wguT", [2 * I1 // 128, 128, D], BF16)
    e.wdnT = scratch("wdnT", [I1 // 128, 128, D], BF16)
    e.h2 = scratch("h2", [I1 // 128, 128, BPC * T], BF16)
    e.tern = scratch("tern", [1, 8])

    with ExitStack() as ctx:
        e.top_ctx = ctx
        e.idb = ctx.enter_context(nc.sbuf_tensor("c_idb", [128, 128], BF16))
        e.idf = ctx.enter_context(nc.sbuf_tensor("c_idf", [128, 128], F32))
        e.onescol = ctx.enter_context(nc.sbuf_tensor("c_onescol", [128, 1], F32))
        e.onesrow = ctx.enter_context(nc.sbuf_tensor("c_onesrow", [1, 128], F32))
        for nm in ("thrfc", "nthrfc", "thrpj", "nthrpj", "amtot"):
            setattr(e, nm, ctx.enter_context(nc.sbuf_tensor("c_" + nm, [128, 1], F32)))
        e.vecs = ctx.enter_context(nc.sbuf_tensor("c_vecs", [128, 9, NDC], F32))
        e.dvecs = ctx.enter_context(nc.sbuf_tensor("c_dvecs", [128, 6, NDC], F32))
        for nm in ("s_pe", "s_dve", "s_act", "s_gp"):
            setattr(e, nm, S(ctx, nc, nm))
        e.LX = [S(ctx, nc, f"lx{i}") for i in range(2)]
        e.LW = [S(ctx, nc, f"lw{i}") for i in range(2)]
        e.STT = [S(ctx, nc, f"st{i}") for i in range(2)]
        e.STX = [S(ctx, nc, f"sx{i}") for i in range(2)]

        block = ctx.enter_context(nc.Block())

        @block.sync
        def _(sync):
            for dst, src in [(e.idb, e.idb_d), (e.idf, e.idf_d),
                             (e.onescol, e.onescol_d), (e.onesrow, e.onesrow_d)]:
                sync.dma_start(dst[:], src[:]).then_inc(e.LX[0].h, 16)
        e.LX[0].tick(16 * 4)

        if "ln0" in phases:
            emit_ln(e, block, e.x, e.xnT, tag="ln0")
        if "tstat" in phases:
            emit_tstat(e, block)
        if "ffn0h" in phases:
            emit_ffn0h(e, block)
        if "ffn0p" in phases:
            emit_ffn0p(e, block)
        if "attn" in phases:
            emit_attn(e, block)
        if "comb0" in phases:
            emit_ln(e, block, e.x, e.xn1T, dst_x_ap=e.x1,
                    s0g0=(e.s0, e.g0), tag="cm0")
        if "kvr" in phases or "wkv" in phases or "wo" in phases:
            emit_l1prep(e, block)
        if "kvr" in phases:
            emit_kvr(e, block)
        if "wkv" in phases:
            emit_wkv(e, block)
        if "wo" in phases:
            emit_wo(e, block)
        if "dq" in phases:
            emit_dq(e, block)
        if "gu" in phases:
            emit_gu(e, block)
        if "down" in phases:
            emit_down(e, block)

        @block.sync
        def _(sync):
            for s in e.STT + e.STX + e.LW + e.LX:
                if s.n:
                    sync.wait_ge(s.h, s.n)
    return nc



def make_consts():
    idb = np.eye(128).astype(ml_dtypes.bfloat16)
    idf = np.eye(128).astype(np.float32)
    j = np.arange(64)
    inv = (10000.0 ** (-(2.0 * j) / 128.0)).astype(np.float64)
    t = np.arange(T, dtype=np.float64)
    ang = np.outer(inv, t)
    cos2 = np.repeat(np.cos(ang), 2, axis=0).astype(np.float32)
    sin = np.sin(ang)
    sin2 = np.empty((128, T), np.float64)
    sin2[0::2] = -sin
    sin2[1::2] = sin
    sin2 = sin2.astype(np.float32)
    onescol = np.ones((128, 1), np.float32)
    onesrow = np.ones((1, 128), np.float32)
    return dict(idb=idb, idf=idf, cos2=cos2, sin2=sin2,
                onescol=onescol, onesrow=onesrow)


def vec_2d(v):
    return np.ascontiguousarray(np.asarray(v).reshape(NDC, 128).T.astype(np.float32))


def make_in_maps(inputs):
    consts = make_consts()
    common = dict(
        w_kvd=inputs["L0_attn_w_kv_down"], w_upk=inputs["L0_attn_w_kv_up_k"],
        w_upv=inputs["L0_attn_w_kv_up_v"], w_qd=inputs["L0_attn_w_q_down"],
        w_qu=inputs["L0_attn_w_q_up"], w_kr=inputs["L0_attn_w_k_rope"],
        w_qr=inputs["L0_attn_w_q_rope"], w_out=inputs["L0_attn_w_out"],
        w_fc=inputs["L0_ffn_w_fc"], w_proj=inputs["L0_ffn_w_proj"],
        Wk=inputs["L1_attn_Wk"], Wv=inputs["L1_attn_Wv"],
        Wr=inputs["L1_attn_Wr"], Wo=inputs["L1_attn_Wo"],
        bk=vec_2d(inputs["L1_attn_bk"]), bv=vec_2d(inputs["L1_attn_bv"]),
        br=vec_2d(inputs["L1_attn_br"]), bo=vec_2d(inputs["L1_attn_bo"]),
        tmk=vec_2d(inputs["L1_attn_tmk"]), tmv=vec_2d(inputs["L1_attn_tmv"]),
        tmr=vec_2d(inputs["L1_attn_tmr"]), td=vec_2d(inputs["L1_attn_time_decay"]),
        tf=vec_2d(inputs["L1_attn_time_first"]),
        gup=inputs["L1_ffn_gate_up_packed"], gus=inputs["L1_ffn_gate_up_scales"],
        dnp=inputs["L1_ffn_down_packed"], dns=inputs["L1_ffn_down_scales"],
        **consts,
    )
    common = {k: np.ascontiguousarray(v) for k, v in common.items()}
    x = np.asarray(inputs["L0_attn_x"])
    in_maps = []
    for c in range(NCORES):
        m = dict(common)
        m["x"] = np.ascontiguousarray(x[c * BPC:(c + 1) * BPC])
        in_maps.append(m)
    return in_maps


def kernel(**inputs):
    in_maps = make_in_maps(inputs)
    nc = build_program()
    res = run_bass_kernel_spmd(nc, in_maps, core_ids=list(range(NCORES)))
    return np.concatenate([np.asarray(r["out"]) for r in res.results], axis=0)


# version 08
# speedup vs baseline: 1.0538x; 1.0538x over previous
import numpy as np
import ml_dtypes
import concourse.bass as bass
import concourse.mybir as mybir
from concourse.bass_utils import run_bass_kernel_spmd
from contextlib import ExitStack

F32 = mybir.dt.float32
BF16 = mybir.dt.bfloat16
F16 = mybir.dt.float16
I32 = mybir.dt.int32
AF = mybir.ActivationFunctionType
ALU = mybir.AluOpType
AX = mybir.AxisListType

B, T, D = 16, 2048, 2048
NCORES = 8
BPC = B // NCORES
HEADS, HD = 16, 128
KVC, QC = 512, 1024
FFH = 5120
I1 = 5632
NT = T // 128
NDC = D // 128

ALL_PHASES = ("ln0", "tstat", "ffn0h", "ffn0p", "attn", "comb0",
              "kvr", "wkv", "wo", "dq", "gu", "down")


class S:
    def __init__(self, ctx, nc, name):
        self.h = ctx.enter_context(nc.semaphore(name))
        self.n = 0

    def tick(self, k=1):
        self.n += k
        return self.n


class Env:
    pass


class Chained:

    def __init__(self, eng, s):
        self._eng = eng
        self._s = s

    def wait_ge(self, h, n):
        self._eng.wait_ge(h, n)

    def __getattr__(self, name):
        f = getattr(self._eng, name)

        def wrapped(*a, **kw):
            self._eng.wait_ge(self._s.h, self._s.n)
            instr = f(*a, **kw)
            instr.then_inc(self._s.h, 1)
            self._s.tick()
            return instr
        return wrapped


def bar_of(e):
    return [(s, s.n) for s in (e.s_pe, e.s_dve, e.s_act, e.s_gp)
            + tuple(e.LX) + tuple(e.LW) + tuple(e.STT) + tuple(e.STX)]


def wait_bar(eng, bar):
    for s, n in bar:
        if n:
            eng.wait_ge(s.h, n)



def emit_ln(e, block, src_x_ap, dst_T_ap, dst_x_ap=None, s0g0=None, tag="ln"):
    nc = e.nc
    comb = s0g0 is not None
    ntiles = BPC * NT
    LD = 3 if comb else 1
    DVE = 41 if comb else 23
    PE = 32 if comb else 16
    ACT = 2
    STD = 17 if comb else 16
    dve0, pe0, act0 = e.s_dve.n, e.s_pe.n, e.s_act.n
    lx0 = [e.LX[0].n, e.LX[1].n]
    st0 = [e.STT[0].n, e.STT[1].n]

    def ld(i):      return lx0[i % 2] + 16 * LD * (i // 2 + 1)
    def dve(i, k):  return dve0 + DVE * i + k
    def pe(i, k):   return pe0 + PE * i + k
    def act(i, k):  return act0 + ACT * i + k
    def st(i):      return st0[i % 2] + 16 * STD * (i // 2 + 1)

    bar = bar_of(e)

    with ExitStack() as ctx:
        def sb(name, shape, dt=F32):
            return ctx.enter_context(nc.sbuf_tensor(tag + name, shape, dt))
        xt = [sb(f"x{i}", [128, D]) for i in range(2)]
        xn = [sb(f"xn{i}", [128, D], BF16) for i in range(2)]
        stage = [sb(f"stg{i}", [128, NDC, 128], BF16) for i in range(2)]
        junk = sb("junk", [128, D], BF16)
        red = [sb(f"red{i}", [128, 8]) for i in range(2)]
        if comb:
            s_ = [sb(f"s{i}", [128, NDC, 128]) for i in range(2)]
            g_ = [sb(f"g{i}", [128, NDC, 128]) for i in range(2)]
            sgT = [sb(f"sgT{i}", [128, NDC, 128]) for i in range(2)]
            sgtok = [sb(f"sgtok{i}", [128, D]) for i in range(2)]
            psf = ctx.enter_context(nc.psum_tensor(tag + "psf", [128, 2048], F32))
        pst = [ctx.enter_context(nc.psum_tensor(tag + f"pst{i}", [128, 2048], BF16))
               for i in range(2)]

        @block.sync
        def _(sync):
            wait_bar(sync, bar)
            for i in range(ntiles):
                b, tt = divmod(i, NT)
                if i >= 2:
                    sync.wait_ge(e.s_dve.h, dve(i - 2, DVE))
                    sync.wait_ge(e.s_act.h, act(i - 2, ACT))
                    if comb:
                        sync.wait_ge(e.STT[i % 2].h, st(i - 2))
                q = e.LX[i % 2].h
                sync.dma_start(xt[i % 2][:],
                               src_x_ap[b, tt * 128:(tt + 1) * 128, :]).then_inc(q, 16)
                if comb:
                    sync.dma_start(s_[i % 2][:],
                                   s0g0[0][b, :, :, tt * 128:(tt + 1) * 128]
                                   .transpose([1, 0, 2])).then_inc(q, 16)
                    sync.dma_start(g_[i % 2][:],
                                   s0g0[1][b, :, :, tt * 128:(tt + 1) * 128]
                                   .transpose([1, 0, 2])).then_inc(q, 16)
        e.LX[0].tick(16 * LD * ((ntiles + 1) // 2))
        e.LX[1].tick(16 * LD * (ntiles // 2))

        @block.tensor
        def _(tensor):
            wait_bar(tensor, bar)
            for i in range(ntiles):
                p = i % 2
                if comb:
                    tensor.wait_ge(e.s_dve.h, dve(i, 1))
                    if i >= 1:
                        tensor.wait_ge(e.s_dve.h, dve(i - 1, 17))
                    for c in range(NDC):
                        tensor.transpose(psf[:, c * 128:(c + 1) * 128],
                                         sgT[p][:, c, :], e.idf[:]
                                         ).then_inc(e.s_pe.h, 1)
                tensor.wait_ge(e.s_dve.h, dve(i, DVE - 16))
                if i >= 2:
                    tensor.wait_ge(e.s_dve.h, dve(i - 2, DVE))
                for c in range(NDC):
                    tensor.transpose(pst[p][:, c * 128:(c + 1) * 128],
                                     xn[p][:, c * 128:(c + 1) * 128],
                                     e.idb[:]).then_inc(e.s_pe.h, 1)
        e.s_pe.tick(PE * ntiles)

        @block.scalar
        def _(scalar):
            scalar = Chained(scalar, e.s_act)
            wait_bar(scalar, bar)
            for i in range(ntiles):
                p = i % 2
                scalar.wait_ge(e.LX[p].h, ld(i))
                if i >= 2:
                    scalar.wait_ge(e.s_dve.h, dve(i - 2, DVE - 16))
                if comb:
                    scalar.wait_ge(e.s_dve.h, dve(i, 18))
                scalar.activation(junk[:], xt[p][:], AF.Square,
                                  accum_out=red[p][:, 1:2])
                scalar.wait_ge(e.s_dve.h, dve(i, DVE - 18))
                scalar.activation(red[p][:, 5:6], red[p][:, 3:4], AF.Sqrt
                                  )

        @block.vector
        def _(vector):
            vector = Chained(vector, e.s_dve)
            wait_bar(vector, bar)
            for i in range(ntiles):
                p = i % 2
                k = 0
                vector.wait_ge(e.LX[p].h, ld(i))
                if comb:
                    vector.tensor_tensor(sgT[p][:], s_[p][:], g_[p][:],
                                         op=ALU.mult)
                    k += 1
                    vector.wait_ge(e.s_pe.h, pe(i, 16))
                    for c in range(NDC):
                        vector.tensor_copy(sgtok[p][:, c * 128:(c + 1) * 128],
                                           psf[:, c * 128:(c + 1) * 128]
                                           )
                        k += 1
                    vector.wait_ge(e.s_dve.h, dve(i, k))
                    vector.tensor_tensor(xt[p][:], xt[p][:], sgtok[p][:],
                                         op=ALU.add)
                    k += 1
                    vector.wait_ge(e.s_dve.h, dve(i, k))
                vector.reduce_sum(red[p][:, 0:1], xt[p][:], axis=AX.X
                                  )
                k += 1
                vector.wait_ge(e.s_dve.h, dve(i, k))
                vector.tensor_scalar(red[p][:, 2:3], red[p][:, 0:1], -1.0 / D, None,
                                     op0=ALU.mult)
                k += 1
                vector.wait_ge(e.s_act.h, act(i, 1))
                vector.tensor_scalar(red[p][:, 3:4], red[p][:, 1:2], 1.0 / D, 1e-5,
                                     op0=ALU.mult, op1=ALU.add)
                k += 1
                vector.wait_ge(e.s_dve.h, dve(i, k))
                vector.tensor_tensor(red[p][:, 4:5], red[p][:, 2:3], red[p][:, 2:3],
                                     op=ALU.mult)
                k += 1
                vector.wait_ge(e.s_dve.h, dve(i, k))
                vector.tensor_tensor(red[p][:, 3:4], red[p][:, 3:4], red[p][:, 4:5],
                                     op=ALU.subtract)
                k += 1
                vector.wait_ge(e.s_act.h, act(i, 2))
                vector.reciprocal(red[p][:, 6:7], red[p][:, 5:6])
                k += 1
                vector.wait_ge(e.s_dve.h, dve(i, k))
                vector.tensor_scalar(xn[p][:], xt[p][:], red[p][:, 2:3], red[p][:, 6:7],
                                     op0=ALU.add, op1=ALU.mult)
                k += 1
                vector.wait_ge(e.s_pe.h, pe(i, PE))
                if i >= 2:
                    vector.wait_ge(e.STT[p].h, st(i - 2))
                for c in range(NDC):
                    vector.tensor_copy(stage[p][:, c, :],
                                       pst[p][:, c * 128:(c + 1) * 128]
                                       )
                    k += 1
                assert k == DVE, (k, DVE)

        @block.gpsimd
        def _(gpsimd):
            wait_bar(gpsimd, bar)
            for i in range(ntiles):
                b, tt = divmod(i, NT)
                p = i % 2
                gpsimd.wait_ge(e.s_dve.h, dve(i, DVE))
                q = e.STT[p].h
                if comb:
                    gpsimd.dma_start(dst_x_ap[b, tt * 128:(tt + 1) * 128, :],
                                     xt[p][:]).then_inc(q, 16)
                for c in range(NDC):
                    gpsimd.dma_start(dst_T_ap[b, c, :, tt * 128:(tt + 1) * 128],
                                     stage[p][:, c, :]).then_inc(q, 16)
        e.STT[0].tick(16 * STD * ((ntiles + 1) // 2))
        e.STT[1].tick(16 * STD * (ntiles // 2))



def emit_tstat(e, block):
    nc = e.nc
    bar = bar_of(e)
    dve0, pe0, act0 = e.s_dve.n, e.s_pe.n, e.s_act.n
    lw0 = [e.LW[0].n, e.LW[1].n]
    NTI = 40 + 16

    def lw(i):  return lw0[i % 2] + 16 * (i // 2 + 1)

    with ExitStack() as ctx:
        def sb(name, shape, dt=F32):
            return ctx.enter_context(nc.sbuf_tensor("ts" + name, shape, dt))
        wt = [sb(f"w{i}", [128, FFH]) for i in range(2)]
        junk = sb("junk", [128, FFH], BF16)
        acc = sb("acc", [128, 56])
        tot = sb("tot", [128, 2])
        sc = sb("sc", [1, 8])
        ps1 = ctx.enter_context(nc.psum_tensor("tsps1", [1, 8], F32))
        ps2 = ctx.enter_context(nc.psum_tensor("tsps2", [128, 8], F32))

        @block.gpsimd
        def _(gp):
            wait_bar(gp, bar)
            for i in range(NTI):
                if i >= 2:
                    gp.wait_ge(e.s_act.h, act0 + i - 1)
                if i < 40:
                    src = e.w_fc[i * 128:(i + 1) * 128, :]
                else:
                    src = e.w_proj[(i - 40) * 128:(i - 39) * 128, :]
                gp.dma_start(wt[i % 2][:, :src.shape[1]], src
                             ).then_inc(e.LW[i % 2].h, 16)
        e.LW[0].tick(16 * ((NTI + 1) // 2))
        e.LW[1].tick(16 * (NTI // 2))

        @block.scalar
        def _(scalar):
            scalar = Chained(scalar, e.s_act)
            wait_bar(scalar, bar)
            for i in range(NTI):
                w = 2048 if i < 40 else FFH
                scalar.wait_ge(e.LW[i % 2].h, lw(i))
                scalar.activation(junk[:, :w], wt[i % 2][:, :w], AF.Abs,
                                  accum_out=acc[:, i:i + 1])

        @block.vector
        def _(vector):
            vector = Chained(vector, e.s_dve)
            wait_bar(vector, bar)
            vector.wait_ge(e.s_act.h, act0 + NTI)
            vector.reduce_sum(tot[:, 0:1], acc[:, 0:40], axis=AX.X)
            vector.reduce_sum(tot[:, 1:2], acc[:, 40:56], axis=AX.X)
            vector.wait_ge(e.s_pe.h, pe0 + 1)
            vector.tensor_scalar(sc[0:1, 0:1], ps1[0:1, 0:1], 1.0 / (FFH * D), 1e-5,
                                 op0=ALU.mult, op1=ALU.max)
            vector.tensor_scalar(sc[0:1, 2:3], ps1[0:1, 1:2], 1.0 / (FFH * D), 1e-5,
                                 op0=ALU.mult, op1=ALU.max)
            vector.wait_ge(e.s_dve.h, dve0 + 4)
            vector.tensor_scalar(sc[0:1, 1:2], sc[0:1, 0:1], 0.7, None,
                                 op0=ALU.mult)
            vector.tensor_scalar(sc[0:1, 3:4], sc[0:1, 2:3], 0.7, None,
                                 op0=ALU.mult)
            vector.tensor_tensor(sc[0:1, 5:6], sc[0:1, 0:1], sc[0:1, 0:1],
                                 op=ALU.mult)
            vector.wait_ge(e.s_dve.h, dve0 + 7)
            vector.tensor_tensor(sc[0:1, 4:5], sc[0:1, 5:6], sc[0:1, 2:3],
                                 op=ALU.mult)
            vector.wait_ge(e.s_pe.h, pe0 + 2)
            vector.tensor_copy(e.thrfc[:], ps2[:, 1:2])
            vector.tensor_copy(e.thrpj[:], ps2[:, 3:4])
            vector.tensor_copy(e.amtot[:], ps2[:, 4:5])
            vector.wait_ge(e.s_dve.h, dve0 + 11)
            vector.tensor_scalar(e.nthrfc[:], e.thrfc[:], -1.0, None,
                                 op0=ALU.mult)
            vector.tensor_scalar(e.nthrpj[:], e.thrpj[:], -1.0, None,
                                 op0=ALU.mult)

        @block.tensor
        def _(tensor):
            wait_bar(tensor, bar)
            tensor.wait_ge(e.s_dve.h, dve0 + 2)
            tensor.matmul(ps1[0:1, 0:2], e.onescol[:], tot[:, 0:2],
                          start=True, stop=True).then_inc(e.s_pe.h, 1)
            tensor.wait_ge(e.s_dve.h, dve0 + 8)
            tensor.matmul(ps2[:, 0:6], e.onesrow[:], sc[0:1, 0:6],
                          start=True, stop=True).then_inc(e.s_pe.h, 1)
        e.s_pe.tick(2)

        @block.gpsimd
        def _(gp):
            gp.wait_ge(e.s_dve.h, dve0 + 13)
            gp.dma_start(e.tern[0:1, 0:6], sc[0:1, 0:6]).then_inc(e.STT[0].h, 16)
        e.STT[0].tick(16)



def emit_ffn0h(e, block):
    nc = e.nc
    bar = bar_of(e)
    NO = FFH // 128
    NSUB = BPC * 4
    DVE = 4 + NSUB
    PE = NDC + NSUB * NDC
    ACT = NSUB
    dve0, pe0, act0 = e.s_dve.n, e.s_pe.n, e.s_act.n
    lx0 = e.LX[0].n
    lw0 = [e.LW[0].n, e.LW[1].n]
    st0 = [e.STX[0].n, e.STX[1].n]

    def lw(i):      return lw0[i % 2] + 16 * (i // 2 + 1)
    def dve(i, k):  return dve0 + DVE * i + k
    def pe(i, k):   return pe0 + PE * i + k
    def act(i, k):  return act0 + ACT * i + k
    def st(gi):     return st0[gi % 2] + 16 * (gi // 2 + 1)

    with ExitStack() as ctx:
        def sb(name, shape, dt=F32):
            return ctx.enter_context(nc.sbuf_tensor("f0h" + name, shape, dt))
        xnT_sb = sb("xnT", [128, BPC * NDC, T], BF16)
        wt = [sb(f"w{i}", [128, D]) for i in range(2)]
        m1 = sb("m1", [128, D])
        m2 = sb("m2", [128, D])
        q = sb("q", [128, D], BF16)
        fcT = sb("fcT", [128, NDC, 128], BF16)
        relu = [sb(f"relu{i}", [128, 512]) for i in range(2)]
        ht = [sb(f"ht{i}", [128, 512], BF16) for i in range(2)]
        psq = ctx.enter_context(nc.psum_tensor("f0hpsq", [128, 2048], BF16))
        psm = [ctx.enter_context(nc.psum_tensor(f"f0hpsm{i}", [128, 512], F32))
               for i in range(2)]

        @block.sync
        def _(sync):
            wait_bar(sync, bar)
            for b in range(BPC):
                for c in range(NDC):
                    sync.dma_start(xnT_sb[:, b * NDC + c, :], e.xnT[b, c, :, :]
                                   ).then_inc(e.LX[0].h, 16)
            for i in range(NO):
                for s_ in range(NSUB):
                    b, ts = divmod(s_, 4)
                    gi = i * NSUB + s_
                    sync.wait_ge(e.s_act.h, act(i, s_ + 1))
                    sync.dma_start(e.hmid[i, :, b * T + ts * 512:b * T + (ts + 1) * 512],
                                   ht[gi % 2][:]).then_inc(e.STX[gi % 2].h, 16)
        e.LX[0].tick(16 * BPC * NDC)
        e.STX[0].tick(16 * (NO * NSUB // 2))
        e.STX[1].tick(16 * (NO * NSUB // 2))

        @block.gpsimd
        def _(gp):
            wait_bar(gp, bar)
            for i in range(NO):
                if i >= 2:
                    gp.wait_ge(e.s_dve.h, dve(i - 2, 3))
                gp.dma_start(wt[i % 2][:], e.w_fc[i * 128:(i + 1) * 128, :]
                             ).then_inc(e.LW[i % 2].h, 16)
        e.LW[0].tick(16 * ((NO + 1) // 2))
        e.LW[1].tick(16 * (NO // 2))

        @block.vector
        def _(vector):
            vector = Chained(vector, e.s_dve)
            wait_bar(vector, bar)
            for i in range(NO):
                p = i % 2
                vector.wait_ge(e.LW[p].h, lw(i))
                k = 0
                vector.tensor_scalar(m1[:], wt[p][:], e.thrfc[:], None,
                                     op0=ALU.is_gt); k += 1
                vector.tensor_scalar(m2[:], wt[p][:], e.nthrfc[:], None,
                                     op0=ALU.is_lt); k += 1
                vector.wait_ge(e.s_dve.h, dve(i, k))
                vector.tensor_tensor(q[:], m1[:], m2[:], op=ALU.subtract
                                     ); k += 1
                vector.wait_ge(e.s_pe.h, pe(i, NDC))
                vector.tensor_copy(fcT[:], psq[:].rearrange("p (c n) -> p c n", n=128)
                                  ); k += 1
                for s_ in range(NSUB):
                    g = i * NSUB + s_
                    vector.wait_ge(e.s_pe.h, pe(i, NDC + (s_ + 1) * NDC))
                    if g >= 2:
                        vector.wait_ge(e.s_act.h, act0 + g - 1)
                    vector.tensor_scalar(relu[s_ % 2][:], psm[s_ % 2][:], 0.0, None,
                                         op0=ALU.max); k += 1

        @block.tensor
        def _(tensor):
            wait_bar(tensor, bar)
            tensor.wait_ge(e.LX[0].h, lx0 + 16 * BPC * NDC)
            for i in range(NO):
                tensor.wait_ge(e.s_dve.h, dve(i, 3))
                if i >= 1:
                    tensor.wait_ge(e.s_dve.h, dve(i - 1, 4))
                for c in range(NDC):
                    tensor.transpose(psq[:, c * 128:(c + 1) * 128],
                                     q[:, c * 128:(c + 1) * 128], e.idb[:]
                                     ).then_inc(e.s_pe.h, 1)
                tensor.wait_ge(e.s_dve.h, dve(i, 4))
                for s_ in range(NSUB):
                    b, ts = divmod(s_, 4)
                    g = i * NSUB + s_
                    if g >= 2:
                        pi, ps = divmod(g - 2, NSUB)
                        tensor.wait_ge(e.s_dve.h, dve(pi, 5 + ps))
                    for c in range(NDC):
                        tensor.matmul(psm[s_ % 2][:], fcT[:, c, :],
                                      xnT_sb[:, b * NDC + c, ts * 512:(ts + 1) * 512],
                                      start=(c == 0), stop=(c == NDC - 1)
                                      ).then_inc(e.s_pe.h, 1)
        e.s_pe.tick(PE * NO)

        @block.scalar
        def _(scalar):
            scalar = Chained(scalar, e.s_act)
            wait_bar(scalar, bar)
            for i in range(NO):
                for s_ in range(NSUB):
                    gi = i * NSUB + s_
                    scalar.wait_ge(e.s_dve.h, dve(i, 5 + s_))
                    if gi >= 2:
                        scalar.wait_ge(e.STX[gi % 2].h, st(gi - 2))
                    scalar.activation(ht[gi % 2][:], relu[s_ % 2][:], AF.Square
                                      )



def emit_ffn0p(e, block):
    nc = e.nc
    NOC = FFH // 128
    NDH = 8
    TS = 256
    NSUB = T // TS
    NU = BPC * NSUB
    bar = bar_of(e)
    dve0, pe0, act0 = e.s_dve.n, e.s_pe.n, e.s_act.n
    lw0 = e.LW[0].n
    lx0 = [e.LX[0].n, e.LX[1].n]
    st0 = [e.STT[0].n, e.STT[1].n]
    PREP_ACT, PREP_DVE, PREP_PE = 2, 5, 40
    MM_PE = NDH * NOC
    H_ACT = PREP_ACT * NDH + NU * NDH
    H_DVE = PREP_DVE * NDH
    H_PE = PREP_PE * NDH + NU * MM_PE

    def p_act(h, i, k): return act0 + H_ACT * h + PREP_ACT * i + k
    def p_dve(h, i, k): return dve0 + H_DVE * h + PREP_DVE * i + k
    def p_pe(h, i, k):  return pe0 + H_PE * h + PREP_PE * i + k
    def m_pe(h, u, j, k):
        return pe0 + H_PE * h + PREP_PE * NDH + MM_PE * u + NOC * j + k
    def m_act(h, u, j):
        return act0 + H_ACT * h + PREP_ACT * NDH + NDH * u + j + 1
    def m_ld(h, u):
        g = h * NU + u
        return lx0[g % 2] + 16 * NOC * (g // 2 + 1)
    def m_st(h, u, j):
        gi = h * NU * NDH + u * NDH + j
        return st0[gi % 2] + 16 * (gi // 2 + 1)

    with ExitStack() as ctx:
        def sb(name, shape, dt=F32):
            return ctx.enter_context(nc.sbuf_tensor("f0p" + name, shape, dt))
        wt = sb("w", [128, FFH])
        a = sb("a", [128, FFH])
        mbf = sb("mbf", [128, FFH], BF16)
        q = sb("q", [128, FFH], BF16)
        projT = sb("projT", [128, NOC, NDH, 128], BF16)
        h_sb = [sb(f"h{i}", [128, NOC, TS], BF16) for i in range(2)]
        g_t = [sb(f"g{i}", [128, TS]) for i in range(2)]
        psq = ctx.enter_context(nc.psum_tensor("f0ppsq", [128, 2048], BF16))
        psm = [ctx.enter_context(nc.psum_tensor(f"f0ppsm{i}", [128, TS], F32))
               for i in range(2)]

        @block.gpsimd
        def _(gp):
            wait_bar(gp, bar)
            for h in range(2):
                for i in range(NDH):
                    gi = h * NDH + i
                    if gi:
                        ph, pi = divmod(gi - 1, NDH)
                        gp.wait_ge(e.s_act.h, p_act(ph, pi, 2))
                    dd0 = gi * 128
                    gp.dma_start(wt[:], e.w_proj[dd0:dd0 + 128, :]
                                 ).then_inc(e.LW[0].h, 16)
                for u in range(NU):
                    b, ts = divmod(u, NSUB)
                    for j in range(NDH):
                        gi = h * NU * NDH + u * NDH + j
                        gp.wait_ge(e.s_act.h, m_act(h, u, j))
                        gp.dma_start(
                            e.g0[b, h * NDH + j, :, ts * TS:(ts + 1) * TS],
                            g_t[(u * NDH + j) % 2][:]).then_inc(e.STT[gi % 2].h, 16)
        e.LW[0].tick(16 * 2 * NDH)
        e.STT[0].tick(16 * NU * NDH)
        e.STT[1].tick(16 * NU * NDH)

        @block.scalar
        def _(scalar):
            scalar = Chained(scalar, e.s_act)
            wait_bar(scalar, bar)
            for h in range(2):
                for i in range(NDH):
                    gi = h * NDH + i
                    scalar.wait_ge(e.LW[0].h, lw0 + 16 * (gi + 1))
                    if gi:
                        ph, pi = divmod(gi - 1, NDH)
                        scalar.wait_ge(e.s_dve.h, p_dve(ph, pi, 2))
                        scalar.wait_ge(e.s_pe.h, p_pe(ph, pi, PREP_PE))
                    scalar.activation(a[:], wt[:], AF.Abs)
                    scalar.activation(q[:], wt[:], AF.Sign)
                for u in range(NU):
                    for j in range(NDH):
                        gi = u * NDH + j
                        gfull = h * NU * NDH + gi
                        scalar.wait_ge(e.s_pe.h, m_pe(h, u, j, NOC))
                        if gfull >= 2:
                            phh, rem = divmod(gfull - 2, NU * NDH)
                            pu, pj = divmod(rem, NDH)
                            scalar.wait_ge(e.STT[gfull % 2].h, m_st(phh, pu, pj))
                        scalar.activation(g_t[gi % 2][:], psm[j % 2][:], AF.Silu,
                                          scale=e.amtot[:, :])

        @block.vector
        def _(vector):
            vector = Chained(vector, e.s_dve)
            wait_bar(vector, bar)
            for h in range(2):
                for i in range(NDH):
                    k = 0
                    vector.wait_ge(e.s_act.h, p_act(h, i, 2))
                    vector.tensor_scalar(mbf[:], a[:], e.thrpj[:], None,
                                         op0=ALU.is_gt); k += 1
                    vector.wait_ge(e.s_dve.h, p_dve(h, i, k))
                    vector.tensor_tensor(q[:], q[:], mbf[:], op=ALU.mult
                                         ); k += 1
                    for r, (o0, no) in enumerate([(0, 16), (16, 16), (32, 8)]):
                        vector.wait_ge(e.s_pe.h, p_pe(h, i, o0 + no))
                        vector.tensor_copy(
                            projT[:, o0:o0 + no, i, :],
                            psq[:, :no * 128].rearrange("p (c n) -> p c n", n=128)
                        ); k += 1

        @block.tensor
        def _(tensor):
            wait_bar(tensor, bar)
            for h in range(2):
                for i in range(NDH):
                    if h:
                        tensor.wait_ge(e.s_pe.h, pe0 + H_PE)
                    tensor.wait_ge(e.s_dve.h, p_dve(h, i, 2))
                    for r, (o0, no) in enumerate([(0, 16), (16, 16), (32, 8)]):
                        if r:
                            tensor.wait_ge(e.s_dve.h, p_dve(h, i, 2 + r))
                        elif h or i:
                            ph, pi = divmod(h * NDH + i - 1, NDH)
                            tensor.wait_ge(e.s_dve.h, p_dve(ph, pi, 5))
                        for c in range(no):
                            tensor.transpose(psq[:, c * 128:(c + 1) * 128],
                                             q[:, (o0 + c) * 128:(o0 + c + 1) * 128],
                                             e.idb[:]).then_inc(e.s_pe.h, 1)
                tensor.wait_ge(e.s_dve.h, p_dve(h, NDH - 1, 5))
                for u in range(NU):
                    g = h * NU + u
                    tensor.wait_ge(e.LX[g % 2].h, m_ld(h, u))
                    for j in range(NDH):
                        gi = u * NDH + j
                        if gi >= 2:
                            pu, pj = divmod(gi - 2, NDH)
                            tensor.wait_ge(e.s_act.h, m_act(h, pu, pj))
                        for c in range(NOC):
                            tensor.matmul(psm[j % 2][:], projT[:, c, j, :],
                                          h_sb[u % 2][:, c, :],
                                          start=(c == 0), stop=(c == NOC - 1)
                                          ).then_inc(e.s_pe.h, 1)
        e.s_pe.tick(2 * H_PE)

        @block.sync
        def _(sync):
            wait_bar(sync, bar)
            for h in range(2):
                for u in range(NU):
                    b, ts = divmod(u, NSUB)
                    g = h * NU + u
                    if g >= 2:
                        ph, pu = divmod(g - 2, NU)
                        sync.wait_ge(e.s_pe.h, m_pe(ph, pu, NDH - 1, NOC))
                    for c in range(NOC):
                        sync.dma_start(
                            h_sb[u % 2][:, c, :],
                            e.hmid[c, :, b * T + ts * TS:b * T + (ts + 1) * TS]
                        ).then_inc(e.LX[g % 2].h, 16)
        e.LX[0].tick(16 * NOC * NU)
        e.LX[1].tick(16 * NOC * NU)



def emit_attn_batch(e, block, b):
    nc = e.nc

    with ExitStack() as bctx:
        def bsb(name, shape, dt=F32):
            return bctx.enter_context(nc.sbuf_tensor(f"at{b}{name}", shape, dt))
        abctx = bctx.enter_context(ExitStack())
        def absb(name, shape, dt=F32):
            return abctx.enter_context(nc.sbuf_tensor(f"at{b}{name}", shape, dt))
        kvlat = absb("kvlat", [128, 4, T], BF16)
        qlat = absb("qlat", [128, 8, T], BF16)
        krT = absb("krT", [128, T], BF16)
        qrT = absb("qrT", [128, T], BF16)
        psA = [bctx.enter_context(nc.psum_tensor(f"at{b}psA{i}", [128, 512], F32))
               for i in range(2)]
        psS = bctx.enter_context(nc.psum_tensor(f"at{b}psS", [128, 2048], F32))
        psT = bctx.enter_context(nc.psum_tensor(f"at{b}psT", [128, 2048], BF16))

        with ExitStack() as actx:
            def asb(name, shape, dt=F32):
                return actx.enter_context(nc.sbuf_tensor(f"at{b}a{name}", shape, dt))
            xnT_sb = asb("xnT", [128, NDC, T], BF16)
            wtmp = [asb(f"wt{i}", [128, 1024]) for i in range(2)]
            a1ctx = actx.enter_context(ExitStack())
            def a1sb(name, shape, dt=F32):
                return a1ctx.enter_context(nc.sbuf_tensor(f"at{b}a{name}", shape, dt))
            cosb = a1sb("cos", [128, T])
            sinb = a1sb("sin", [128, T])
            krw = a1sb("krw", [128, NDC, 128], BF16)
            qrw = a1sb("qrw", [128, NDC, 128], BF16)
            krf = a1sb("krf", [128, T])
            qrf = a1sb("qrf", [128, T])
            krs = a1sb("krs", [128, T])

            bar = bar_of(e)
            dve0, pe0 = e.s_dve.n, e.s_pe.n
            lx0 = e.LX[0].n
            lw0 = [e.LW[0].n, e.LW[1].n]

            @block.sync
            def _(sync):
                wait_bar(sync, bar)
                sync.dma_start(cosb[:], e.cos2_d[:]).then_inc(e.LX[0].h, 16)
                sync.dma_start(sinb[:], e.sin2_d[:]).then_inc(e.LX[0].h, 16)
                for c in range(NDC):
                    sync.dma_start(xnT_sb[:, c, :], e.xnT[b, c, :, :]
                                   ).then_inc(e.LX[0].h, 16)
            e.LX[0].tick(16 * (2 + NDC))

            @block.gpsimd
            def _(gp):
                wait_bar(gp, bar)
                for l in range(32):
                    w = e.w_kr if l < 16 else e.w_qr
                    c = l % 16
                    if l >= 2:
                        gp.wait_ge(e.s_dve.h, dve0 + l - 1)
                    gp.dma_start(wtmp[l % 2][:, 0:128], w[c * 128:(c + 1) * 128, :]
                                 ).then_inc(e.LW[l % 2].h, 16)
            e.LW[0].tick(16 * 16)
            e.LW[1].tick(16 * 16)

            @block.vector
            def _(vector):
                vector = Chained(vector, e.s_dve)
                wait_bar(vector, bar)
                for l in range(32):
                    vector.wait_ge(e.LW[l % 2].h, lw0[l % 2] + 16 * (l // 2 + 1))
                    dst = krw if l < 16 else qrw
                    vector.tensor_copy(dst[:, l % 16, :], wtmp[l % 2][:, 0:128])
                for g in range(8):
                    ts, j = divmod(g, 2)
                    vector.wait_ge(e.s_pe.h, pe0 + 16 * (g + 1))
                    dst = krf if j == 0 else qrf
                    vector.tensor_copy(dst[:, ts * 512:(ts + 1) * 512],
                                       psA[g % 2][:])
                vector.wait_ge(e.LX[0].h, lx0 + 16 * (2 + NDC))
                vector.stream_shuffle(krs[:], krf[:],
                                      [i ^ 1 for i in range(32)])
                vector.tensor_tensor(krf[:], krf[:], cosb[:], op=ALU.mult)
                vector.tensor_tensor(krs[:], krs[:], sinb[:], op=ALU.mult)
                vector.tensor_tensor(krT[:], krf[:], krs[:], op=ALU.add)
                vector.stream_shuffle(krs[:], qrf[:],
                                      [i ^ 1 for i in range(32)])
                vector.tensor_tensor(qrf[:], qrf[:], cosb[:], op=ALU.mult)
                vector.tensor_tensor(krs[:], krs[:], sinb[:], op=ALU.mult)
                vector.tensor_tensor(qrT[:], qrf[:], krs[:], op=ALU.add)

            @block.tensor
            def _(tensor):
                wait_bar(tensor, bar)
                tensor.wait_ge(e.LX[0].h, lx0 + 16 * (2 + NDC))
                for g in range(8):
                    ts, j = divmod(g, 2)
                    tensor.wait_ge(e.s_dve.h, dve0 + (16 if j == 0 else 32))
                    if g >= 2:
                        tensor.wait_ge(e.s_dve.h, dve0 + 32 + g - 1)
                    w = krw if j == 0 else qrw
                    for c in range(NDC):
                        tensor.matmul(psA[g % 2][:], w[:, c, :],
                                      xnT_sb[:, c, ts * 512:(ts + 1) * 512],
                                      start=(c == 0), stop=(c == NDC - 1)
                                      ).then_inc(e.s_pe.h, 1)
            e.s_pe.tick(16 * 8)

            a1ctx.close()
            kvdb = asb("kvdb", [128, NDC, 512], BF16)
            qdb = asb("qdb", [128, NDC, 1024], BF16)
            dve0, pe0 = e.s_dve.n, e.s_pe.n
            lw0 = [e.LW[0].n, e.LW[1].n]

            @block.gpsimd
            def _(gp):
                for l in range(32):
                    gp.wait_ge(e.s_dve.h, dve0 + (l - 1 if l >= 2 else 0))
                    if l < 16:
                        src = e.w_kvd[l * 128:(l + 1) * 128, :]
                        gp.dma_start(wtmp[l % 2][:, 0:512], src
                                     ).then_inc(e.LW[l % 2].h, 16)
                    else:
                        c = l - 16
                        src = e.w_qd[c * 128:(c + 1) * 128, :]
                        gp.dma_start(wtmp[l % 2][:, 0:1024], src
                                     ).then_inc(e.LW[l % 2].h, 16)
            e.LW[0].tick(16 * 16)
            e.LW[1].tick(16 * 16)

            @block.vector
            def _(vector):
                vector = Chained(vector, e.s_dve)
                for l in range(32):
                    vector.wait_ge(e.LW[l % 2].h, lw0[l % 2] + 16 * (l // 2 + 1))
                    if l < 16:
                        vector.tensor_copy(kvdb[:, l, :], wtmp[l % 2][:, 0:512])
                    else:
                        vector.tensor_copy(qdb[:, l - 16, :], wtmp[l % 2][:, 0:1024])
                for g in range(48):
                    ts, j = divmod(g, 12)
                    vector.wait_ge(e.s_pe.h, pe0 + 16 * (g + 1))
                    if j < 4:
                        vector.tensor_copy(kvlat[:, j, ts * 512:(ts + 1) * 512],
                                           psA[g % 2][:])
                    else:
                        vector.tensor_copy(qlat[:, j - 4, ts * 512:(ts + 1) * 512],
                                           psA[g % 2][:])

            @block.tensor
            def _(tensor):
                tensor.wait_ge(e.s_dve.h, dve0)
                for g in range(48):
                    ts, j = divmod(g, 12)
                    tensor.wait_ge(e.s_dve.h, dve0 + (16 if j < 4 else 32))
                    if g >= 2:
                        tensor.wait_ge(e.s_dve.h, dve0 + 32 + g - 1)
                    for c in range(NDC):
                        if j < 4:
                            lhs = kvdb[:, c, j * 128:(j + 1) * 128]
                        else:
                            lhs = qdb[:, c, (j - 4) * 128:(j - 3) * 128]
                        tensor.matmul(psA[g % 2][:], lhs,
                                      xnT_sb[:, c, ts * 512:(ts + 1) * 512],
                                      start=(c == 0), stop=(c == NDC - 1)
                                      ).then_inc(e.s_pe.h, 1)
            e.s_pe.tick(16 * 48)

        sto0 = [e.STT[0].n, e.STT[1].n]
        with ExitStack() as hctx:
            def hsb(name, shape, dt=F32):
                return hctx.enter_context(nc.sbuf_tensor(f"at{b}h{name}", shape, dt))
            wtm2 = [hsb(f"wt{i}", [128, 128]) for i in range(2)]
            upk = hsb("upk", [128, 4, 128], BF16)
            upv = hsb("upv", [128, 4, 128], BF16)
            quw = hsb("quw", [128, 8, 128], BF16)
            kcT = hsb("kcT", [128, T], BF16)
            vtok = hsb("vtok", [128, NDC, 128], BF16)
            qcT = hsb("qcT", [128, T], BF16)
            ef = [hsb(f"ef{i}", [128, T]) for i in range(2)]
            attnT = hsb("attnT", [128, NDC, 512], BF16)
            nmx = hsb("nmx", [128, 4])
            sums = hsb("sums", [128, 2, 4])
            stg_o = [hsb(f"stgo{i}", [128, 512], BF16) for i in range(2)]
            prep_mm = [4] * 4 + [4] * 16 + [8] * 4
            ends = []
            acc = 0
            for m in prep_mm:
                acc += m
                ends.append(acc)
            PREP = acc
            pe_sA = {}
            pe_sB = {}
            pe_tr = {}
            pe_out = {}
            p = PREP
            for gq in range(16):
                p += 4
                pe_sA[gq] = p
                p += 4
                pe_sB[gq] = p
                if gq >= 1:
                    p += 16
                    pe_tr[gq - 1] = p
                if gq >= 4 and gq % 4 == 0:
                    p += 16
                    pe_out[gq // 4 - 1] = p
            p += 16
            pe_tr[15] = p
            p += 16
            pe_out[3] = p
            PEH = p
            d_nma = {}
            d_nm16 = {}
            d_rs = {}
            d_ab = {}
            p = 40
            for gq in range(16):
                p += 1
                d_nma[gq] = p
                p += 3
                d_nm16[gq] = p
                if gq >= 1:
                    p += 2
                    d_rs[gq - 1] = p - 1
                    p += 1
                    d_ab[gq - 1] = p
            p += 3
            d_rs[15] = p - 2
            d_ab[15] = p
            DVEH = p
            a_ea = {}
            a_eb = {}
            a_ev = {}
            a_oe = {}
            p = 0
            for gq in range(16):
                p += 1
                a_ea[gq] = p
                p += 1
                a_eb[gq] = p
                if gq >= 1:
                    p += 1
                    a_ev[gq - 1] = p
                if gq >= 4 and gq % 4 == 0:
                    p += 1
                    a_oe[gq // 4 - 1] = p
            p += 1
            a_ev[15] = p
            p += 1
            a_oe[3] = p
            ACTH = p

            for h in range(HEADS):
                bar = bar_of(e)
                dve0, pe0, act0 = e.s_dve.n, e.s_pe.n, e.s_act.n
                lw0 = [e.LW[0].n, e.LW[1].n]

                @block.gpsimd
                def _(gp):
                    for l in range(16):
                        gp.wait_ge(e.s_dve.h, dve0 + (l - 1 if l >= 2 else 0))
                        if l < 4:
                            src = e.w_upk[l * 128:(l + 1) * 128,
                                          h * 128:(h + 1) * 128]
                        elif l < 8:
                            src = e.w_upv[(l - 4) * 128:(l - 3) * 128,
                                          h * 128:(h + 1) * 128]
                        else:
                            src = e.w_qu[(l - 8) * 128:(l - 7) * 128,
                                         h * 128:(h + 1) * 128]
                        gp.dma_start(wtm2[l % 2][:], src).then_inc(e.LW[l % 2].h, 16)
                    for qg in range(4):
                        oi = h * 4 + qg
                        gp.wait_ge(e.s_act.h, act0 + a_oe[qg])
                        gp.dma_start(e.outTd[b, h, :, qg * 512:(qg + 1) * 512],
                                     stg_o[oi % 2][:]).then_inc(e.STT[oi % 2].h, 16)
                e.LW[0].tick(16 * 8)
                e.LW[1].tick(16 * 8)
                e.STT[0].tick(16 * 2)
                e.STT[1].tick(16 * 2)

                @block.vector
                def _(vector):
                    vector = Chained(vector, e.s_dve)
                    vector.wait_ge(e.s_pe.h, pe0)
                    for l in range(16):
                        vector.wait_ge(e.LW[l % 2].h, lw0[l % 2] + 16 * (l // 2 + 1))
                        if l < 4:
                            vector.tensor_copy(upk[:, l, :], wtm2[l % 2][:])
                        elif l < 8:
                            vector.tensor_copy(upv[:, l - 4, :], wtm2[l % 2][:])
                        else:
                            vector.tensor_copy(quw[:, l - 8, :], wtm2[l % 2][:])
                    for g in range(24):
                        vector.wait_ge(e.s_pe.h, pe0 + ends[g])
                        if g < 4:
                            vector.tensor_copy(kcT[:, g * 512:(g + 1) * 512],
                                               psA[g % 2][:])
                        elif g < 20:
                            tt = g - 4
                            vector.tensor_copy(vtok[:, tt, :],
                                               psA[g % 2][:, 0:128])
                        else:
                            ts = g - 20
                            vector.tensor_copy(qcT[:, ts * 512:(ts + 1) * 512],
                                               psA[g % 2][:])

                    def lagged(gq):
                        pp_ = gq % 2
                        vector.wait_ge(e.s_act.h, act0 + a_eb[gq])
                        vector.reduce_sum(sums[:, pp_, 2:3], sums[:, pp_, 0:2],
                                          axis=AX.X)
                        vector.reciprocal(sums[:, pp_, 3:4], sums[:, pp_, 2:3])
                        if gq >= 1:
                            vector.wait_ge(e.s_pe.h, pe0 + pe_tr[gq - 1])
                        vector.tensor_scalar(attn_bf(e)[:], ef[pp_][:],
                                             sums[:, pp_, 3:4], None,
                                             op0=ALU.mult)

                    for gq in range(16):
                        qg, qt = divmod(gq, 4)
                        base_pe = pe0 + pe_sA[gq]
                        vector.wait_ge(e.s_pe.h, pe0 + pe_sA[gq])
                        vector.reduce_max(nmx[:, 0:1], psS[:, 0:1024], axis=AX.X,
                                          negate=True)
                        vector.wait_ge(e.s_pe.h, pe0 + pe_sB[gq])
                        vector.reduce_max(nmx[:, 1:2], psS[:, 1024:2048],
                                          axis=AX.X, negate=True)
                        vector.tensor_reduce(nmx[:, 2:3], nmx[:, 0:2],
                                             axis=AX.X, op=ALU.min)
                        if gq >= 1:
                            vector.wait_ge(e.s_act.h, act0 + a_eb[gq - 1])
                        vector.tensor_scalar(nmx[:, 3:4], nmx[:, 2:3],
                                             1.0 / 16.0, None, op0=ALU.mult)
                        if gq >= 1:
                            lagged(gq - 1)
                    lagged(15)

                @block.scalar
                def _(scalar):
                    scalar = Chained(scalar, e.s_act)

                    def evac(gq):
                        scalar.wait_ge(e.s_pe.h, pe0 + pe_tr[gq])
                        scalar.copy(attnT[:, :, (gq % 4) * 128:(gq % 4 + 1) * 128],
                                    psT[:].rearrange("p (c n) -> p c n", n=128))

                    def outev(qg):
                        oi = h * 4 + qg
                        scalar.wait_ge(e.s_pe.h, pe0 + pe_out[qg])
                        if oi >= 2:
                            scalar.wait_ge(e.STT[oi % 2].h,
                                           sto0[oi % 2] + 16 * (oi // 2))
                        scalar.copy(stg_o[oi % 2][:], psA[qg % 2][:])

                    for gq in range(16):
                        pp_ = gq % 2
                        if gq >= 2:
                            scalar.wait_ge(e.s_dve.h, dve0 + d_ab[gq - 2])
                        scalar.wait_ge(e.s_dve.h, dve0 + d_nm16[gq])
                        scalar.wait_ge(e.s_pe.h, pe0 + pe_sA[gq])
                        scalar.activation(ef[pp_][:, 0:1024], psS[:, 0:1024],
                                          AF.Exp, bias=nmx[:, 3:4],
                                          scale=1.0 / 16.0,
                                          accum_out=sums[:, pp_, 0:1])
                        scalar.wait_ge(e.s_pe.h, pe0 + pe_sB[gq])
                        scalar.activation(ef[pp_][:, 1024:2048], psS[:, 1024:2048],
                                          AF.Exp, bias=nmx[:, 3:4],
                                          scale=1.0 / 16.0,
                                          accum_out=sums[:, pp_, 1:2])
                        if gq >= 1:
                            evac(gq - 1)
                        if gq >= 4 and gq % 4 == 0:
                            outev(gq // 4 - 1)
                    evac(15)
                    outev(3)

                @block.tensor
                def _(tensor):
                    tensor.wait_ge(e.s_dve.h, dve0)
                    tensor.wait_ge(e.s_act.h, act0)
                    for g in range(24):
                        if g < 4:
                            tensor.wait_ge(e.s_dve.h, dve0 + 4)
                        elif g < 20:
                            tensor.wait_ge(e.s_dve.h, dve0 + 8)
                        else:
                            tensor.wait_ge(e.s_dve.h, dve0 + 16)
                        if g >= 2:
                            tensor.wait_ge(e.s_dve.h, dve0 + 16 + g - 1)
                        if g < 4:
                            ts = g
                            for kc in range(4):
                                tensor.matmul(psA[g % 2][:], upk[:, kc, :],
                                              kvlat[:, kc, ts * 512:(ts + 1) * 512],
                                              start=(kc == 0), stop=(kc == 3)
                                              ).then_inc(e.s_pe.h, 1)
                        elif g < 20:
                            tt = g - 4
                            for kc in range(4):
                                tensor.matmul(psA[g % 2][:, 0:128],
                                              kvlat[:, kc, tt * 128:(tt + 1) * 128],
                                              upv[:, kc, :],
                                              start=(kc == 0), stop=(kc == 3)
                                              ).then_inc(e.s_pe.h, 1)
                        else:
                            ts = g - 20
                            for qc in range(8):
                                tensor.matmul(psA[g % 2][:], quw[:, qc, :],
                                              qlat[:, qc, ts * 512:(ts + 1) * 512],
                                              start=(qc == 0), stop=(qc == 7)
                                              ).then_inc(e.s_pe.h, 1)
                    tensor.wait_ge(e.s_dve.h, dve0 + 40)

                    def transposes(gq):
                        tensor.wait_ge(e.s_dve.h, dve0 + d_ab[gq])
                        if gq >= 1:
                            tensor.wait_ge(e.s_act.h, act0 + a_ev[gq - 1])
                        for c in range(NDC):
                            tensor.transpose(psT[:, c * 128:(c + 1) * 128],
                                             attn_bf(e)[:, c * 128:(c + 1) * 128],
                                             e.idb[:]).then_inc(e.s_pe.h, 1)

                    def outmm(qg):
                        tensor.wait_ge(e.s_act.h, act0 + a_ev[4 * qg + 3])
                        if qg >= 2:
                            tensor.wait_ge(e.s_act.h, act0 + a_oe[qg - 2])
                        for c in range(NDC):
                            tensor.matmul(psA[qg % 2][:], vtok[:, c, :],
                                          attnT[:, c, :],
                                          start=(c == 0), stop=(c == NDC - 1)
                                          ).then_inc(e.s_pe.h, 1)

                    for gq in range(16):
                        qg, qt = divmod(gq, 4)
                        q0 = qg * 512 + qt * 128
                        for half2 in range(2):
                            if gq >= 1:
                                tensor.wait_ge(e.s_dve.h,
                                               dve0 + d_nma[gq - 1] + half2)
                                tensor.wait_ge(e.s_act.h,
                                               act0 + (a_ea if half2 == 0
                                                       else a_eb)[gq - 1])
                            for ks in (0, 1):
                                kk0 = half2 * 1024 + ks * 512
                                tensor.matmul(psS[:, kk0:kk0 + 512],
                                              qcT[:, q0:q0 + 128],
                                              kcT[:, kk0:kk0 + 512],
                                              start=True, stop=False
                                              ).then_inc(e.s_pe.h, 1)
                            for ks in (0, 1):
                                kk0 = half2 * 1024 + ks * 512
                                tensor.matmul(psS[:, kk0:kk0 + 512],
                                              qrT[:, q0:q0 + 128],
                                              krT[:, kk0:kk0 + 512],
                                              start=False, stop=True
                                              ).then_inc(e.s_pe.h, 1)
                        if gq >= 1:
                            transposes(gq - 1)
                        if gq >= 4 and gq % 4 == 0:
                            outmm(gq // 4 - 1)
                    transposes(15)
                    outmm(3)
                e.s_pe.tick(PEH)

        abctx.close()
        with ExitStack() as cctx:
            def csb(name, shape, dt=F32):
                return cctx.enter_context(nc.sbuf_tensor(f"at{b}c{name}", shape, dt))
            wtm3 = [csb(f"wt{i}", [128, D]) for i in range(2)]
            woutb = csb("wout", [128, NDC, D], BF16)
            outT = csb("outT", [128, HEADS, T], BF16)
            stg = [csb(f"stg{i}", [128, 512]) for i in range(2)]
            bar = bar_of(e)
            dve0, pe0 = e.s_dve.n, e.s_pe.n
            lw0 = [e.LW[0].n, e.LW[1].n]
            st0 = [e.STT[0].n, e.STT[1].n]
            lx0c = e.LX[0].n

            @block.sync
            def _(sync):
                for s_h, n in bar:
                    if n:
                        sync.wait_ge(s_h.h, n)
                for hh in range(HEADS):
                    sync.dma_start(outT[:, hh, :], e.outTd[b, hh, :, :]
                                   ).then_inc(e.LX[0].h, 16)
            e.LX[0].tick(16 * HEADS)

            @block.gpsimd
            def _(gp):
                for l in range(16):
                    gp.wait_ge(e.s_dve.h, dve0 + (l - 1 if l >= 2 else 0))
                    gp.dma_start(wtm3[l % 2][:], e.w_out[l * 128:(l + 1) * 128, :]
                                 ).then_inc(e.LW[l % 2].h, 16)
                for u in range(64):
                    ts, ddt = divmod(u, NDC)
                    gp.wait_ge(e.s_dve.h, dve0 + 16 + u + 1)
                    gp.dma_start(e.s0[b, ddt, :, ts * 512:(ts + 1) * 512],
                                 stg[u % 2][:]).then_inc(e.STT[u % 2].h, 16)
            e.LW[0].tick(16 * 8)
            e.LW[1].tick(16 * 8)
            e.STT[0].tick(16 * 32)
            e.STT[1].tick(16 * 32)

            @block.vector
            def _(vector):
                vector = Chained(vector, e.s_dve)
                for l in range(16):
                    vector.wait_ge(e.LW[l % 2].h, lw0[l % 2] + 16 * (l // 2 + 1))
                    vector.tensor_copy(woutb[:, l, :], wtm3[l % 2][:])
                for u in range(64):
                    vector.wait_ge(e.s_pe.h, pe0 + 16 * (u + 1))
                    if u >= 2:
                        vector.wait_ge(e.STT[u % 2].h, st0[u % 2] + 16 * (u // 2))
                    vector.tensor_copy(stg[u % 2][:], psA[u % 2][:])

            @block.tensor
            def _(tensor):
                tensor.wait_ge(e.LX[0].h, lx0c + 16 * HEADS)
                for u in range(64):
                    ts, ddt = divmod(u, NDC)
                    tensor.wait_ge(e.s_dve.h, dve0 + 16)
                    if u >= 2:
                        tensor.wait_ge(e.s_dve.h, dve0 + 16 + u - 1)
                    for c in range(NDC):
                        tensor.matmul(psA[u % 2][:],
                                      woutb[:, c, ddt * 128:(ddt + 1) * 128],
                                      outT[:, c, ts * 512:(ts + 1) * 512],
                                      start=(c == 0), stop=(c == NDC - 1)
                                      ).then_inc(e.s_pe.h, 1)
            e.s_pe.tick(16 * 64)


def attn_bf(e):
    return e._attn_bf


def emit_attn(e, block):
    e._attn_bf_guard = e.top_ctx.enter_context(
        e.nc.sbuf_tensor("attnbf", [128, T], BF16))
    e._attn_bf = e._attn_bf_guard
    for b in range(BPC):
        emit_attn_batch(e, block, b)



def emit_l1prep(e, block):
    nc = e.nc
    bar = bar_of(e)
    dve0, act0 = e.s_dve.n, e.s_act.n
    lx0 = e.LX[0].n

    @block.sync
    def _(sync):
        wait_bar(sync, bar)
        for i, v in enumerate(("bk", "bv", "br", "bo", "tmk", "tmv", "tmr",
                               "td", "tf")):
            sync.dma_start(e.vecs[:, i, :], getattr(e, v)[:]).then_inc(e.LX[0].h, 16)
    e.LX[0].tick(16 * 9)

    @block.scalar
    def _(scalar):
        scalar = Chained(scalar, e.s_act)
        wait_bar(scalar, bar)
        scalar.wait_ge(e.LX[0].h, lx0 + 16 * 9)
        scalar.activation(e.dvecs[:, 0, :], e.vecs[:, 7, :], AF.Exp)

    @block.vector
    def _(vector):
        vector = Chained(vector, e.s_dve)
        wait_bar(vector, bar)
        vector.wait_ge(e.LX[0].h, lx0 + 16 * 9)
        vector.wait_ge(e.s_act.h, act0 + 1)
        vector.tensor_scalar(e.dvecs[:, 0, :], e.dvecs[:, 0, :], -1.0, None,
                             op0=ALU.mult)
        vector.tensor_scalar(e.dvecs[:, 1, :], e.vecs[:, 8, :], -1.0, None,
                             op0=ALU.mult)
        for j in range(3):
            vector.tensor_scalar(e.dvecs[:, 2 + j, :], e.vecs[:, 4 + j, :],
                                 -1.0, 1.0, op0=ALU.mult, op1=ALU.add)
        vector.memset(e.dvecs[:, 5, :], 0.0)



def emit_kvr(e, block):
    nc = e.nc
    with ExitStack() as ctx:
        def sb(name, shape, dt=F32):
            return ctx.enter_context(nc.sbuf_tensor("kvr" + name, shape, dt))
        wtmp = [sb(f"wt{i}", [128, D]) for i in range(2)]
        wbf = sb("wbf", [128, D], BF16)
        WT = sb("WT", [128, NDC, NDC, 128], BF16)
        xp = sb("xp", [128, NDC, T + 1], BF16)
        xk = sb("xk", [128, NDC, 512], BF16)
        stg = [sb(f"stg{i}", [128, 512]) for i in range(2)]
        psq = ctx.enter_context(nc.psum_tensor("kvrpsq", [128, 2048], BF16))
        psm = [ctx.enter_context(nc.psum_tensor(f"kvrpsm{i}", [128, 512], F32))
               for i in range(2)]

        for wi, (wsrc, dstT, bcol, tmcol) in enumerate(
                [(e.Wk, e.kT, 0, 2), (e.Wv, e.vT, 1, 3), (e.Wr, e.rT, 2, 4)]):
            bar = bar_of(e)
            dve0, pe0 = e.s_dve.n, e.s_pe.n
            lw0 = [e.LW[0].n, e.LW[1].n]

            @block.gpsimd
            def _(gp):
                wait_bar(gp, bar)
                for l in range(NDC):
                    gp.wait_ge(e.s_dve.h, dve0 + (2 * (l - 2) + 2 if l >= 2 else 0))
                    gp.dma_start(wtmp[l % 2][:], wsrc[l * 128:(l + 1) * 128, :]
                                 ).then_inc(e.LW[l % 2].h, 16)
            e.LW[0].tick(16 * 8)
            e.LW[1].tick(16 * 8)

            @block.vector
            def _(vector):
                vector = Chained(vector, e.s_dve)
                wait_bar(vector, bar)
                for l in range(NDC):
                    vector.wait_ge(e.LW[l % 2].h, lw0[l % 2] + 16 * (l // 2 + 1))
                    vector.wait_ge(e.s_pe.h, pe0 + 16 * l)
                    vector.tensor_copy(wbf[:], wtmp[l % 2][:])
                    vector.wait_ge(e.s_pe.h, pe0 + 16 * (l + 1))
                    vector.tensor_copy(WT[:, :, l, :],
                                       psq[:].rearrange("p (c n) -> p c n", n=128))

            @block.tensor
            def _(tensor):
                wait_bar(tensor, bar)
                for l in range(NDC):
                    tensor.wait_ge(e.s_dve.h, dve0 + 2 * l + 1)
                    for c in range(NDC):
                        tensor.transpose(psq[:, c * 128:(c + 1) * 128],
                                         wbf[:, c * 128:(c + 1) * 128],
                                         e.idb[:]).then_inc(e.s_pe.h, 1)
            e.s_pe.tick(16 * NDC)

            for b in range(BPC):
                bar = bar_of(e)
                dve0, pe0 = e.s_dve.n, e.s_pe.n
                lx0 = e.LX[0].n
                st0 = [e.STT[0].n, e.STT[1].n]

                @block.sync
                def _(sync):
                    for s_h, n in bar:
                        if n:
                            sync.wait_ge(s_h.h, n)
                    for c in range(NDC):
                        sync.dma_start(xp[:, c, 1:T + 1], e.xn1T[b, c, :, :]
                                       ).then_inc(e.LX[0].h, 16)
                e.LX[0].tick(16 * NDC)

                LRP = 2 * NDC
                TSD = LRP + NDC

                @block.vector
                def _(vector):
                    vector = Chained(vector, e.s_dve)
                    for s_h, n in bar:
                        if n:
                            vector.wait_ge(s_h.h, n)
                    vector.wait_ge(e.LX[0].h, lx0 + 16 * NDC)
                    vector.memset(xp[:, :, 0:1], 0.0)
                    for ts in range(4):
                        if ts >= 1:
                            vector.wait_ge(e.s_pe.h, pe0 + 16 * NDC * ts)
                        for c in range(NDC):
                            vector.tensor_scalar(
                                xk[:, c, :], xp[:, c, ts * 512 + 1:ts * 512 + 513],
                                e.vecs[:, 4 + wi, c:c + 1], None, op0=ALU.mult)
                            vector.scalar_tensor_tensor(
                                xk[:, c, :], xp[:, c, ts * 512:ts * 512 + 512],
                                e.dvecs[:, 2 + wi, c:c + 1], xk[:, c, :],
                                op0=ALU.mult, op1=ALU.add)
                        for ot in range(NDC):
                            g = ts * NDC + ot
                            vector.wait_ge(e.s_pe.h, pe0 + 16 * (g + 1))
                            if g >= 2:
                                vector.wait_ge(e.STT[g % 2].h,
                                               st0[g % 2] + 16 * (g // 2))
                            vector.tensor_scalar(stg[g % 2][:], psm[g % 2][:],
                                                 e.vecs[:, bcol, ot:ot + 1], None,
                                                 op0=ALU.add)

                @block.tensor
                def _(tensor):
                    for ts in range(4):
                        tensor.wait_ge(e.s_dve.h, dve0 + 1 + TSD * ts + LRP)
                        for ot in range(NDC):
                            g = ts * NDC + ot
                            if g >= 2:
                                tensor.wait_ge(e.s_dve.h,
                                               dve0 + 1 + TSD * ((g - 2) // NDC)
                                               + LRP + ((g - 2) % NDC) + 1)
                            for c in range(NDC):
                                tensor.matmul(psm[g % 2][:],
                                              WT[:, c, ot, :], xk[:, c, :],
                                              start=(c == 0), stop=(c == NDC - 1)
                                              ).then_inc(e.s_pe.h, 1)
                e.s_pe.tick(16 * 64)

                @block.gpsimd
                def _(gp):
                    for ts in range(4):
                        for ot in range(NDC):
                            g = ts * NDC + ot
                            gp.wait_ge(e.s_dve.h, dve0 + 1 + TSD * ts + LRP + ot + 1)
                            gp.dma_start(dstT[b, ot, :, ts * 512:(ts + 1) * 512],
                                         stg[g % 2][:]).then_inc(e.STT[g % 2].h, 16)
                e.STT[0].tick(16 * 32)
                e.STT[1].tick(16 * 32)



def emit_wkv(e, block):
    nc = e.nc
    CH = 256
    NCHK = T // CH
    HA = NDC // 2
    with ExitStack() as ctx:
        def sb(name, shape, dt=F32):
            return ctx.enter_context(nc.sbuf_tensor("wkv" + name, shape, dt))
        kc = sb("kc", [128, NDC, CH])
        vc = sb("vc", [128, NDC, CH])
        rc = sb("rc", [128, NDC, CH])
        t1 = sb("t1", [128, NDC, CH])
        t2 = sb("t2", [128, NDC, CH])
        t3 = sb("t3", [128, NDC, CH])
        t4 = sb("t4", [128, NDC, CH])
        pp = sb("pp", [128, NDC, CH + 1])
        aa = sb("aa", [128, NDC, CH + 1])
        bb = sb("bb", [128, NDC, CH + 1])
        rw = [sb(f"rw{i}", [128, NDC, CH], BF16) for i in range(2)]
        decB = sb("decB", [128, NDC // 2, CH])
        nbB = sb("nbB", [128, NDC // 2, CH])
        zeroB = sb("zeroB", [128, NDC // 2, CH])

        DVE_CHUNK = 73
        GP_CHUNK = 13
        ACT_CHUNK = 5

        def half(buf, h):
            return buf[:, h * HA:(h + 1) * HA, :]

        for b in range(BPC):
            bar = bar_of(e)
            dve0, gp0, act0 = e.s_dve.n, e.s_gp.n, e.s_act.n
            lx0 = [e.LX[0].n, e.LX[1].n]
            st0 = [e.STT[0].n, e.STT[1].n]

            def dvec(ci, k):  return dve0 + 9 + DVE_CHUNK * ci + k
            def gpc(ci, k):   return gp0 + GP_CHUNK * ci + k
            def actc(ci, k):  return act0 + ACT_CHUNK * ci + k

            @block.sync
            def _(sync):
                for s_h, n in bar:
                    if n:
                        sync.wait_ge(s_h.h, n)
                for ci in range(NCHK):
                    if ci >= 1:
                        sync.wait_ge(e.s_dve.h, dvec(ci - 1, DVE_CHUNK))
                        sync.wait_ge(e.s_gp.h, gpc(ci - 1, GP_CHUNK))
                        sync.wait_ge(e.s_act.h, actc(ci - 1, ACT_CHUNK))
                    q = e.LX[ci % 2].h
                    for c in range(NDC):
                        sync.dma_start(kc[:, c, :],
                                       e.kT[b, c, :, ci * CH:(ci + 1) * CH]
                                       ).then_inc(q, 16)
                    for c in range(NDC):
                        sync.dma_start(vc[:, c, :],
                                       e.vT[b, c, :, ci * CH:(ci + 1) * CH]
                                       ).then_inc(q, 16)
                    for c in range(NDC):
                        sync.dma_start(rc[:, c, :],
                                       e.rT[b, c, :, ci * CH:(ci + 1) * CH]
                                       ).then_inc(q, 16)
            e.LX[0].tick(16 * 3 * NDC * (NCHK // 2))
            e.LX[1].tick(16 * 3 * NDC * (NCHK // 2))

            def emit_half_init(eng, h):
                eng.memset(pp[:, h * HA:(h + 1) * HA, 0:1], -1e30)
                eng.memset(aa[:, h * HA:(h + 1) * HA, 0:1], 0.0)
                eng.memset(bb[:, h * HA:(h + 1) * HA, 0:1], 0.0)

            def emit_dve_chunk(vector, ci):
                h = 0
                ppP = pp[:, 0:HA, 0:CH]
                ppC = pp[:, 0:HA, 1:CH + 1]
                aaP = aa[:, 0:HA, 0:CH]
                bbP = bb[:, 0:HA, 0:CH]
                kch, vch, rch = half(kc, 0), half(vc, 0), half(rc, 0)
                t1h, t2h, t3h, t4h = half(t1, 0), half(t2, 0), half(t3, 0), half(t4, 0)
                k = 0
                vector.wait_ge(e.LX[ci % 2].h,
                               lx0[ci % 2] + 16 * 3 * NDC * (ci // 2 + 1))
                for c in range(NDC):
                    vector.tensor_tensor_scan(
                        pp[:, c, 1:CH + 1],
                        e.dvecs[:, 0, c:c + 1].broadcast_to([128, CH]),
                        kc[:, c, :], pp[:, c, 0:1],
                        op0=ALU.add, op1=ALU.max)
                    k += 1
                vector.tensor_tensor(t1h, ppP, ppC, op=ALU.subtract); k += 1
                vector.tensor_tensor(
                    t1h, t1h, e.dvecs[:, 0, 0:HA].unsqueeze(2)
                    .broadcast_to([128, HA, CH]), op=ALU.add); k += 1
                vector.wait_ge(e.s_act.h, actc(ci, 1))
                vector.tensor_tensor(t1h, kch, ppC, op=ALU.subtract); k += 1
                vector.wait_ge(e.s_act.h, actc(ci, 2))
                vector.tensor_tensor(t4h, t3h, vch, op=ALU.mult); k += 1
                for c in range(HA):
                    vector.tensor_tensor_scan(
                        aa[:, c, 1:CH + 1], t2[:, c, :], t4[:, c, :],
                        aa[:, c, 0:1], op0=ALU.mult, op1=ALU.add)
                    k += 1
                vector.wait_ge(e.s_gp.h, gpc(ci, 4))
                for c in range(HA, NDC):
                    vector.tensor_tensor_scan(
                        aa[:, c, 1:CH + 1], t2[:, c, :], t4[:, c, :],
                        aa[:, c, 0:1], op0=ALU.mult, op1=ALU.add)
                    k += 1
                for c in range(NDC):
                    vector.tensor_tensor_scan(
                        bb[:, c, 1:CH + 1], t2[:, c, :], t3[:, c, :],
                        bb[:, c, 0:1], op0=ALU.mult, op1=ALU.add)
                    k += 1
                vector.tensor_tensor(t1h, ppP, kch, op=ALU.subtract); k += 1
                vector.tensor_tensor(
                    t1h, t1h, e.dvecs[:, 1, 0:HA].unsqueeze(2)
                    .broadcast_to([128, HA, CH]), op=ALU.add); k += 1
                vector.tensor_scalar(t4h, t1h, 0.0, None, op0=ALU.min); k += 1
                vector.tensor_scalar(t1h, t1h, 0.0, None, op0=ALU.max); k += 1
                vector.wait_ge(e.s_gp.h, gpc(ci, 6))
                vector.tensor_scalar(half(t4, 1), half(t1, 1), 0.0, None,
                                     op0=ALU.min); k += 1
                vector.tensor_scalar(half(t1, 1), half(t1, 1), 0.0, None,
                                     op0=ALU.max); k += 1
                vector.wait_ge(e.s_act.h, actc(ci, 4))
                vector.tensor_tensor(t3h, t4h, aaP, op=ALU.mult); k += 1
                vector.tensor_tensor(vch, t1h, vch, op=ALU.mult); k += 1
                vector.tensor_tensor(t3h, t3h, vch, op=ALU.add); k += 1
                vector.tensor_tensor(t4h, t4h, bbP, op=ALU.mult); k += 1
                vector.tensor_tensor(t4h, t4h, t1h, op=ALU.add); k += 1
                vector.reciprocal(t4h, t4h); k += 1
                vector.wait_ge(e.s_gp.h, gpc(ci, 11))
                vector.reciprocal(half(t4, 1), half(t4, 1)); k += 1
                vector.tensor_tensor(t3h, t3h, t4h, op=ALU.mult); k += 1
                vector.wait_ge(e.s_act.h, actc(ci, 5))
                if ci >= 2:
                    oi = b * NCHK + ci
                    vector.wait_ge(e.STT[oi % 2].h,
                                   st0[oi % 2] + 16 * NDC * (ci // 2))
                vector.tensor_tensor(half(rw[ci % 2], 0), t3h, rch,
                                     op=ALU.mult); k += 1
                vector.tensor_copy(pp[:, 0:HA, 0:1], pp[:, 0:HA, CH:CH + 1]); k += 1
                vector.tensor_copy(aa[:, 0:HA, 0:1], aa[:, 0:HA, CH:CH + 1]); k += 1
                vector.tensor_copy(bb[:, 0:HA, 0:1], bb[:, 0:HA, CH:CH + 1]); k += 1
                vector.wait_ge(e.s_gp.h, gpc(ci, GP_CHUNK))
                vector.tensor_copy(pp[:, HA:NDC, 0:1], pp[:, HA:NDC, CH:CH + 1]); k += 1
                vector.tensor_copy(aa[:, HA:NDC, 0:1], aa[:, HA:NDC, CH:CH + 1]); k += 1
                vector.tensor_copy(bb[:, HA:NDC, 0:1], bb[:, HA:NDC, CH:CH + 1]); k += 1
                assert k == DVE_CHUNK, (k, DVE_CHUNK)

            def emit_gp_chunk(gpx, ci):
                ppP = pp[:, HA:NDC, 0:CH]
                ppC = pp[:, HA:NDC, 1:CH + 1]
                aaP = aa[:, HA:NDC, 0:CH]
                bbP = bb[:, HA:NDC, 0:CH]
                kch, vch, rch = half(kc, 1), half(vc, 1), half(rc, 1)
                t1h, t2h, t3h, t4h = half(t1, 1), half(t2, 1), half(t3, 1), half(t4, 1)
                zb = zeroB[:]
                k = 0
                gpx.wait_ge(e.LX[ci % 2].h,
                            lx0[ci % 2] + 16 * 3 * NDC * (ci // 2 + 1))
                gpx.wait_ge(e.s_dve.h, dvec(ci, 16))
                gpx.tensor_tensor(t1h, ppP, ppC, op=ALU.subtract); k += 1
                gpx.tensor_tensor(t1h, t1h, decB[:], op=ALU.add); k += 1
                gpx.wait_ge(e.s_act.h, actc(ci, 1))
                gpx.tensor_tensor(t1h, kch, ppC, op=ALU.subtract); k += 1
                gpx.wait_ge(e.s_act.h, actc(ci, 2))
                gpx.tensor_tensor(t4h, t3h, vch, op=ALU.mult); k += 1
                gpx.wait_ge(e.s_dve.h, dvec(ci, 36))
                gpx.tensor_tensor(t1h, ppP, kch, op=ALU.subtract); k += 1
                gpx.tensor_tensor(t1h, t1h, nbB[:], op=ALU.add); k += 1
                gpx.wait_ge(e.s_act.h, actc(ci, 4))
                gpx.wait_ge(e.s_dve.h, dvec(ci, 58))
                gpx.tensor_tensor(t3h, t4h, aaP, op=ALU.mult); k += 1
                gpx.tensor_tensor(vch, t1h, vch, op=ALU.mult); k += 1
                gpx.tensor_tensor(t3h, t3h, vch, op=ALU.add); k += 1
                gpx.tensor_tensor(t4h, t4h, bbP, op=ALU.mult); k += 1
                gpx.tensor_tensor(t4h, t4h, t1h, op=ALU.add); k += 1
                gpx.wait_ge(e.s_dve.h, dvec(ci, 65))
                gpx.tensor_tensor(t3h, t3h, t4h, op=ALU.mult); k += 1
                gpx.wait_ge(e.s_act.h, actc(ci, 5))
                if ci >= 2:
                    oi = b * NCHK + ci
                    gpx.wait_ge(e.STT[oi % 2].h,
                                st0[oi % 2] + 16 * NDC * (ci // 2))
                gpx.tensor_tensor(half(rw[ci % 2], 1), t3h, rch,
                                  op=ALU.mult); k += 1
                assert k == GP_CHUNK, (k, GP_CHUNK)

            @block.vector
            def _(vector):
                vector = Chained(vector, e.s_dve)
                for s_h, n in bar:
                    if n:
                        vector.wait_ge(s_h.h, n)
                emit_half_init(vector, 0)
                emit_half_init(vector, 1)
                vector.tensor_copy(decB[:], e.dvecs[:, 0, HA:NDC].unsqueeze(2)
                                   .broadcast_to([128, HA, CH]))
                vector.tensor_copy(nbB[:], e.dvecs[:, 1, HA:NDC].unsqueeze(2)
                                   .broadcast_to([128, HA, CH]))
                vector.memset(zeroB[:], 0.0)
                for ci in range(NCHK):
                    emit_dve_chunk(vector, ci)

            @block.gpsimd
            def _(gp):
                gpx = Chained(gp, e.s_gp)
                for s_h, n in bar:
                    if n:
                        gp.wait_ge(s_h.h, n)
                for ci in range(NCHK):
                    emit_gp_chunk(gpx, ci)
                    oi = b * NCHK + ci
                    gp.wait_ge(e.s_gp.h, gpc(ci, GP_CHUNK))
                    gp.wait_ge(e.s_dve.h, dvec(ci, DVE_CHUNK - 6))
                    for c in range(NDC):
                        gp.dma_start(e.rwkvT[b, c, :, ci * CH:(ci + 1) * CH],
                                     rw[ci % 2][:, c, :]).then_inc(e.STT[oi % 2].h, 16)

            e.STT[0].tick(16 * NDC * (NCHK // 2))
            e.STT[1].tick(16 * NDC * (NCHK // 2))

            @block.scalar
            def _(scalar):
                scalar = Chained(scalar, e.s_act)
                for s_h, n in bar:
                    if n:
                        scalar.wait_ge(s_h.h, n)
                for ci in range(NCHK):
                    scalar.wait_ge(e.s_dve.h, dvec(ci, 18))
                    scalar.wait_ge(e.s_gp.h, gpc(ci, 2))
                    scalar.activation(t2[:], t1[:], AF.Exp)
                    scalar.wait_ge(e.s_dve.h, dvec(ci, 19))
                    scalar.wait_ge(e.s_gp.h, gpc(ci, 3))
                    scalar.activation(t3[:], t1[:], AF.Exp)
                    scalar.wait_ge(e.s_dve.h, dvec(ci, 58))
                    scalar.wait_ge(e.s_gp.h, gpc(ci, 6))
                    scalar.activation(t4[:], t4[:], AF.Exp)
                    scalar.activation(t1[:], t1[:], AF.Exp, scale=-1.0)
                    scalar.wait_ge(e.LX[ci % 2].h,
                                   lx0[ci % 2] + 16 * 3 * NDC * (ci // 2 + 1))
                    scalar.activation(rc[:], rc[:], AF.Sigmoid)



def emit_wo(e, block):
    nc = e.nc
    with ExitStack() as ctx:
        def sb(name, shape, dt=F32):
            return ctx.enter_context(nc.sbuf_tensor("wo" + name, shape, dt))
        wtmp = [sb(f"wt{i}", [128, D]) for i in range(2)]
        wbf = sb("wbf", [128, D], BF16)
        WT = sb("WT", [128, NDC, NDC, 128], BF16)
        rws = sb("rws", [128, NDC, T], BF16)
        stg = [sb(f"stg{i}", [128, 512]) for i in range(2)]
        psq = ctx.enter_context(nc.psum_tensor("wopsq", [128, 2048], BF16))
        psm = [ctx.enter_context(nc.psum_tensor(f"wopsm{i}", [128, 512], F32))
               for i in range(2)]

        bar = bar_of(e)
        dve0, pe0 = e.s_dve.n, e.s_pe.n
        lw0 = [e.LW[0].n, e.LW[1].n]

        @block.gpsimd
        def _(gp):
            wait_bar(gp, bar)
            for l in range(NDC):
                gp.wait_ge(e.s_dve.h, dve0 + (2 * (l - 2) + 2 if l >= 2 else 0))
                gp.dma_start(wtmp[l % 2][:], e.Wo[l * 128:(l + 1) * 128, :]
                             ).then_inc(e.LW[l % 2].h, 16)
        e.LW[0].tick(16 * 8)
        e.LW[1].tick(16 * 8)

        @block.vector
        def _(vector):
            vector = Chained(vector, e.s_dve)
            wait_bar(vector, bar)
            for l in range(NDC):
                vector.wait_ge(e.LW[l % 2].h, lw0[l % 2] + 16 * (l // 2 + 1))
                vector.wait_ge(e.s_pe.h, pe0 + 16 * l)
                vector.tensor_copy(wbf[:], wtmp[l % 2][:])
                vector.wait_ge(e.s_pe.h, pe0 + 16 * (l + 1))
                vector.tensor_copy(WT[:, :, l, :],
                                   psq[:].rearrange("p (c n) -> p c n", n=128))

        @block.tensor
        def _(tensor):
            wait_bar(tensor, bar)
            for l in range(NDC):
                tensor.wait_ge(e.s_dve.h, dve0 + 2 * l + 1)
                for c in range(NDC):
                    tensor.transpose(psq[:, c * 128:(c + 1) * 128],
                                     wbf[:, c * 128:(c + 1) * 128],
                                     e.idb[:]).then_inc(e.s_pe.h, 1)
        e.s_pe.tick(16 * NDC)

        for b in range(BPC):
            bar = bar_of(e)
            dve0, pe0 = e.s_dve.n, e.s_pe.n
            lx0 = e.LX[0].n
            st0 = [e.STT[0].n, e.STT[1].n]

            @block.sync
            def _(sync):
                for s_h, n in bar:
                    if n:
                        sync.wait_ge(s_h.h, n)
                for c in range(NDC):
                    sync.dma_start(rws[:, c, :], e.rwkvT[b, c, :, :]
                                   ).then_inc(e.LX[0].h, 16)
            e.LX[0].tick(16 * NDC)

            @block.vector
            def _(vector):
                vector = Chained(vector, e.s_dve)
                for s_h, n in bar:
                    if n:
                        vector.wait_ge(s_h.h, n)
                for g in range(64):
                    ts, ot = divmod(g, NDC)
                    vector.wait_ge(e.s_pe.h, pe0 + 16 * (g + 1))
                    if g >= 2:
                        vector.wait_ge(e.STT[g % 2].h, st0[g % 2] + 16 * (g // 2))
                    vector.tensor_scalar(stg[g % 2][:], psm[g % 2][:],
                                         e.vecs[:, 3, ot:ot + 1], None,
                                         op0=ALU.add)

            @block.tensor
            def _(tensor):
                tensor.wait_ge(e.LX[0].h, lx0 + 16 * NDC)
                for g in range(64):
                    ts, ot = divmod(g, NDC)
                    if g >= 2:
                        tensor.wait_ge(e.s_dve.h, dve0 + g - 1)
                    for c in range(NDC):
                        tensor.matmul(psm[g % 2][:], WT[:, c, ot, :],
                                      rws[:, c, ts * 512:(ts + 1) * 512],
                                      start=(c == 0), stop=(c == NDC - 1)
                                      ).then_inc(e.s_pe.h, 1)
            e.s_pe.tick(16 * 64)

            @block.gpsimd
            def _(gp):
                for g in range(64):
                    ts, ot = divmod(g, NDC)
                    gp.wait_ge(e.s_dve.h, dve0 + g + 1)
                    gp.dma_start(e.s1[b, ot, :, ts * 512:(ts + 1) * 512],
                                 stg[g % 2][:]).then_inc(e.STT[g % 2].h, 16)
            e.STT[0].tick(16 * 32)
            e.STT[1].tick(16 * 32)



def emit_dq(e, block):
    nc = e.nc
    F1 = 0x3F000000
    F2 = 0x3F800000
    NTI = 2 * I1 // 128 + D // 128

    def tinfo(i):
        if i < 88:
            return ("gu", 256, 16)
        return ("dn", 704, 44)

    def rounds(nblk):
        out = []
        o = 0
        while o < nblk:
            out.append((o, min(16, nblk - o)))
            o += 16
        return out

    dve0, pe0 = e.s_dve.n, e.s_pe.n
    lw0 = [e.LW[0].n, e.LW[1].n]
    lx0 = e.LX[0].n
    st0 = [e.STT[0].n, e.STT[1].n]
    dve_n = dve0
    pe_n = pe0
    st_n = [st0[0], st0[1]]
    sched = []
    for i in range(NTI):
        kind, W, nblk = tinfo(i)
        s = {}
        if i == 0:
            dve_n += 2
        dve_n += 88
        s["plane_end"] = dve_n
        pe_n += 1
        s["sc_tr"] = pe_n
        dve_n += 1
        s["scb_end"] = dve_n
        dve_n += nblk
        s["mults_end"] = dve_n
        s["evac_end"] = []
        s["pe_round"] = []
        s["st_round"] = []
        for (o, nb) in rounds(nblk):
            pe_n += nb
            s["pe_round"].append(pe_n)
            dve_n += 1
            s["evac_end"].append(dve_n)
            st_n[i % 2] += 16 * nb
            s["st_round"].append(st_n[i % 2])
        sched.append(s)

    bar = bar_of(e)
    with ExitStack() as ctx:
        def sb(name, shape, dt=F32):
            return ctx.enter_context(nc.sbuf_tensor("dq" + name, shape, dt))
        gusf = sb("gusf", [64, 2 * I1])
        dnsf = sb("dnsf", [64, D])
        gush = sb("gush", [64, 2 * I1], F16)
        dnsh = sb("dnsh", [64, D], F16)
        pk = [sb(f"pk{i}", [128, 704], I32) for i in range(2)]
        nib = sb("nib", [128, 704], I32)
        b3s = sb("b3s", [128, 704], I32)
        rr = sb("rr", [128, 704], I32)
        bits = sb("bits", [128, 704], I32)
        msk = sb("msk", [128, 704], I32)
        s1i = sb("s1i", [128, 704], I32)
        alt = sb("alt", [128, 704])
        val = sb("val", [128, I1])
        scb = sb("scb", [128, 44])
        vbf = sb("vbf", [128, I1], BF16)
        stg = sb("stg", [128, NDC, 128], BF16)
        psq = ctx.enter_context(nc.psum_tensor("dqpsq", [128, 2048], BF16))
        pss = ctx.enter_context(nc.psum_tensor("dqpss", [128, 64], F32))

        @block.sync
        def _(sync):
            wait_bar(sync, bar)
            sync.dma_start(gush[0:16, :], e.gus[:]).then_inc(e.LX[0].h, 16)
            sync.dma_start(dnsh[0:44, :], e.dns[:]).then_inc(e.LX[0].h, 16)
        e.LX[0].tick(16 * 2)

        @block.gpsimd
        def _(gp):
            wait_bar(gp, bar)
            for i in range(NTI):
                kind, W, nblk = tinfo(i)
                if i >= 2:
                    gp.wait_ge(e.s_dve.h, sched[i - 2]["plane_end"])
                if kind == "gu":
                    gp.dma_start(pk[i % 2][:, 0:256],
                                 e.gup[i * 128:(i + 1) * 128, :]
                                 ).then_inc(e.LW[i % 2].h, 16)
                else:
                    j = i - 88
                    gp.dma_start(pk[i % 2][:],
                                 e.dnp[j * 128:(j + 1) * 128, :]
                                 ).then_inc(e.LW[i % 2].h, 16)
                for r, (o, nb) in enumerate(rounds(nblk)):
                    gp.wait_ge(e.s_dve.h, sched[i]["evac_end"][r])
                    for c2 in range(nb):
                        blk = o + c2
                        if kind == "gu":
                            dst = e.wguT[i, :, blk * 128:(blk + 1) * 128]
                        else:
                            dst = e.wdnT[blk, :, (i - 88) * 128:(i - 87) * 128]
                        gp.dma_start(dst, stg[:, c2, :]).then_inc(e.STT[i % 2].h, 16)
        e.STT[0].tick(st_n[0] - st0[0])
        e.STT[1].tick(st_n[1] - st0[1])
        e.LW[0].tick(16 * 44)
        e.LW[1].tick(16 * 44)

        @block.vector
        def _(vector):
            vector = Chained(vector, e.s_dve)
            wait_bar(vector, bar)
            for i in range(NTI):
                kind, W, nblk = tinfo(i)
                if i == 0:
                    vector.wait_ge(e.LX[0].h, lx0 + 32)
                    vector.tensor_copy(gusf[0:16, :], gush[0:16, :])
                    vector.tensor_copy(dnsf[0:44, :], dnsh[0:44, :])
                vector.wait_ge(e.LW[i % 2].h, lw0[i % 2] + 16 * (i // 2 + 1))
                p = pk[i % 2]
                for j in range(8):
                    vector.tensor_scalar(nib[:, :W], p[:, :W], 4 * j, 15,
                                         op0=ALU.logical_shift_right,
                                         op1=ALU.bitwise_and)
                    vector.tensor_scalar(b3s[:, :W], nib[:, :W], 8, 28,
                                         op0=ALU.bitwise_and,
                                         op1=ALU.logical_shift_left)
                    vector.tensor_scalar(rr[:, :W], nib[:, :W], 7, None,
                                         op0=ALU.bitwise_and)
                    vector.tensor_scalar(bits[:, :W], rr[:, :W], 22, None,
                                         op0=ALU.logical_shift_left)
                    vector.tensor_scalar(bits[:, :W], bits[:, :W], F1, None,
                                         op0=ALU.add)
                    vector.tensor_tensor(bits[:, :W], bits[:, :W], b3s[:, :W],
                                         op=ALU.bitwise_or)
                    vector.tensor_scalar(msk[:, :W], rr[:, :W], 2, None,
                                         op0=ALU.is_lt)
                    vector.tensor_scalar(s1i[:, :W], b3s[:, :W], F2, None,
                                         op0=ALU.bitwise_or)
                    vector.tensor_scalar(alt[:, :W], bits[:, :W].bitcast(F32),
                                         2.0, None, op0=ALU.mult)
                    vector.tensor_tensor(alt[:, :W], alt[:, :W],
                                         s1i[:, :W].bitcast(F32), op=ALU.subtract)
                    vector.select(val[:, j:8 * W:8], msk[:, :W], alt[:, :W],
                                  bits[:, :W].bitcast(F32), add_drain=True)
                vector.wait_ge(e.s_pe.h, sched[i]["sc_tr"])
                vector.tensor_copy(scb[:, 0:nblk], pss[:, 0:nblk])
                for g2 in range(nblk):
                    vector.tensor_scalar(vbf[:, g2 * 128:(g2 + 1) * 128],
                                         val[:, g2 * 128:(g2 + 1) * 128],
                                         scb[:, g2:g2 + 1], None, op0=ALU.mult)
                for r, (o, nb) in enumerate(rounds(nblk)):
                    vector.wait_ge(e.s_pe.h, sched[i]["pe_round"][r])
                    if r > 0:
                        vector.wait_ge(e.STT[i % 2].h, sched[i]["st_round"][r - 1])
                    elif i >= 1:
                        vector.wait_ge(e.STT[(i - 1) % 2].h,
                                       sched[i - 1]["st_round"][-1])
                    vector.tensor_copy(stg[:, 0:nb, :],
                                       psq[:, :nb * 128]
                                       .rearrange("p (c n) -> p c n", n=128))

        @block.tensor
        def _(tensor):
            wait_bar(tensor, bar)
            for i in range(NTI):
                kind, W, nblk = tinfo(i)
                tensor.wait_ge(e.s_dve.h,
                               sched[i - 1]["scb_end"] if i else dve0 + 2)
                if kind == "gu":
                    tensor.transpose(pss[:, 0:16],
                                     gusf[0:16, i * 128:(i + 1) * 128],
                                     e.idf[0:16, 0:16]).then_inc(e.s_pe.h, 1)
                else:
                    j = i - 88
                    tensor.transpose(pss[:, 0:44],
                                     dnsf[0:44, j * 128:(j + 1) * 128],
                                     e.idf[0:44, 0:44]).then_inc(e.s_pe.h, 1)
                tensor.wait_ge(e.s_dve.h, sched[i]["mults_end"])
                for r, (o, nb) in enumerate(rounds(nblk)):
                    if r > 0:
                        tensor.wait_ge(e.s_dve.h, sched[i]["evac_end"][r - 1])
                    elif i >= 1:
                        tensor.wait_ge(e.s_dve.h, sched[i - 1]["evac_end"][-1])
                    for c2 in range(nb):
                        tensor.transpose(psq[:, c2 * 128:(c2 + 1) * 128],
                                         vbf[:, (o + c2) * 128:(o + c2 + 1) * 128],
                                         e.idb[:]).then_inc(e.s_pe.h, 1)
        e.s_pe.tick(pe_n - pe0)



def emit_gu(e, block):
    nc = e.nc
    NP = 44
    NU = BPC * 4
    bar = bar_of(e)
    dve0, pe0, act0 = e.s_dve.n, e.s_pe.n, e.s_act.n
    lx0 = [e.LX[0].n, e.LX[1].n]
    st0 = [e.STT[0].n, e.STT[1].n]
    PEP = 32 * NU

    def pe_iu(i, u):   return pe0 + PEP * i + 32 * (u + 1)
    def act_iu(i, u):  return act0 + NU * i + u + 1
    def dve_iu(i, u):  return dve0 + NU * i + u + 1
    def st_iu(i, u):
        gi = i * NU + u
        return st0[gi % 2] + 16 * (gi // 2 + 1)

    def lx_pair(i):
        if i % 2 == 0:
            return e.LX[0], lx0[0] + 16 * BPC * NDC + 16 * 2 * (i // 2 + 1)
        return e.LX[1], lx0[1] + 16 * 2 * (i // 2 + 1)

    with ExitStack() as ctx:
        def sb(name, shape, dt=F32):
            return ctx.enter_context(nc.sbuf_tensor("gu" + name, shape, dt))
        xn1 = sb("xn1", [128, BPC * NDC, T], BF16)
        wg = [sb(f"wg{i}", [128, D], BF16) for i in range(2)]
        wu = [sb(f"wu{i}", [128, D], BF16) for i in range(2)]
        sg = [sb(f"sg{i}", [128, 512]) for i in range(2)]
        h2s = [sb(f"h2s{i}", [128, 512], BF16) for i in range(2)]
        psG = [ctx.enter_context(nc.psum_tensor(f"gupsG{i}", [128, 512], F32))
               for i in range(2)]
        psU = [ctx.enter_context(nc.psum_tensor(f"gupsU{i}", [128, 512], F32))
               for i in range(2)]

        @block.sync
        def _(sync):
            wait_bar(sync, bar)
            for b in range(BPC):
                for c in range(NDC):
                    sync.dma_start(xn1[:, b * NDC + c, :], e.xn1T[b, c, :, :]
                                   ).then_inc(e.LX[0].h, 16)
            for i in range(NP):
                if i >= 2:
                    sync.wait_ge(e.s_pe.h, pe_iu(i - 2, NU - 1))
                q = e.LX[i % 2].h
                sync.dma_start(wg[i % 2][:], e.wguT[i, :, :]).then_inc(q, 16)
                sync.dma_start(wu[i % 2][:], e.wguT[i + 44, :, :]).then_inc(q, 16)
        e.LX[0].tick(16 * BPC * NDC + 16 * 2 * ((NP + 1) // 2))
        e.LX[1].tick(16 * 2 * (NP // 2))

        @block.tensor
        def _(tensor):
            wait_bar(tensor, bar)
            for i in range(NP):
                s_, v_ = lx_pair(i)
                tensor.wait_ge(s_.h, v_)
                for u in range(NU):
                    b, ts = divmod(u, 4)
                    if i * NU + u >= 2:
                        pi, pu = divmod(i * NU + u - 2, NU)
                        tensor.wait_ge(e.s_act.h, act_iu(pi, pu))
                        tensor.wait_ge(e.s_dve.h, dve_iu(pi, pu))
                    for c in range(NDC):
                        tensor.matmul(psG[u % 2][:],
                                      wg[i % 2][:, c * 128:(c + 1) * 128],
                                      xn1[:, b * NDC + c, ts * 512:(ts + 1) * 512],
                                      start=(c == 0), stop=(c == NDC - 1)
                                      ).then_inc(e.s_pe.h, 1)
                    for c in range(NDC):
                        tensor.matmul(psU[u % 2][:],
                                      wu[i % 2][:, c * 128:(c + 1) * 128],
                                      xn1[:, b * NDC + c, ts * 512:(ts + 1) * 512],
                                      start=(c == 0), stop=(c == NDC - 1)
                                      ).then_inc(e.s_pe.h, 1)
        e.s_pe.tick(PEP * NP)

        @block.scalar
        def _(scalar):
            scalar = Chained(scalar, e.s_act)
            wait_bar(scalar, bar)
            for i in range(NP):
                for u in range(NU):
                    scalar.wait_ge(e.s_pe.h, pe0 + PEP * i + 32 * u + 16)
                    if i * NU + u >= 2:
                        pi, pu = divmod(i * NU + u - 2, NU)
                        scalar.wait_ge(e.s_dve.h, dve_iu(pi, pu))
                    scalar.activation(sg[u % 2][:], psG[u % 2][:], AF.Silu)

        @block.vector
        def _(vector):
            vector = Chained(vector, e.s_dve)
            wait_bar(vector, bar)
            for i in range(NP):
                for u in range(NU):
                    gi = i * NU + u
                    vector.wait_ge(e.s_act.h, act_iu(i, u))
                    vector.wait_ge(e.s_pe.h, pe_iu(i, u))
                    if gi >= 2:
                        vector.wait_ge(e.STT[gi % 2].h, st_iu(i, u) - 16)
                    vector.tensor_tensor(h2s[u % 2][:], sg[u % 2][:],
                                         psU[u % 2][:], op=ALU.mult)

        @block.gpsimd
        def _(gp):
            wait_bar(gp, bar)
            for i in range(NP):
                for u in range(NU):
                    b, ts = divmod(u, 4)
                    gi = i * NU + u
                    gp.wait_ge(e.s_dve.h, dve_iu(i, u))
                    gp.dma_start(e.h2[i, :, b * T + ts * 512:b * T + (ts + 1) * 512],
                                 h2s[u % 2][:]).then_inc(e.STT[gi % 2].h, 16)
        e.STT[0].tick(16 * (NP * NU // 2))
        e.STT[1].tick(16 * (NP * NU // 2))



def emit_down(e, block):
    nc = e.nc
    NH2 = 44
    with ExitStack() as ctx:
        def sb(name, shape, dt=F32):
            return ctx.enter_context(nc.sbuf_tensor("dn" + name, shape, dt))
        h2c = sb("h2c", [128, NH2, 512], BF16)
        wq = sb("wq", [128, NH2, 512], BF16)
        ybuf = sb("ybuf", [128, 512])
        ytok = [sb(f"ytok{i}", [128, D]) for i in range(4)]
        junk = sb("junk", [128, D], BF16)
        red = sb("red", [128, 4])
        g1 = sb("g1", [128, D])
        s1c = sb("s1c", [128, NDC, 128])
        s1tok = sb("s1tok", [128, D])
        x1t = sb("x1t", [128, D])
        outt = [sb(f"outt{i}", [128, D]) for i in range(2)]
        psm = [ctx.enter_context(nc.psum_tensor(f"dnpsm{i}", [128, 512], F32))
               for i in range(2)]
        psf = ctx.enter_context(nc.psum_tensor("dnpsf", [128, 512], F32))
        psf2 = ctx.enter_context(nc.psum_tensor("dnpsf2", [128, 2048], F32))

        for b in range(BPC):
            for tsb in range(4):
                gidx = b * 4 + tsb
                bar = bar_of(e)
                dve0, pe0, act0 = e.s_dve.n, e.s_pe.n, e.s_act.n
                lx0 = [e.LX[0].n, e.LX[1].n]
                st0 = [e.STT[0].n, e.STT[1].n]

                def fdve(tt, k):  return dve0 + 80 + 5 * tt + k
                def fact(tt, k):  return act0 + 3 * tt + k
                def fst(tt):
                    return st0[tt % 2] + 16 * (tt // 2 + 1)

                @block.sync
                def _(sync):
                    for s_h, n in bar:
                        if n:
                            sync.wait_ge(s_h.h, n)
                    for c in range(NH2):
                        sync.dma_start(
                            h2c[:, c, :],
                            e.h2[c, :, b * T + tsb * 512:b * T + (tsb + 1) * 512]
                        ).then_inc(e.LX[0].h, 16)
                    for q in range(4):
                        if q > 0:
                            sync.wait_ge(e.s_pe.h, pe0 + 192 * q)
                        for c in range(NH2):
                            sync.dma_start(wq[:, c, :],
                                           e.wdnT[c, :, q * 512:(q + 1) * 512]
                                           ).then_inc(e.LX[1].h, 16)
                    for tt in range(4):
                        t0 = tsb * 512 + tt * 128
                        if tt >= 1:
                            sync.wait_ge(e.s_pe.h, pe0 + 768 + 16 * tt)
                            sync.wait_ge(e.s_dve.h, fdve(tt - 1, 5))
                        sync.dma_start(s1c[:, :, :],
                                       e.s1[b, :, :, t0:t0 + 128]
                                       .transpose([1, 0, 2])).then_inc(e.LX[0].h, 16)
                        sync.dma_start(x1t[:], e.x1[b, t0:t0 + 128, :]
                                       ).then_inc(e.LX[0].h, 16)
                e.LX[0].tick(16 * (NH2 + 8))
                e.LX[1].tick(16 * 4 * NH2)

                @block.tensor
                def _(tensor):
                    for s_h, n in bar:
                        if n:
                            tensor.wait_ge(s_h.h, n)
                    tensor.wait_ge(e.LX[0].h, lx0[0] + 16 * NH2)
                    for q in range(4):
                        tensor.wait_ge(e.LX[1].h, lx0[1] + 16 * NH2 * (q + 1))
                        for dq_ in range(4):
                            g = q * 4 + dq_
                            if g >= 2:
                                tensor.wait_ge(e.s_dve.h, dve0 + 5 * (g - 2) + 1)
                            for c in range(NH2):
                                tensor.matmul(psm[g % 2][:],
                                              wq[:, c, dq_ * 128:(dq_ + 1) * 128],
                                              h2c[:, c, :],
                                              start=(c == 0), stop=(c == NH2 - 1)
                                              ).then_inc(e.s_pe.h, 1)
                            tensor.wait_ge(e.s_dve.h, dve0 + 5 * g + 1)
                            if g >= 1:
                                tensor.wait_ge(e.s_dve.h, dve0 + 5 * g)
                            for tt in range(4):
                                tensor.transpose(psf[:, tt * 128:(tt + 1) * 128],
                                                 ybuf[:, tt * 128:(tt + 1) * 128],
                                                 e.idf[:]).then_inc(e.s_pe.h, 1)
                    for tt in range(4):
                        tensor.wait_ge(e.LX[0].h,
                                       lx0[0] + 16 * (NH2 + 2 * (tt + 1)))
                        if tt >= 1:
                            tensor.wait_ge(e.s_dve.h, fdve(tt - 1, 3))
                        for c in range(NDC):
                            tensor.transpose(psf2[:, c * 128:(c + 1) * 128],
                                             s1c[:, c, :], e.idf[:]
                                             ).then_inc(e.s_pe.h, 1)
                e.s_pe.tick(192 * 4 + 64)

                @block.scalar
                def _(scalar):
                    scalar = Chained(scalar, e.s_act)
                    for s_h, n in bar:
                        if n:
                            scalar.wait_ge(s_h.h, n)
                    for tt in range(4):
                        scalar.wait_ge(e.s_dve.h, dve0 + 80)
                        if tt >= 1:
                            scalar.wait_ge(e.s_dve.h, fdve(tt - 1, 5))
                        scalar.activation(junk[:], ytok[tt][:], AF.Square,
                                          accum_out=red[:, 0:1])
                        scalar.wait_ge(e.s_dve.h, fdve(tt, 1))
                        scalar.activation(red[:, 2:3], red[:, 1:2], AF.Sqrt)
                        scalar.wait_ge(e.s_dve.h, fdve(tt, 2))
                        scalar.activation(g1[:], ytok[tt][:], AF.Silu,
                                          scale=red[:, 3:4])

                @block.vector
                def _(vector):
                    vector = Chained(vector, e.s_dve)
                    for s_h, n in bar:
                        if n:
                            vector.wait_ge(s_h.h, n)
                    for q in range(4):
                        for dq_ in range(4):
                            g = q * 4 + dq_
                            vector.wait_ge(e.s_pe.h, pe0 + 48 * g + 44)
                            vector.tensor_copy(ybuf[:], psm[g % 2][:])
                            vector.wait_ge(e.s_pe.h, pe0 + 48 * (g + 1))
                            for tt in range(4):
                                dc = (q * 4 + dq_) * 128
                                vector.tensor_copy(ytok[tt][:, dc:dc + 128],
                                                   psf[:, tt * 128:(tt + 1) * 128])
                    for tt in range(4):
                        vector.wait_ge(e.s_act.h, fact(tt, 1))
                        vector.tensor_scalar(red[:, 1:2], red[:, 0:1], 1.0 / D,
                                             1e-6, op0=ALU.mult, op1=ALU.add)
                        vector.wait_ge(e.s_act.h, fact(tt, 2))
                        vector.reciprocal(red[:, 3:4], red[:, 2:3])
                        vector.wait_ge(e.s_pe.h, pe0 + 768 + 16 * (tt + 1))
                        vector.tensor_copy(s1tok[:], psf2[:])
                        vector.wait_ge(e.s_act.h, fact(tt, 3))
                        if tt >= 2:
                            vector.wait_ge(e.STT[tt % 2].h, fst(tt - 2))
                        vector.tensor_tensor(outt[tt % 2][:], s1tok[:], g1[:],
                                             op=ALU.mult)
                        vector.tensor_tensor(outt[tt % 2][:], outt[tt % 2][:],
                                             x1t[:], op=ALU.add)

                @block.gpsimd
                def _(gp):
                    for tt in range(4):
                        t0 = tsb * 512 + tt * 128
                        fi = gidx * 4 + tt
                        gp.wait_ge(e.s_dve.h, fdve(tt, 5))
                        gp.dma_start(e.out[b, t0:t0 + 128, :], outt[tt % 2][:]
                                     ).then_inc(e.STT[tt % 2].h, 16)
                e.STT[0].tick(16 * 2)
                e.STT[1].tick(16 * 2)


def build_program(phases=ALL_PHASES, debug_out=()):
    nc = bass.Bass("TRN2", target_bir_lowering=False)
    e = Env()
    e.nc = nc

    def din(name, shape, dt=F32):
        return nc.dram_tensor(name, list(shape), dt, kind="ExternalInput").ap()

    e.x = din("x", [BPC, T, D])
    e.w_kvd = din("w_kvd", [D, KVC])
    e.w_upk = din("w_upk", [KVC, D])
    e.w_upv = din("w_upv", [KVC, D])
    e.w_qd = din("w_qd", [D, QC])
    e.w_qu = din("w_qu", [QC, D])
    e.w_kr = din("w_kr", [D, HD])
    e.w_qr = din("w_qr", [D, HD])
    e.w_out = din("w_out", [D, D])
    e.w_fc = din("w_fc", [FFH, D])
    e.w_proj = din("w_proj", [D, FFH])
    e.Wk = din("Wk", [D, D])
    e.Wv = din("Wv", [D, D])
    e.Wr = din("Wr", [D, D])
    e.Wo = din("Wo", [D, D])
    for v in ("bk", "bv", "br", "bo", "tmk", "tmv", "tmr", "td", "tf"):
        setattr(e, v, din(v, [128, NDC]))
    e.gup = din("gup", [2 * I1, D // 8], I32)
    e.gus = din("gus", [D // 128, 2 * I1], F16)
    e.dnp = din("dnp", [D, I1 // 8], I32)
    e.dns = din("dns", [I1 // 128, D], F16)
    e.idb_d = din("idb", [128, 128], BF16)
    e.idf_d = din("idf", [128, 128], F32)
    e.cos2_d = din("cos2", [128, T])
    e.sin2_d = din("sin2", [128, T])
    e.onescol_d = din("onescol", [128, 1])
    e.onesrow_d = din("onesrow", [1, 128])

    e.out = nc.dram_tensor("out", [BPC, T, D], F32, kind="ExternalOutput").ap()

    def scratch(name, shape, dt=F32):
        if name in debug_out:
            return nc.dram_tensor(name, list(shape), dt, kind="ExternalOutput").ap()
        return nc.dram_tensor(name, list(shape), dt, kind="Internal").ap()

    e.xnT = scratch("xnT", [BPC, NDC, 128, T], BF16)
    e.hmid = scratch("hmid", [FFH // 128, 128, BPC * T], BF16)
    e.g0 = scratch("g0", [BPC, NDC, 128, T])
    e.s0 = scratch("s0", [BPC, NDC, 128, T])
    e.outTd = scratch("outTd", [BPC, HEADS, 128, T], BF16)
    e.x1 = scratch("x1", [BPC, T, D])
    e.xn1T = scratch("xn1T", [BPC, NDC, 128, T], BF16)
    e.kT = scratch("kT", [BPC, NDC, 128, T])
    e.vT = scratch("vT", [BPC, NDC, 128, T])
    e.rT = scratch("rT", [BPC, NDC, 128, T])
    e.rwkvT = scratch("rwkvT", [BPC, NDC, 128, T], BF16)
    e.s1 = scratch("s1", [BPC, NDC, 128, T])
    e.wguT = scratch("wguT", [2 * I1 // 128, 128, D], BF16)
    e.wdnT = scratch("wdnT", [I1 // 128, 128, D], BF16)
    e.h2 = scratch("h2", [I1 // 128, 128, BPC * T], BF16)
    e.tern = scratch("tern", [1, 8])

    with ExitStack() as ctx:
        e.top_ctx = ctx
        e.idb = ctx.enter_context(nc.sbuf_tensor("c_idb", [128, 128], BF16))
        e.idf = ctx.enter_context(nc.sbuf_tensor("c_idf", [128, 128], F32))
        e.onescol = ctx.enter_context(nc.sbuf_tensor("c_onescol", [128, 1], F32))
        e.onesrow = ctx.enter_context(nc.sbuf_tensor("c_onesrow", [1, 128], F32))
        for nm in ("thrfc", "nthrfc", "thrpj", "nthrpj", "amtot"):
            setattr(e, nm, ctx.enter_context(nc.sbuf_tensor("c_" + nm, [128, 1], F32)))
        e.vecs = ctx.enter_context(nc.sbuf_tensor("c_vecs", [128, 9, NDC], F32))
        e.dvecs = ctx.enter_context(nc.sbuf_tensor("c_dvecs", [128, 6, NDC], F32))
        for nm in ("s_pe", "s_dve", "s_act", "s_gp"):
            setattr(e, nm, S(ctx, nc, nm))
        e.LX = [S(ctx, nc, f"lx{i}") for i in range(2)]
        e.LW = [S(ctx, nc, f"lw{i}") for i in range(2)]
        e.STT = [S(ctx, nc, f"st{i}") for i in range(2)]
        e.STX = [S(ctx, nc, f"sx{i}") for i in range(2)]

        block = ctx.enter_context(nc.Block())

        @block.sync
        def _(sync):
            for dst, src in [(e.idb, e.idb_d), (e.idf, e.idf_d),
                             (e.onescol, e.onescol_d), (e.onesrow, e.onesrow_d)]:
                sync.dma_start(dst[:], src[:]).then_inc(e.LX[0].h, 16)
        e.LX[0].tick(16 * 4)

        if "ln0" in phases:
            emit_ln(e, block, e.x, e.xnT, tag="ln0")
        if "tstat" in phases:
            emit_tstat(e, block)
        if "ffn0h" in phases:
            emit_ffn0h(e, block)
        if "ffn0p" in phases:
            emit_ffn0p(e, block)
        if "attn" in phases:
            emit_attn(e, block)
        if "comb0" in phases:
            emit_ln(e, block, e.x, e.xn1T, dst_x_ap=e.x1,
                    s0g0=(e.s0, e.g0), tag="cm0")
        if "kvr" in phases or "wkv" in phases or "wo" in phases:
            emit_l1prep(e, block)
        if "kvr" in phases:
            emit_kvr(e, block)
        if "wkv" in phases:
            emit_wkv(e, block)
        if "wo" in phases:
            emit_wo(e, block)
        if "dq" in phases:
            emit_dq(e, block)
        if "gu" in phases:
            emit_gu(e, block)
        if "down" in phases:
            emit_down(e, block)

        @block.sync
        def _(sync):
            for s in e.STT + e.STX + e.LW + e.LX:
                if s.n:
                    sync.wait_ge(s.h, s.n)
    return nc



def make_consts():
    idb = np.eye(128).astype(ml_dtypes.bfloat16)
    idf = np.eye(128).astype(np.float32)
    j = np.arange(64)
    inv = (10000.0 ** (-(2.0 * j) / 128.0)).astype(np.float64)
    t = np.arange(T, dtype=np.float64)
    ang = np.outer(inv, t)
    cos2 = np.repeat(np.cos(ang), 2, axis=0).astype(np.float32)
    sin = np.sin(ang)
    sin2 = np.empty((128, T), np.float64)
    sin2[0::2] = -sin
    sin2[1::2] = sin
    sin2 = sin2.astype(np.float32)
    onescol = np.ones((128, 1), np.float32)
    onesrow = np.ones((1, 128), np.float32)
    return dict(idb=idb, idf=idf, cos2=cos2, sin2=sin2,
                onescol=onescol, onesrow=onesrow)


def vec_2d(v):
    return np.ascontiguousarray(np.asarray(v).reshape(NDC, 128).T.astype(np.float32))


def make_in_maps(inputs):
    consts = make_consts()
    common = dict(
        w_kvd=inputs["L0_attn_w_kv_down"], w_upk=inputs["L0_attn_w_kv_up_k"],
        w_upv=inputs["L0_attn_w_kv_up_v"], w_qd=inputs["L0_attn_w_q_down"],
        w_qu=inputs["L0_attn_w_q_up"], w_kr=inputs["L0_attn_w_k_rope"],
        w_qr=inputs["L0_attn_w_q_rope"], w_out=inputs["L0_attn_w_out"],
        w_fc=inputs["L0_ffn_w_fc"], w_proj=inputs["L0_ffn_w_proj"],
        Wk=inputs["L1_attn_Wk"], Wv=inputs["L1_attn_Wv"],
        Wr=inputs["L1_attn_Wr"], Wo=inputs["L1_attn_Wo"],
        bk=vec_2d(inputs["L1_attn_bk"]), bv=vec_2d(inputs["L1_attn_bv"]),
        br=vec_2d(inputs["L1_attn_br"]), bo=vec_2d(inputs["L1_attn_bo"]),
        tmk=vec_2d(inputs["L1_attn_tmk"]), tmv=vec_2d(inputs["L1_attn_tmv"]),
        tmr=vec_2d(inputs["L1_attn_tmr"]), td=vec_2d(inputs["L1_attn_time_decay"]),
        tf=vec_2d(inputs["L1_attn_time_first"]),
        gup=inputs["L1_ffn_gate_up_packed"], gus=inputs["L1_ffn_gate_up_scales"],
        dnp=inputs["L1_ffn_down_packed"], dns=inputs["L1_ffn_down_scales"],
        **consts,
    )
    common = {k: np.ascontiguousarray(v) for k, v in common.items()}
    x = np.asarray(inputs["L0_attn_x"])
    in_maps = []
    for c in range(NCORES):
        m = dict(common)
        m["x"] = np.ascontiguousarray(x[c * BPC:(c + 1) * BPC])
        in_maps.append(m)
    return in_maps


def kernel(**inputs):
    in_maps = make_in_maps(inputs)
    nc = build_program()
    res = run_bass_kernel_spmd(nc, in_maps, core_ids=list(range(NCORES)))
    return np.concatenate([np.asarray(r["out"]) for r in res.results], axis=0)


# version 09
# speedup vs baseline: 1.0961x; 1.0401x over previous
import numpy as np
import ml_dtypes
import concourse.bass as bass
import concourse.mybir as mybir
from concourse.bass_utils import run_bass_kernel_spmd
from contextlib import ExitStack

F32 = mybir.dt.float32
BF16 = mybir.dt.bfloat16
F16 = mybir.dt.float16
I32 = mybir.dt.int32
AF = mybir.ActivationFunctionType
ALU = mybir.AluOpType
AX = mybir.AxisListType

B, T, D = 16, 2048, 2048
NCORES = 8
BPC = B // NCORES
HEADS, HD = 16, 128
KVC, QC = 512, 1024
FFH = 5120
I1 = 5632
NT = T // 128
NDC = D // 128

ALL_PHASES = ("ln0", "tstat", "ffn0h", "ffn0p", "attn", "comb0",
              "kvr", "wkv", "wo", "dq", "gu", "down")


class S:
    def __init__(self, ctx, nc, name):
        self.h = ctx.enter_context(nc.semaphore(name))
        self.n = 0

    def tick(self, k=1):
        self.n += k
        return self.n


class Env:
    pass


class Chained:

    def __init__(self, eng, s):
        self._eng = eng
        self._s = s

    def wait_ge(self, h, n):
        self._eng.wait_ge(h, n)

    def __getattr__(self, name):
        f = getattr(self._eng, name)

        def wrapped(*a, **kw):
            self._eng.wait_ge(self._s.h, self._s.n)
            instr = f(*a, **kw)
            instr.then_inc(self._s.h, 1)
            self._s.tick()
            return instr
        return wrapped


def bar_of(e):
    return [(s, s.n) for s in (e.s_pe, e.s_dve, e.s_act, e.s_gp)
            + tuple(e.LX) + tuple(e.LW) + tuple(e.STT) + tuple(e.STX)]


def wait_bar(eng, bar):
    for s, n in bar:
        if n:
            eng.wait_ge(s.h, n)



def emit_ln(e, block, src_x_ap, dst_T_ap, dst_x_ap=None, s0g0=None, tag="ln"):
    nc = e.nc
    comb = s0g0 is not None
    ntiles = BPC * NT
    LD = 3 if comb else 1
    DVE = 41 if comb else 23
    PE = 32 if comb else 16
    ACT = 2
    STD = 17 if comb else 16
    dve0, pe0, act0 = e.s_dve.n, e.s_pe.n, e.s_act.n
    lx0 = [e.LX[0].n, e.LX[1].n]
    st0 = [e.STT[0].n, e.STT[1].n]

    def ld(i):      return lx0[i % 2] + 16 * LD * (i // 2 + 1)
    def dve(i, k):  return dve0 + DVE * i + k
    def pe(i, k):   return pe0 + PE * i + k
    def act(i, k):  return act0 + ACT * i + k
    def st(i):      return st0[i % 2] + 16 * STD * (i // 2 + 1)

    bar = bar_of(e)

    with ExitStack() as ctx:
        def sb(name, shape, dt=F32):
            return ctx.enter_context(nc.sbuf_tensor(tag + name, shape, dt))
        xt = [sb(f"x{i}", [128, D]) for i in range(2)]
        xn = [sb(f"xn{i}", [128, D], BF16) for i in range(2)]
        stage = [sb(f"stg{i}", [128, NDC, 128], BF16) for i in range(2)]
        junk = sb("junk", [128, D], BF16)
        red = [sb(f"red{i}", [128, 8]) for i in range(2)]
        if comb:
            s_ = [sb(f"s{i}", [128, NDC, 128]) for i in range(2)]
            g_ = [sb(f"g{i}", [128, NDC, 128]) for i in range(2)]
            sgT = [sb(f"sgT{i}", [128, NDC, 128]) for i in range(2)]
            sgtok = [sb(f"sgtok{i}", [128, D]) for i in range(2)]
            psf = ctx.enter_context(nc.psum_tensor(tag + "psf", [128, 2048], F32))
        pst = [ctx.enter_context(nc.psum_tensor(tag + f"pst{i}", [128, 2048], BF16))
               for i in range(2)]

        @block.sync
        def _(sync):
            wait_bar(sync, bar)
            for i in range(ntiles):
                b, tt = divmod(i, NT)
                if i >= 2:
                    sync.wait_ge(e.s_dve.h, dve(i - 2, DVE))
                    sync.wait_ge(e.s_act.h, act(i - 2, ACT))
                    if comb:
                        sync.wait_ge(e.STT[i % 2].h, st(i - 2))
                q = e.LX[i % 2].h
                sync.dma_start(xt[i % 2][:],
                               src_x_ap[b, tt * 128:(tt + 1) * 128, :]).then_inc(q, 16)
                if comb:
                    sync.dma_start(s_[i % 2][:],
                                   s0g0[0][b, :, :, tt * 128:(tt + 1) * 128]
                                   .transpose([1, 0, 2])).then_inc(q, 16)
                    sync.dma_start(g_[i % 2][:],
                                   s0g0[1][b, :, :, tt * 128:(tt + 1) * 128]
                                   .transpose([1, 0, 2])).then_inc(q, 16)
        e.LX[0].tick(16 * LD * ((ntiles + 1) // 2))
        e.LX[1].tick(16 * LD * (ntiles // 2))

        @block.tensor
        def _(tensor):
            wait_bar(tensor, bar)
            for i in range(ntiles):
                p = i % 2
                if comb:
                    tensor.wait_ge(e.s_dve.h, dve(i, 1))
                    if i >= 1:
                        tensor.wait_ge(e.s_dve.h, dve(i - 1, 17))
                    for c in range(NDC):
                        tensor.transpose(psf[:, c * 128:(c + 1) * 128],
                                         sgT[p][:, c, :], e.idf[:]
                                         ).then_inc(e.s_pe.h, 1)
                tensor.wait_ge(e.s_dve.h, dve(i, DVE - 16))
                if i >= 2:
                    tensor.wait_ge(e.s_dve.h, dve(i - 2, DVE))
                for c in range(NDC):
                    tensor.transpose(pst[p][:, c * 128:(c + 1) * 128],
                                     xn[p][:, c * 128:(c + 1) * 128],
                                     e.idb[:]).then_inc(e.s_pe.h, 1)
        e.s_pe.tick(PE * ntiles)

        @block.scalar
        def _(scalar):
            scalar = Chained(scalar, e.s_act)
            wait_bar(scalar, bar)
            for i in range(ntiles):
                p = i % 2
                scalar.wait_ge(e.LX[p].h, ld(i))
                if i >= 2:
                    scalar.wait_ge(e.s_dve.h, dve(i - 2, DVE - 16))
                if comb:
                    scalar.wait_ge(e.s_dve.h, dve(i, 18))
                scalar.activation(junk[:], xt[p][:], AF.Square,
                                  accum_out=red[p][:, 1:2])
                scalar.wait_ge(e.s_dve.h, dve(i, DVE - 18))
                scalar.activation(red[p][:, 5:6], red[p][:, 3:4], AF.Sqrt
                                  )

        @block.vector
        def _(vector):
            vector = Chained(vector, e.s_dve)
            wait_bar(vector, bar)
            for i in range(ntiles):
                p = i % 2
                k = 0
                vector.wait_ge(e.LX[p].h, ld(i))
                if comb:
                    vector.tensor_tensor(sgT[p][:], s_[p][:], g_[p][:],
                                         op=ALU.mult)
                    k += 1
                    vector.wait_ge(e.s_pe.h, pe(i, 16))
                    for c in range(NDC):
                        vector.tensor_copy(sgtok[p][:, c * 128:(c + 1) * 128],
                                           psf[:, c * 128:(c + 1) * 128]
                                           )
                        k += 1
                    vector.wait_ge(e.s_dve.h, dve(i, k))
                    vector.tensor_tensor(xt[p][:], xt[p][:], sgtok[p][:],
                                         op=ALU.add)
                    k += 1
                    vector.wait_ge(e.s_dve.h, dve(i, k))
                vector.reduce_sum(red[p][:, 0:1], xt[p][:], axis=AX.X
                                  )
                k += 1
                vector.wait_ge(e.s_dve.h, dve(i, k))
                vector.tensor_scalar(red[p][:, 2:3], red[p][:, 0:1], -1.0 / D, None,
                                     op0=ALU.mult)
                k += 1
                vector.wait_ge(e.s_act.h, act(i, 1))
                vector.tensor_scalar(red[p][:, 3:4], red[p][:, 1:2], 1.0 / D, 1e-5,
                                     op0=ALU.mult, op1=ALU.add)
                k += 1
                vector.wait_ge(e.s_dve.h, dve(i, k))
                vector.tensor_tensor(red[p][:, 4:5], red[p][:, 2:3], red[p][:, 2:3],
                                     op=ALU.mult)
                k += 1
                vector.wait_ge(e.s_dve.h, dve(i, k))
                vector.tensor_tensor(red[p][:, 3:4], red[p][:, 3:4], red[p][:, 4:5],
                                     op=ALU.subtract)
                k += 1
                vector.wait_ge(e.s_act.h, act(i, 2))
                vector.reciprocal(red[p][:, 6:7], red[p][:, 5:6])
                k += 1
                vector.wait_ge(e.s_dve.h, dve(i, k))
                vector.tensor_scalar(xn[p][:], xt[p][:], red[p][:, 2:3], red[p][:, 6:7],
                                     op0=ALU.add, op1=ALU.mult)
                k += 1
                vector.wait_ge(e.s_pe.h, pe(i, PE))
                if i >= 2:
                    vector.wait_ge(e.STT[p].h, st(i - 2))
                for c in range(NDC):
                    vector.tensor_copy(stage[p][:, c, :],
                                       pst[p][:, c * 128:(c + 1) * 128]
                                       )
                    k += 1
                assert k == DVE, (k, DVE)

        @block.gpsimd
        def _(gpsimd):
            wait_bar(gpsimd, bar)
            for i in range(ntiles):
                b, tt = divmod(i, NT)
                p = i % 2
                gpsimd.wait_ge(e.s_dve.h, dve(i, DVE))
                q = e.STT[p].h
                if comb:
                    gpsimd.dma_start(dst_x_ap[b, tt * 128:(tt + 1) * 128, :],
                                     xt[p][:]).then_inc(q, 16)
                for c in range(NDC):
                    gpsimd.dma_start(dst_T_ap[b, c, :, tt * 128:(tt + 1) * 128],
                                     stage[p][:, c, :]).then_inc(q, 16)
        e.STT[0].tick(16 * STD * ((ntiles + 1) // 2))
        e.STT[1].tick(16 * STD * (ntiles // 2))



def emit_tstat(e, block):
    nc = e.nc
    bar = bar_of(e)
    dve0, pe0, act0 = e.s_dve.n, e.s_pe.n, e.s_act.n
    lw0 = [e.LW[0].n, e.LW[1].n]
    NTI = 40 + 16

    def lw(i):  return lw0[i % 2] + 16 * (i // 2 + 1)

    with ExitStack() as ctx:
        def sb(name, shape, dt=F32):
            return ctx.enter_context(nc.sbuf_tensor("ts" + name, shape, dt))
        wt = [sb(f"w{i}", [128, FFH]) for i in range(2)]
        junk = sb("junk", [128, FFH], BF16)
        acc = sb("acc", [128, 56])
        tot = sb("tot", [128, 2])
        sc = sb("sc", [1, 8])
        ps1 = ctx.enter_context(nc.psum_tensor("tsps1", [1, 8], F32))
        ps2 = ctx.enter_context(nc.psum_tensor("tsps2", [128, 8], F32))

        @block.gpsimd
        def _(gp):
            wait_bar(gp, bar)
            for i in range(NTI):
                if i >= 2:
                    gp.wait_ge(e.s_act.h, act0 + i - 1)
                if i < 40:
                    src = e.w_fc[i * 128:(i + 1) * 128, :]
                else:
                    src = e.w_proj[(i - 40) * 128:(i - 39) * 128, :]
                gp.dma_start(wt[i % 2][:, :src.shape[1]], src
                             ).then_inc(e.LW[i % 2].h, 16)
        e.LW[0].tick(16 * ((NTI + 1) // 2))
        e.LW[1].tick(16 * (NTI // 2))

        @block.scalar
        def _(scalar):
            scalar = Chained(scalar, e.s_act)
            wait_bar(scalar, bar)
            for i in range(NTI):
                w = 2048 if i < 40 else FFH
                scalar.wait_ge(e.LW[i % 2].h, lw(i))
                scalar.activation(junk[:, :w], wt[i % 2][:, :w], AF.Abs,
                                  accum_out=acc[:, i:i + 1])

        @block.vector
        def _(vector):
            vector = Chained(vector, e.s_dve)
            wait_bar(vector, bar)
            vector.wait_ge(e.s_act.h, act0 + NTI)
            vector.reduce_sum(tot[:, 0:1], acc[:, 0:40], axis=AX.X)
            vector.reduce_sum(tot[:, 1:2], acc[:, 40:56], axis=AX.X)
            vector.wait_ge(e.s_pe.h, pe0 + 1)
            vector.tensor_scalar(sc[0:1, 0:1], ps1[0:1, 0:1], 1.0 / (FFH * D), 1e-5,
                                 op0=ALU.mult, op1=ALU.max)
            vector.tensor_scalar(sc[0:1, 2:3], ps1[0:1, 1:2], 1.0 / (FFH * D), 1e-5,
                                 op0=ALU.mult, op1=ALU.max)
            vector.wait_ge(e.s_dve.h, dve0 + 4)
            vector.tensor_scalar(sc[0:1, 1:2], sc[0:1, 0:1], 0.7, None,
                                 op0=ALU.mult)
            vector.tensor_scalar(sc[0:1, 3:4], sc[0:1, 2:3], 0.7, None,
                                 op0=ALU.mult)
            vector.tensor_tensor(sc[0:1, 5:6], sc[0:1, 0:1], sc[0:1, 0:1],
                                 op=ALU.mult)
            vector.wait_ge(e.s_dve.h, dve0 + 7)
            vector.tensor_tensor(sc[0:1, 4:5], sc[0:1, 5:6], sc[0:1, 2:3],
                                 op=ALU.mult)
            vector.wait_ge(e.s_pe.h, pe0 + 2)
            vector.tensor_copy(e.thrfc[:], ps2[:, 1:2])
            vector.tensor_copy(e.thrpj[:], ps2[:, 3:4])
            vector.tensor_copy(e.amtot[:], ps2[:, 4:5])
            vector.wait_ge(e.s_dve.h, dve0 + 11)
            vector.tensor_scalar(e.nthrfc[:], e.thrfc[:], -1.0, None,
                                 op0=ALU.mult)
            vector.tensor_scalar(e.nthrpj[:], e.thrpj[:], -1.0, None,
                                 op0=ALU.mult)

        @block.tensor
        def _(tensor):
            wait_bar(tensor, bar)
            tensor.wait_ge(e.s_dve.h, dve0 + 2)
            tensor.matmul(ps1[0:1, 0:2], e.onescol[:], tot[:, 0:2],
                          start=True, stop=True).then_inc(e.s_pe.h, 1)
            tensor.wait_ge(e.s_dve.h, dve0 + 8)
            tensor.matmul(ps2[:, 0:6], e.onesrow[:], sc[0:1, 0:6],
                          start=True, stop=True).then_inc(e.s_pe.h, 1)
        e.s_pe.tick(2)

        @block.gpsimd
        def _(gp):
            gp.wait_ge(e.s_dve.h, dve0 + 13)
            gp.dma_start(e.tern[0:1, 0:6], sc[0:1, 0:6]).then_inc(e.STT[0].h, 16)
        e.STT[0].tick(16)



def emit_ffn0h(e, block):
    nc = e.nc
    bar = bar_of(e)
    NO = FFH // 128
    NSUB = BPC * 4
    DVE = 4 + NSUB
    PE = NDC + NSUB * NDC
    ACT = NSUB
    dve0, pe0, act0 = e.s_dve.n, e.s_pe.n, e.s_act.n
    lx0 = e.LX[0].n
    lw0 = [e.LW[0].n, e.LW[1].n]
    st0 = [e.STX[0].n, e.STX[1].n]

    def lw(i):      return lw0[i % 2] + 16 * (i // 2 + 1)
    def dve(i, k):  return dve0 + DVE * i + k
    def pe(i, k):   return pe0 + PE * i + k
    def act(i, k):  return act0 + ACT * i + k
    def st(gi):     return st0[gi % 2] + 16 * (gi // 2 + 1)

    with ExitStack() as ctx:
        def sb(name, shape, dt=F32):
            return ctx.enter_context(nc.sbuf_tensor("f0h" + name, shape, dt))
        xnT_sb = sb("xnT", [128, BPC * NDC, T], BF16)
        wt = [sb(f"w{i}", [128, D]) for i in range(2)]
        m1 = sb("m1", [128, D])
        m2 = sb("m2", [128, D])
        q = sb("q", [128, D], BF16)
        fcT = sb("fcT", [128, NDC, 128], BF16)
        relu = [sb(f"relu{i}", [128, 512]) for i in range(2)]
        ht = [sb(f"ht{i}", [128, 512], BF16) for i in range(2)]
        psq = ctx.enter_context(nc.psum_tensor("f0hpsq", [128, 2048], BF16))
        psm = [ctx.enter_context(nc.psum_tensor(f"f0hpsm{i}", [128, 512], F32))
               for i in range(2)]

        @block.sync
        def _(sync):
            wait_bar(sync, bar)
            for b in range(BPC):
                for c in range(NDC):
                    sync.dma_start(xnT_sb[:, b * NDC + c, :], e.xnT[b, c, :, :]
                                   ).then_inc(e.LX[0].h, 16)
            for i in range(NO):
                for s_ in range(NSUB):
                    b, ts = divmod(s_, 4)
                    gi = i * NSUB + s_
                    sync.wait_ge(e.s_act.h, act(i, s_ + 1))
                    sync.dma_start(e.hmid[i, :, b * T + ts * 512:b * T + (ts + 1) * 512],
                                   ht[gi % 2][:]).then_inc(e.STX[gi % 2].h, 16)
        e.LX[0].tick(16 * BPC * NDC)
        e.STX[0].tick(16 * (NO * NSUB // 2))
        e.STX[1].tick(16 * (NO * NSUB // 2))

        @block.gpsimd
        def _(gp):
            wait_bar(gp, bar)
            for i in range(NO):
                if i >= 2:
                    gp.wait_ge(e.s_dve.h, dve(i - 2, 3))
                gp.dma_start(wt[i % 2][:], e.w_fc[i * 128:(i + 1) * 128, :]
                             ).then_inc(e.LW[i % 2].h, 16)
        e.LW[0].tick(16 * ((NO + 1) // 2))
        e.LW[1].tick(16 * (NO // 2))

        @block.vector
        def _(vector):
            vector = Chained(vector, e.s_dve)
            wait_bar(vector, bar)
            for i in range(NO):
                p = i % 2
                vector.wait_ge(e.LW[p].h, lw(i))
                k = 0
                vector.tensor_scalar(m1[:], wt[p][:], e.thrfc[:], None,
                                     op0=ALU.is_gt); k += 1
                vector.tensor_scalar(m2[:], wt[p][:], e.nthrfc[:], None,
                                     op0=ALU.is_lt); k += 1
                vector.wait_ge(e.s_dve.h, dve(i, k))
                vector.tensor_tensor(q[:], m1[:], m2[:], op=ALU.subtract
                                     ); k += 1
                vector.wait_ge(e.s_pe.h, pe(i, NDC))
                vector.tensor_copy(fcT[:], psq[:].rearrange("p (c n) -> p c n", n=128)
                                  ); k += 1
                for s_ in range(NSUB):
                    g = i * NSUB + s_
                    vector.wait_ge(e.s_pe.h, pe(i, NDC + (s_ + 1) * NDC))
                    if g >= 2:
                        vector.wait_ge(e.s_act.h, act0 + g - 1)
                    vector.tensor_scalar(relu[s_ % 2][:], psm[s_ % 2][:], 0.0, None,
                                         op0=ALU.max); k += 1

        @block.tensor
        def _(tensor):
            wait_bar(tensor, bar)
            tensor.wait_ge(e.LX[0].h, lx0 + 16 * BPC * NDC)
            for i in range(NO):
                tensor.wait_ge(e.s_dve.h, dve(i, 3))
                if i >= 1:
                    tensor.wait_ge(e.s_dve.h, dve(i - 1, 4))
                for c in range(NDC):
                    tensor.transpose(psq[:, c * 128:(c + 1) * 128],
                                     q[:, c * 128:(c + 1) * 128], e.idb[:]
                                     ).then_inc(e.s_pe.h, 1)
                tensor.wait_ge(e.s_dve.h, dve(i, 4))
                for s_ in range(NSUB):
                    b, ts = divmod(s_, 4)
                    g = i * NSUB + s_
                    if g >= 2:
                        pi, ps = divmod(g - 2, NSUB)
                        tensor.wait_ge(e.s_dve.h, dve(pi, 5 + ps))
                    for c in range(NDC):
                        tensor.matmul(psm[s_ % 2][:], fcT[:, c, :],
                                      xnT_sb[:, b * NDC + c, ts * 512:(ts + 1) * 512],
                                      start=(c == 0), stop=(c == NDC - 1)
                                      ).then_inc(e.s_pe.h, 1)
        e.s_pe.tick(PE * NO)

        @block.scalar
        def _(scalar):
            scalar = Chained(scalar, e.s_act)
            wait_bar(scalar, bar)
            for i in range(NO):
                for s_ in range(NSUB):
                    gi = i * NSUB + s_
                    scalar.wait_ge(e.s_dve.h, dve(i, 5 + s_))
                    if gi >= 2:
                        scalar.wait_ge(e.STX[gi % 2].h, st(gi - 2))
                    scalar.activation(ht[gi % 2][:], relu[s_ % 2][:], AF.Square
                                      )



def emit_ffn0p(e, block):
    nc = e.nc
    NOC = FFH // 128
    NDH = 8
    TS = 256
    NSUB = T // TS
    NU = BPC * NSUB
    bar = bar_of(e)
    dve0, pe0, act0 = e.s_dve.n, e.s_pe.n, e.s_act.n
    lw0 = e.LW[0].n
    lx0 = [e.LX[0].n, e.LX[1].n]
    st0 = [e.STT[0].n, e.STT[1].n]
    PREP_ACT, PREP_DVE, PREP_PE = 2, 5, 40
    MM_PE = NDH * NOC
    H_ACT = PREP_ACT * NDH + NU * NDH
    H_DVE = PREP_DVE * NDH
    H_PE = PREP_PE * NDH + NU * MM_PE

    def p_act(h, i, k): return act0 + H_ACT * h + PREP_ACT * i + k
    def p_dve(h, i, k): return dve0 + H_DVE * h + PREP_DVE * i + k
    def p_pe(h, i, k):  return pe0 + H_PE * h + PREP_PE * i + k
    def m_pe(h, u, j, k):
        return pe0 + H_PE * h + PREP_PE * NDH + MM_PE * u + NOC * j + k
    def m_act(h, u, j):
        return act0 + H_ACT * h + PREP_ACT * NDH + NDH * u + j + 1
    def m_ld(h, u):
        g = h * NU + u
        return lx0[g % 2] + 16 * NOC * (g // 2 + 1)
    def m_st(h, u, j):
        gi = h * NU * NDH + u * NDH + j
        return st0[gi % 2] + 16 * (gi // 2 + 1)

    with ExitStack() as ctx:
        def sb(name, shape, dt=F32):
            return ctx.enter_context(nc.sbuf_tensor("f0p" + name, shape, dt))
        wt = sb("w", [128, FFH])
        a = sb("a", [128, FFH])
        mbf = sb("mbf", [128, FFH], BF16)
        q = sb("q", [128, FFH], BF16)
        projT = sb("projT", [128, NOC, NDH, 128], BF16)
        h_sb = [sb(f"h{i}", [128, NOC, TS], BF16) for i in range(2)]
        g_t = [sb(f"g{i}", [128, TS]) for i in range(2)]
        psq = ctx.enter_context(nc.psum_tensor("f0ppsq", [128, 2048], BF16))
        psm = [ctx.enter_context(nc.psum_tensor(f"f0ppsm{i}", [128, TS], F32))
               for i in range(2)]

        @block.gpsimd
        def _(gp):
            wait_bar(gp, bar)
            for h in range(2):
                for i in range(NDH):
                    gi = h * NDH + i
                    if gi:
                        ph, pi = divmod(gi - 1, NDH)
                        gp.wait_ge(e.s_act.h, p_act(ph, pi, 2))
                    dd0 = gi * 128
                    gp.dma_start(wt[:], e.w_proj[dd0:dd0 + 128, :]
                                 ).then_inc(e.LW[0].h, 16)
                for u in range(NU):
                    b, ts = divmod(u, NSUB)
                    for j in range(NDH):
                        gi = h * NU * NDH + u * NDH + j
                        gp.wait_ge(e.s_act.h, m_act(h, u, j))
                        gp.dma_start(
                            e.g0[b, h * NDH + j, :, ts * TS:(ts + 1) * TS],
                            g_t[(u * NDH + j) % 2][:]).then_inc(e.STT[gi % 2].h, 16)
        e.LW[0].tick(16 * 2 * NDH)
        e.STT[0].tick(16 * NU * NDH)
        e.STT[1].tick(16 * NU * NDH)

        @block.scalar
        def _(scalar):
            scalar = Chained(scalar, e.s_act)
            wait_bar(scalar, bar)
            for h in range(2):
                for i in range(NDH):
                    gi = h * NDH + i
                    scalar.wait_ge(e.LW[0].h, lw0 + 16 * (gi + 1))
                    if gi:
                        ph, pi = divmod(gi - 1, NDH)
                        scalar.wait_ge(e.s_dve.h, p_dve(ph, pi, 2))
                        scalar.wait_ge(e.s_pe.h, p_pe(ph, pi, PREP_PE))
                    scalar.activation(a[:], wt[:], AF.Abs)
                    scalar.activation(q[:], wt[:], AF.Sign)
                for u in range(NU):
                    for j in range(NDH):
                        gi = u * NDH + j
                        gfull = h * NU * NDH + gi
                        scalar.wait_ge(e.s_pe.h, m_pe(h, u, j, NOC))
                        if gfull >= 2:
                            phh, rem = divmod(gfull - 2, NU * NDH)
                            pu, pj = divmod(rem, NDH)
                            scalar.wait_ge(e.STT[gfull % 2].h, m_st(phh, pu, pj))
                        scalar.activation(g_t[gi % 2][:], psm[j % 2][:], AF.Silu,
                                          scale=e.amtot[:, :])

        @block.vector
        def _(vector):
            vector = Chained(vector, e.s_dve)
            wait_bar(vector, bar)
            for h in range(2):
                for i in range(NDH):
                    k = 0
                    vector.wait_ge(e.s_act.h, p_act(h, i, 2))
                    vector.tensor_scalar(mbf[:], a[:], e.thrpj[:], None,
                                         op0=ALU.is_gt); k += 1
                    vector.wait_ge(e.s_dve.h, p_dve(h, i, k))
                    vector.tensor_tensor(q[:], q[:], mbf[:], op=ALU.mult
                                         ); k += 1
                    for r, (o0, no) in enumerate([(0, 16), (16, 16), (32, 8)]):
                        vector.wait_ge(e.s_pe.h, p_pe(h, i, o0 + no))
                        vector.tensor_copy(
                            projT[:, o0:o0 + no, i, :],
                            psq[:, :no * 128].rearrange("p (c n) -> p c n", n=128)
                        ); k += 1

        @block.tensor
        def _(tensor):
            wait_bar(tensor, bar)
            for h in range(2):
                for i in range(NDH):
                    if h:
                        tensor.wait_ge(e.s_pe.h, pe0 + H_PE)
                    tensor.wait_ge(e.s_dve.h, p_dve(h, i, 2))
                    for r, (o0, no) in enumerate([(0, 16), (16, 16), (32, 8)]):
                        if r:
                            tensor.wait_ge(e.s_dve.h, p_dve(h, i, 2 + r))
                        elif h or i:
                            ph, pi = divmod(h * NDH + i - 1, NDH)
                            tensor.wait_ge(e.s_dve.h, p_dve(ph, pi, 5))
                        for c in range(no):
                            tensor.transpose(psq[:, c * 128:(c + 1) * 128],
                                             q[:, (o0 + c) * 128:(o0 + c + 1) * 128],
                                             e.idb[:]).then_inc(e.s_pe.h, 1)
                tensor.wait_ge(e.s_dve.h, p_dve(h, NDH - 1, 5))
                for u in range(NU):
                    g = h * NU + u
                    tensor.wait_ge(e.LX[g % 2].h, m_ld(h, u))
                    for j in range(NDH):
                        gi = u * NDH + j
                        if gi >= 2:
                            pu, pj = divmod(gi - 2, NDH)
                            tensor.wait_ge(e.s_act.h, m_act(h, pu, pj))
                        for c in range(NOC):
                            tensor.matmul(psm[j % 2][:], projT[:, c, j, :],
                                          h_sb[u % 2][:, c, :],
                                          start=(c == 0), stop=(c == NOC - 1)
                                          ).then_inc(e.s_pe.h, 1)
        e.s_pe.tick(2 * H_PE)

        @block.sync
        def _(sync):
            wait_bar(sync, bar)
            for h in range(2):
                for u in range(NU):
                    b, ts = divmod(u, NSUB)
                    g = h * NU + u
                    if g >= 2:
                        ph, pu = divmod(g - 2, NU)
                        sync.wait_ge(e.s_pe.h, m_pe(ph, pu, NDH - 1, NOC))
                    for c in range(NOC):
                        sync.dma_start(
                            h_sb[u % 2][:, c, :],
                            e.hmid[c, :, b * T + ts * TS:b * T + (ts + 1) * TS]
                        ).then_inc(e.LX[g % 2].h, 16)
        e.LX[0].tick(16 * NOC * NU)
        e.LX[1].tick(16 * NOC * NU)



def emit_attn_batch(e, block, b):
    nc = e.nc

    with ExitStack() as bctx:
        def bsb(name, shape, dt=F32):
            return bctx.enter_context(nc.sbuf_tensor(f"at{b}{name}", shape, dt))
        abctx = bctx.enter_context(ExitStack())
        def absb(name, shape, dt=F32):
            return abctx.enter_context(nc.sbuf_tensor(f"at{b}{name}", shape, dt))
        kvlat = absb("kvlat", [128, 4, T], BF16)
        qlat = absb("qlat", [128, 8, T], BF16)
        krT = absb("krT", [128, T], BF16)
        qrT = absb("qrT", [128, T], BF16)
        psA = [bctx.enter_context(nc.psum_tensor(f"at{b}psA{i}", [128, 512], F32))
               for i in range(2)]
        psS = bctx.enter_context(nc.psum_tensor(f"at{b}psS", [128, 2048], F32))
        psT = bctx.enter_context(nc.psum_tensor(f"at{b}psT", [128, 2048], BF16))

        with ExitStack() as actx:
            def asb(name, shape, dt=F32):
                return actx.enter_context(nc.sbuf_tensor(f"at{b}a{name}", shape, dt))
            xnT_sb = asb("xnT", [128, NDC, T], BF16)
            wtmp = [asb(f"wt{i}", [128, 1024]) for i in range(2)]
            a1ctx = actx.enter_context(ExitStack())
            def a1sb(name, shape, dt=F32):
                return a1ctx.enter_context(nc.sbuf_tensor(f"at{b}a{name}", shape, dt))
            cosb = a1sb("cos", [128, T])
            sinb = a1sb("sin", [128, T])
            krw = a1sb("krw", [128, NDC, 128], BF16)
            qrw = a1sb("qrw", [128, NDC, 128], BF16)
            krf = a1sb("krf", [128, T])
            qrf = a1sb("qrf", [128, T])
            krs = a1sb("krs", [128, T])

            bar = bar_of(e)
            dve0, pe0 = e.s_dve.n, e.s_pe.n
            lx0 = e.LX[0].n
            lw0 = [e.LW[0].n, e.LW[1].n]

            @block.sync
            def _(sync):
                wait_bar(sync, bar)
                sync.dma_start(cosb[:], e.cos2_d[:]).then_inc(e.LX[0].h, 16)
                sync.dma_start(sinb[:], e.sin2_d[:]).then_inc(e.LX[0].h, 16)
                for c in range(NDC):
                    sync.dma_start(xnT_sb[:, c, :], e.xnT[b, c, :, :]
                                   ).then_inc(e.LX[0].h, 16)
            e.LX[0].tick(16 * (2 + NDC))

            @block.gpsimd
            def _(gp):
                wait_bar(gp, bar)
                for l in range(32):
                    w = e.w_kr if l < 16 else e.w_qr
                    c = l % 16
                    if l >= 2:
                        gp.wait_ge(e.s_dve.h, dve0 + l - 1)
                    gp.dma_start(wtmp[l % 2][:, 0:128], w[c * 128:(c + 1) * 128, :]
                                 ).then_inc(e.LW[l % 2].h, 16)
            e.LW[0].tick(16 * 16)
            e.LW[1].tick(16 * 16)

            @block.vector
            def _(vector):
                vector = Chained(vector, e.s_dve)
                wait_bar(vector, bar)
                for l in range(32):
                    vector.wait_ge(e.LW[l % 2].h, lw0[l % 2] + 16 * (l // 2 + 1))
                    dst = krw if l < 16 else qrw
                    vector.tensor_copy(dst[:, l % 16, :], wtmp[l % 2][:, 0:128])
                for g in range(8):
                    ts, j = divmod(g, 2)
                    vector.wait_ge(e.s_pe.h, pe0 + 16 * (g + 1))
                    dst = krf if j == 0 else qrf
                    vector.tensor_copy(dst[:, ts * 512:(ts + 1) * 512],
                                       psA[g % 2][:])
                vector.wait_ge(e.LX[0].h, lx0 + 16 * (2 + NDC))
                vector.stream_shuffle(krs[:], krf[:],
                                      [i ^ 1 for i in range(32)])
                vector.tensor_tensor(krf[:], krf[:], cosb[:], op=ALU.mult)
                vector.tensor_tensor(krs[:], krs[:], sinb[:], op=ALU.mult)
                vector.tensor_tensor(krT[:], krf[:], krs[:], op=ALU.add)
                vector.stream_shuffle(krs[:], qrf[:],
                                      [i ^ 1 for i in range(32)])
                vector.tensor_tensor(qrf[:], qrf[:], cosb[:], op=ALU.mult)
                vector.tensor_tensor(krs[:], krs[:], sinb[:], op=ALU.mult)
                vector.tensor_tensor(qrT[:], qrf[:], krs[:], op=ALU.add)

            @block.tensor
            def _(tensor):
                wait_bar(tensor, bar)
                tensor.wait_ge(e.LX[0].h, lx0 + 16 * (2 + NDC))
                for g in range(8):
                    ts, j = divmod(g, 2)
                    tensor.wait_ge(e.s_dve.h, dve0 + (16 if j == 0 else 32))
                    if g >= 2:
                        tensor.wait_ge(e.s_dve.h, dve0 + 32 + g - 1)
                    w = krw if j == 0 else qrw
                    for c in range(NDC):
                        tensor.matmul(psA[g % 2][:], w[:, c, :],
                                      xnT_sb[:, c, ts * 512:(ts + 1) * 512],
                                      start=(c == 0), stop=(c == NDC - 1)
                                      ).then_inc(e.s_pe.h, 1)
            e.s_pe.tick(16 * 8)

            a1ctx.close()
            kvdb = asb("kvdb", [128, NDC, 512], BF16)
            qdb = asb("qdb", [128, NDC, 1024], BF16)
            dve0, pe0 = e.s_dve.n, e.s_pe.n
            lw0 = [e.LW[0].n, e.LW[1].n]

            @block.gpsimd
            def _(gp):
                for l in range(32):
                    gp.wait_ge(e.s_dve.h, dve0 + (l - 1 if l >= 2 else 0))
                    if l < 16:
                        src = e.w_kvd[l * 128:(l + 1) * 128, :]
                        gp.dma_start(wtmp[l % 2][:, 0:512], src
                                     ).then_inc(e.LW[l % 2].h, 16)
                    else:
                        c = l - 16
                        src = e.w_qd[c * 128:(c + 1) * 128, :]
                        gp.dma_start(wtmp[l % 2][:, 0:1024], src
                                     ).then_inc(e.LW[l % 2].h, 16)
            e.LW[0].tick(16 * 16)
            e.LW[1].tick(16 * 16)

            @block.vector
            def _(vector):
                vector = Chained(vector, e.s_dve)
                for l in range(32):
                    vector.wait_ge(e.LW[l % 2].h, lw0[l % 2] + 16 * (l // 2 + 1))
                    if l < 16:
                        vector.tensor_copy(kvdb[:, l, :], wtmp[l % 2][:, 0:512])
                    else:
                        vector.tensor_copy(qdb[:, l - 16, :], wtmp[l % 2][:, 0:1024])
                for g in range(48):
                    ts, j = divmod(g, 12)
                    vector.wait_ge(e.s_pe.h, pe0 + 16 * (g + 1))
                    if j < 4:
                        vector.tensor_copy(kvlat[:, j, ts * 512:(ts + 1) * 512],
                                           psA[g % 2][:])
                    else:
                        vector.tensor_copy(qlat[:, j - 4, ts * 512:(ts + 1) * 512],
                                           psA[g % 2][:])

            @block.tensor
            def _(tensor):
                tensor.wait_ge(e.s_dve.h, dve0)
                for g in range(48):
                    ts, j = divmod(g, 12)
                    tensor.wait_ge(e.s_dve.h, dve0 + (16 if j < 4 else 32))
                    if g >= 2:
                        tensor.wait_ge(e.s_dve.h, dve0 + 32 + g - 1)
                    for c in range(NDC):
                        if j < 4:
                            lhs = kvdb[:, c, j * 128:(j + 1) * 128]
                        else:
                            lhs = qdb[:, c, (j - 4) * 128:(j - 3) * 128]
                        tensor.matmul(psA[g % 2][:], lhs,
                                      xnT_sb[:, c, ts * 512:(ts + 1) * 512],
                                      start=(c == 0), stop=(c == NDC - 1)
                                      ).then_inc(e.s_pe.h, 1)
            e.s_pe.tick(16 * 48)

        sto0 = [e.STT[0].n, e.STT[1].n]
        with ExitStack() as hctx:
            def hsb(name, shape, dt=F32):
                return hctx.enter_context(nc.sbuf_tensor(f"at{b}h{name}", shape, dt))
            wtm2 = [hsb(f"wt{i}", [128, 128]) for i in range(2)]
            upk = hsb("upk", [128, 4, 128], BF16)
            upv = hsb("upv", [128, 4, 128], BF16)
            quw = hsb("quw", [128, 8, 128], BF16)
            kcT = hsb("kcT", [128, T], BF16)
            vtok = hsb("vtok", [128, NDC, 128], BF16)
            qcT = hsb("qcT", [128, T], BF16)
            ef = [hsb(f"ef{i}", [128, T]) for i in range(2)]
            attnT = hsb("attnT", [128, NDC, 512], BF16)
            nmx = hsb("nmx", [128, 4])
            sums = hsb("sums", [128, 2, 4])
            stg_o = [hsb(f"stgo{i}", [128, 512], BF16) for i in range(2)]
            prep_mm = [4] * 4 + [4] * 16 + [8] * 4
            ends = []
            acc = 0
            for m in prep_mm:
                acc += m
                ends.append(acc)
            PREP = acc
            pe_sA = {}
            pe_sB = {}
            pe_tr = {}
            pe_out = {}
            p = PREP
            for gq in range(16):
                p += 4
                pe_sA[gq] = p
                p += 4
                pe_sB[gq] = p
                if gq >= 1:
                    p += 16
                    pe_tr[gq - 1] = p
                if gq >= 4 and gq % 4 == 0:
                    p += 16
                    pe_out[gq // 4 - 1] = p
            p += 16
            pe_tr[15] = p
            p += 16
            pe_out[3] = p
            PEH = p
            d_nma = {}
            d_nm16 = {}
            d_rs = {}
            d_ab = {}
            p = 40
            for gq in range(16):
                p += 1
                d_nma[gq] = p
                p += 3
                d_nm16[gq] = p
                if gq >= 1:
                    p += 2
                    d_rs[gq - 1] = p - 1
                    p += 1
                    d_ab[gq - 1] = p
            p += 3
            d_rs[15] = p - 2
            d_ab[15] = p
            DVEH = p
            a_ea = {}
            a_eb = {}
            a_ev = {}
            a_oe = {}
            p = 0
            for gq in range(16):
                p += 1
                a_ea[gq] = p
                p += 1
                a_eb[gq] = p
                if gq >= 1:
                    p += 1
                    a_ev[gq - 1] = p
                if gq >= 4 and gq % 4 == 0:
                    p += 1
                    a_oe[gq // 4 - 1] = p
            p += 1
            a_ev[15] = p
            p += 1
            a_oe[3] = p
            ACTH = p

            for h in range(HEADS):
                bar = bar_of(e)
                dve0, pe0, act0 = e.s_dve.n, e.s_pe.n, e.s_act.n
                lw0 = [e.LW[0].n, e.LW[1].n]

                @block.gpsimd
                def _(gp):
                    for l in range(16):
                        gp.wait_ge(e.s_dve.h, dve0 + (l - 1 if l >= 2 else 0))
                        if l < 4:
                            src = e.w_upk[l * 128:(l + 1) * 128,
                                          h * 128:(h + 1) * 128]
                        elif l < 8:
                            src = e.w_upv[(l - 4) * 128:(l - 3) * 128,
                                          h * 128:(h + 1) * 128]
                        else:
                            src = e.w_qu[(l - 8) * 128:(l - 7) * 128,
                                         h * 128:(h + 1) * 128]
                        gp.dma_start(wtm2[l % 2][:], src).then_inc(e.LW[l % 2].h, 16)
                    for qg in range(4):
                        oi = h * 4 + qg
                        gp.wait_ge(e.s_act.h, act0 + a_oe[qg])
                        gp.dma_start(e.outTd[b, h, :, qg * 512:(qg + 1) * 512],
                                     stg_o[oi % 2][:]).then_inc(e.STT[oi % 2].h, 16)
                e.LW[0].tick(16 * 8)
                e.LW[1].tick(16 * 8)
                e.STT[0].tick(16 * 2)
                e.STT[1].tick(16 * 2)

                @block.vector
                def _(vector):
                    vector = Chained(vector, e.s_dve)
                    vector.wait_ge(e.s_pe.h, pe0)
                    for l in range(16):
                        vector.wait_ge(e.LW[l % 2].h, lw0[l % 2] + 16 * (l // 2 + 1))
                        if l < 4:
                            vector.tensor_copy(upk[:, l, :], wtm2[l % 2][:])
                        elif l < 8:
                            vector.tensor_copy(upv[:, l - 4, :], wtm2[l % 2][:])
                        else:
                            vector.tensor_copy(quw[:, l - 8, :], wtm2[l % 2][:])
                    for g in range(24):
                        vector.wait_ge(e.s_pe.h, pe0 + ends[g])
                        if g < 4:
                            vector.tensor_copy(kcT[:, g * 512:(g + 1) * 512],
                                               psA[g % 2][:])
                        elif g < 20:
                            tt = g - 4
                            vector.tensor_copy(vtok[:, tt, :],
                                               psA[g % 2][:, 0:128])
                        else:
                            ts = g - 20
                            vector.tensor_copy(qcT[:, ts * 512:(ts + 1) * 512],
                                               psA[g % 2][:])

                    def lagged(gq):
                        pp_ = gq % 2
                        vector.wait_ge(e.s_act.h, act0 + a_eb[gq])
                        vector.reduce_sum(sums[:, pp_, 2:3], sums[:, pp_, 0:2],
                                          axis=AX.X)
                        vector.reciprocal(sums[:, pp_, 3:4], sums[:, pp_, 2:3])
                        if gq >= 1:
                            vector.wait_ge(e.s_pe.h, pe0 + pe_tr[gq - 1])
                        vector.tensor_scalar(attn_bf(e)[:], ef[pp_][:],
                                             sums[:, pp_, 3:4], None,
                                             op0=ALU.mult)

                    for gq in range(16):
                        qg, qt = divmod(gq, 4)
                        base_pe = pe0 + pe_sA[gq]
                        vector.wait_ge(e.s_pe.h, pe0 + pe_sA[gq])
                        vector.reduce_max(nmx[:, 0:1], psS[:, 0:1024], axis=AX.X,
                                          negate=True)
                        vector.wait_ge(e.s_pe.h, pe0 + pe_sB[gq])
                        vector.reduce_max(nmx[:, 1:2], psS[:, 1024:2048],
                                          axis=AX.X, negate=True)
                        vector.tensor_reduce(nmx[:, 2:3], nmx[:, 0:2],
                                             axis=AX.X, op=ALU.min)
                        if gq >= 1:
                            vector.wait_ge(e.s_act.h, act0 + a_eb[gq - 1])
                        vector.tensor_scalar(nmx[:, 3:4], nmx[:, 2:3],
                                             1.0 / 16.0, None, op0=ALU.mult)
                        if gq >= 1:
                            lagged(gq - 1)
                    lagged(15)

                @block.scalar
                def _(scalar):
                    scalar = Chained(scalar, e.s_act)

                    def evac(gq):
                        scalar.wait_ge(e.s_pe.h, pe0 + pe_tr[gq])
                        scalar.copy(attnT[:, :, (gq % 4) * 128:(gq % 4 + 1) * 128],
                                    psT[:].rearrange("p (c n) -> p c n", n=128))

                    def outev(qg):
                        oi = h * 4 + qg
                        scalar.wait_ge(e.s_pe.h, pe0 + pe_out[qg])
                        if oi >= 2:
                            scalar.wait_ge(e.STT[oi % 2].h,
                                           sto0[oi % 2] + 16 * (oi // 2))
                        scalar.copy(stg_o[oi % 2][:], psA[qg % 2][:])

                    for gq in range(16):
                        pp_ = gq % 2
                        if gq >= 2:
                            scalar.wait_ge(e.s_dve.h, dve0 + d_ab[gq - 2])
                        scalar.wait_ge(e.s_dve.h, dve0 + d_nm16[gq])
                        scalar.wait_ge(e.s_pe.h, pe0 + pe_sA[gq])
                        scalar.activation(ef[pp_][:, 0:1024], psS[:, 0:1024],
                                          AF.Exp, bias=nmx[:, 3:4],
                                          scale=1.0 / 16.0,
                                          accum_out=sums[:, pp_, 0:1])
                        scalar.wait_ge(e.s_pe.h, pe0 + pe_sB[gq])
                        scalar.activation(ef[pp_][:, 1024:2048], psS[:, 1024:2048],
                                          AF.Exp, bias=nmx[:, 3:4],
                                          scale=1.0 / 16.0,
                                          accum_out=sums[:, pp_, 1:2])
                        if gq >= 1:
                            evac(gq - 1)
                        if gq >= 4 and gq % 4 == 0:
                            outev(gq // 4 - 1)
                    evac(15)
                    outev(3)

                @block.tensor
                def _(tensor):
                    tensor.wait_ge(e.s_dve.h, dve0)
                    tensor.wait_ge(e.s_act.h, act0)
                    for g in range(24):
                        if g < 4:
                            tensor.wait_ge(e.s_dve.h, dve0 + 4)
                        elif g < 20:
                            tensor.wait_ge(e.s_dve.h, dve0 + 8)
                        else:
                            tensor.wait_ge(e.s_dve.h, dve0 + 16)
                        if g >= 2:
                            tensor.wait_ge(e.s_dve.h, dve0 + 16 + g - 1)
                        if g < 4:
                            ts = g
                            for kc in range(4):
                                tensor.matmul(psA[g % 2][:], upk[:, kc, :],
                                              kvlat[:, kc, ts * 512:(ts + 1) * 512],
                                              start=(kc == 0), stop=(kc == 3)
                                              ).then_inc(e.s_pe.h, 1)
                        elif g < 20:
                            tt = g - 4
                            for kc in range(4):
                                tensor.matmul(psA[g % 2][:, 0:128],
                                              kvlat[:, kc, tt * 128:(tt + 1) * 128],
                                              upv[:, kc, :],
                                              start=(kc == 0), stop=(kc == 3)
                                              ).then_inc(e.s_pe.h, 1)
                        else:
                            ts = g - 20
                            for qc in range(8):
                                tensor.matmul(psA[g % 2][:], quw[:, qc, :],
                                              qlat[:, qc, ts * 512:(ts + 1) * 512],
                                              start=(qc == 0), stop=(qc == 7)
                                              ).then_inc(e.s_pe.h, 1)
                    tensor.wait_ge(e.s_dve.h, dve0 + 40)

                    def transposes(gq):
                        tensor.wait_ge(e.s_dve.h, dve0 + d_ab[gq])
                        if gq >= 1:
                            tensor.wait_ge(e.s_act.h, act0 + a_ev[gq - 1])
                        for c in range(NDC):
                            tensor.transpose(psT[:, c * 128:(c + 1) * 128],
                                             attn_bf(e)[:, c * 128:(c + 1) * 128],
                                             e.idb[:]).then_inc(e.s_pe.h, 1)

                    def outmm(qg):
                        tensor.wait_ge(e.s_act.h, act0 + a_ev[4 * qg + 3])
                        if qg >= 2:
                            tensor.wait_ge(e.s_act.h, act0 + a_oe[qg - 2])
                        for c in range(NDC):
                            tensor.matmul(psA[qg % 2][:], vtok[:, c, :],
                                          attnT[:, c, :],
                                          start=(c == 0), stop=(c == NDC - 1)
                                          ).then_inc(e.s_pe.h, 1)

                    for gq in range(16):
                        qg, qt = divmod(gq, 4)
                        q0 = qg * 512 + qt * 128
                        for half2 in range(2):
                            if gq >= 1:
                                tensor.wait_ge(e.s_dve.h,
                                               dve0 + d_nma[gq - 1] + half2)
                                tensor.wait_ge(e.s_act.h,
                                               act0 + (a_ea if half2 == 0
                                                       else a_eb)[gq - 1])
                            for ks in (0, 1):
                                kk0 = half2 * 1024 + ks * 512
                                tensor.matmul(psS[:, kk0:kk0 + 512],
                                              qcT[:, q0:q0 + 128],
                                              kcT[:, kk0:kk0 + 512],
                                              start=True, stop=False
                                              ).then_inc(e.s_pe.h, 1)
                            for ks in (0, 1):
                                kk0 = half2 * 1024 + ks * 512
                                tensor.matmul(psS[:, kk0:kk0 + 512],
                                              qrT[:, q0:q0 + 128],
                                              krT[:, kk0:kk0 + 512],
                                              start=False, stop=True
                                              ).then_inc(e.s_pe.h, 1)
                        if gq >= 1:
                            transposes(gq - 1)
                        if gq >= 4 and gq % 4 == 0:
                            outmm(gq // 4 - 1)
                    transposes(15)
                    outmm(3)
                e.s_pe.tick(PEH)

        abctx.close()
        with ExitStack() as cctx:
            def csb(name, shape, dt=F32):
                return cctx.enter_context(nc.sbuf_tensor(f"at{b}c{name}", shape, dt))
            wtm3 = [csb(f"wt{i}", [128, D]) for i in range(2)]
            woutb = csb("wout", [128, NDC, D], BF16)
            outT = csb("outT", [128, HEADS, T], BF16)
            stg = [csb(f"stg{i}", [128, 512]) for i in range(2)]
            bar = bar_of(e)
            dve0, pe0 = e.s_dve.n, e.s_pe.n
            lw0 = [e.LW[0].n, e.LW[1].n]
            st0 = [e.STT[0].n, e.STT[1].n]
            lx0c = e.LX[0].n

            @block.sync
            def _(sync):
                for s_h, n in bar:
                    if n:
                        sync.wait_ge(s_h.h, n)
                for hh in range(HEADS):
                    sync.dma_start(outT[:, hh, :], e.outTd[b, hh, :, :]
                                   ).then_inc(e.LX[0].h, 16)
            e.LX[0].tick(16 * HEADS)

            @block.gpsimd
            def _(gp):
                for l in range(16):
                    gp.wait_ge(e.s_dve.h, dve0 + (l - 1 if l >= 2 else 0))
                    gp.dma_start(wtm3[l % 2][:], e.w_out[l * 128:(l + 1) * 128, :]
                                 ).then_inc(e.LW[l % 2].h, 16)
                for u in range(64):
                    ts, ddt = divmod(u, NDC)
                    gp.wait_ge(e.s_dve.h, dve0 + 16 + u + 1)
                    gp.dma_start(e.s0[b, ddt, :, ts * 512:(ts + 1) * 512],
                                 stg[u % 2][:]).then_inc(e.STT[u % 2].h, 16)
            e.LW[0].tick(16 * 8)
            e.LW[1].tick(16 * 8)
            e.STT[0].tick(16 * 32)
            e.STT[1].tick(16 * 32)

            @block.vector
            def _(vector):
                vector = Chained(vector, e.s_dve)
                for l in range(16):
                    vector.wait_ge(e.LW[l % 2].h, lw0[l % 2] + 16 * (l // 2 + 1))
                    vector.tensor_copy(woutb[:, l, :], wtm3[l % 2][:])
                for u in range(64):
                    vector.wait_ge(e.s_pe.h, pe0 + 16 * (u + 1))
                    if u >= 2:
                        vector.wait_ge(e.STT[u % 2].h, st0[u % 2] + 16 * (u // 2))
                    vector.tensor_copy(stg[u % 2][:], psA[u % 2][:])

            @block.tensor
            def _(tensor):
                tensor.wait_ge(e.LX[0].h, lx0c + 16 * HEADS)
                for u in range(64):
                    ts, ddt = divmod(u, NDC)
                    tensor.wait_ge(e.s_dve.h, dve0 + 16)
                    if u >= 2:
                        tensor.wait_ge(e.s_dve.h, dve0 + 16 + u - 1)
                    for c in range(NDC):
                        tensor.matmul(psA[u % 2][:],
                                      woutb[:, c, ddt * 128:(ddt + 1) * 128],
                                      outT[:, c, ts * 512:(ts + 1) * 512],
                                      start=(c == 0), stop=(c == NDC - 1)
                                      ).then_inc(e.s_pe.h, 1)
            e.s_pe.tick(16 * 64)


def attn_bf(e):
    return e._attn_bf


def emit_attn(e, block):
    e._attn_bf_guard = e.top_ctx.enter_context(
        e.nc.sbuf_tensor("attnbf", [128, T], BF16))
    e._attn_bf = e._attn_bf_guard
    for b in range(BPC):
        emit_attn_batch(e, block, b)



def emit_l1prep(e, block):
    nc = e.nc
    bar = bar_of(e)
    dve0, act0 = e.s_dve.n, e.s_act.n
    lx0 = e.LX[0].n

    @block.sync
    def _(sync):
        wait_bar(sync, bar)
        for i, v in enumerate(("bk", "bv", "br", "bo", "tmk", "tmv", "tmr",
                               "td", "tf")):
            sync.dma_start(e.vecs[:, i, :], getattr(e, v)[:]).then_inc(e.LX[0].h, 16)
    e.LX[0].tick(16 * 9)

    @block.scalar
    def _(scalar):
        scalar = Chained(scalar, e.s_act)
        wait_bar(scalar, bar)
        scalar.wait_ge(e.LX[0].h, lx0 + 16 * 9)
        scalar.activation(e.dvecs[:, 0, :], e.vecs[:, 7, :], AF.Exp)

    @block.vector
    def _(vector):
        vector = Chained(vector, e.s_dve)
        wait_bar(vector, bar)
        vector.wait_ge(e.LX[0].h, lx0 + 16 * 9)
        vector.wait_ge(e.s_act.h, act0 + 1)
        vector.tensor_scalar(e.dvecs[:, 0, :], e.dvecs[:, 0, :], -1.0, None,
                             op0=ALU.mult)
        vector.tensor_scalar(e.dvecs[:, 1, :], e.vecs[:, 8, :], -1.0, None,
                             op0=ALU.mult)
        for j in range(3):
            vector.tensor_scalar(e.dvecs[:, 2 + j, :], e.vecs[:, 4 + j, :],
                                 -1.0, 1.0, op0=ALU.mult, op1=ALU.add)
        vector.memset(e.dvecs[:, 5, :], 0.0)



def emit_kvr(e, block):
    nc = e.nc
    with ExitStack() as ctx:
        def sb(name, shape, dt=F32):
            return ctx.enter_context(nc.sbuf_tensor("kvr" + name, shape, dt))
        wtmp = [sb(f"wt{i}", [128, D]) for i in range(2)]
        wbf = sb("wbf", [128, D], BF16)
        WT = sb("WT", [128, NDC, NDC, 128], BF16)
        xp = sb("xp", [128, NDC, T + 1], BF16)
        xk = sb("xk", [128, NDC, 512], BF16)
        stg = [sb(f"stg{i}", [128, 512]) for i in range(2)]
        psq = ctx.enter_context(nc.psum_tensor("kvrpsq", [128, 2048], BF16))
        psm = [ctx.enter_context(nc.psum_tensor(f"kvrpsm{i}", [128, 512], F32))
               for i in range(2)]

        for wi, (wsrc, dstT, bcol, tmcol) in enumerate(
                [(e.Wk, e.kT, 0, 2), (e.Wv, e.vT, 1, 3), (e.Wr, e.rT, 2, 4)]):
            bar = bar_of(e)
            dve0, pe0 = e.s_dve.n, e.s_pe.n
            lw0 = [e.LW[0].n, e.LW[1].n]

            @block.gpsimd
            def _(gp):
                wait_bar(gp, bar)
                for l in range(NDC):
                    gp.wait_ge(e.s_dve.h, dve0 + (2 * (l - 2) + 2 if l >= 2 else 0))
                    gp.dma_start(wtmp[l % 2][:], wsrc[l * 128:(l + 1) * 128, :]
                                 ).then_inc(e.LW[l % 2].h, 16)
            e.LW[0].tick(16 * 8)
            e.LW[1].tick(16 * 8)

            @block.vector
            def _(vector):
                vector = Chained(vector, e.s_dve)
                wait_bar(vector, bar)
                for l in range(NDC):
                    vector.wait_ge(e.LW[l % 2].h, lw0[l % 2] + 16 * (l // 2 + 1))
                    vector.wait_ge(e.s_pe.h, pe0 + 16 * l)
                    vector.tensor_copy(wbf[:], wtmp[l % 2][:])
                    vector.wait_ge(e.s_pe.h, pe0 + 16 * (l + 1))
                    vector.tensor_copy(WT[:, :, l, :],
                                       psq[:].rearrange("p (c n) -> p c n", n=128))

            @block.tensor
            def _(tensor):
                wait_bar(tensor, bar)
                for l in range(NDC):
                    tensor.wait_ge(e.s_dve.h, dve0 + 2 * l + 1)
                    for c in range(NDC):
                        tensor.transpose(psq[:, c * 128:(c + 1) * 128],
                                         wbf[:, c * 128:(c + 1) * 128],
                                         e.idb[:]).then_inc(e.s_pe.h, 1)
            e.s_pe.tick(16 * NDC)

            for b in range(BPC):
                bar = bar_of(e)
                dve0, pe0 = e.s_dve.n, e.s_pe.n
                lx0 = e.LX[0].n
                st0 = [e.STT[0].n, e.STT[1].n]

                @block.sync
                def _(sync):
                    for s_h, n in bar:
                        if n:
                            sync.wait_ge(s_h.h, n)
                    for c in range(NDC):
                        sync.dma_start(xp[:, c, 1:T + 1], e.xn1T[b, c, :, :]
                                       ).then_inc(e.LX[0].h, 16)
                e.LX[0].tick(16 * NDC)

                LRP = 2 * NDC
                TSD = LRP + NDC

                @block.vector
                def _(vector):
                    vector = Chained(vector, e.s_dve)
                    for s_h, n in bar:
                        if n:
                            vector.wait_ge(s_h.h, n)
                    vector.wait_ge(e.LX[0].h, lx0 + 16 * NDC)
                    vector.memset(xp[:, :, 0:1], 0.0)
                    for ts in range(4):
                        if ts >= 1:
                            vector.wait_ge(e.s_pe.h, pe0 + 16 * NDC * ts)
                        for c in range(NDC):
                            vector.tensor_scalar(
                                xk[:, c, :], xp[:, c, ts * 512 + 1:ts * 512 + 513],
                                e.vecs[:, 4 + wi, c:c + 1], None, op0=ALU.mult)
                            vector.scalar_tensor_tensor(
                                xk[:, c, :], xp[:, c, ts * 512:ts * 512 + 512],
                                e.dvecs[:, 2 + wi, c:c + 1], xk[:, c, :],
                                op0=ALU.mult, op1=ALU.add)
                        for ot in range(NDC):
                            g = ts * NDC + ot
                            vector.wait_ge(e.s_pe.h, pe0 + 16 * (g + 1))
                            if g >= 2:
                                vector.wait_ge(e.STT[g % 2].h,
                                               st0[g % 2] + 16 * (g // 2))
                            vector.tensor_scalar(stg[g % 2][:], psm[g % 2][:],
                                                 e.vecs[:, bcol, ot:ot + 1], None,
                                                 op0=ALU.add)

                @block.tensor
                def _(tensor):
                    for ts in range(4):
                        tensor.wait_ge(e.s_dve.h, dve0 + 1 + TSD * ts + LRP)
                        for ot in range(NDC):
                            g = ts * NDC + ot
                            if g >= 2:
                                tensor.wait_ge(e.s_dve.h,
                                               dve0 + 1 + TSD * ((g - 2) // NDC)
                                               + LRP + ((g - 2) % NDC) + 1)
                            for c in range(NDC):
                                tensor.matmul(psm[g % 2][:],
                                              WT[:, c, ot, :], xk[:, c, :],
                                              start=(c == 0), stop=(c == NDC - 1)
                                              ).then_inc(e.s_pe.h, 1)
                e.s_pe.tick(16 * 64)

                @block.gpsimd
                def _(gp):
                    for ts in range(4):
                        for ot in range(NDC):
                            g = ts * NDC + ot
                            gp.wait_ge(e.s_dve.h, dve0 + 1 + TSD * ts + LRP + ot + 1)
                            gp.dma_start(dstT[b, ot, :, ts * 512:(ts + 1) * 512],
                                         stg[g % 2][:]).then_inc(e.STT[g % 2].h, 16)
                e.STT[0].tick(16 * 32)
                e.STT[1].tick(16 * 32)



def emit_wkv(e, block):
    nc = e.nc
    CH = 128
    NCHK = T // CH
    HA = NDC // 2
    with ExitStack() as ctx:
        def sb(name, shape, dt=F32):
            return ctx.enter_context(nc.sbuf_tensor("wkv" + name, shape, dt))
        kcb = [sb(f"kc{i}", [128, NDC, CH]) for i in range(2)]
        vcb = [sb(f"vc{i}", [128, NDC, CH]) for i in range(2)]
        rcb = [sb(f"rc{i}", [128, NDC, CH]) for i in range(2)]
        t1 = sb("t1", [128, NDC, CH])
        t2 = sb("t2", [128, NDC, CH])
        t3 = sb("t3", [128, NDC, CH])
        t4 = sb("t4", [128, NDC, CH])
        pp = sb("pp", [128, NDC, CH + 1])
        aa = sb("aa", [128, NDC, CH + 1])
        bb = sb("bb", [128, NDC, CH + 1])
        rw = [sb(f"rw{i}", [128, NDC, CH], BF16) for i in range(2)]
        decB = sb("decB", [128, NDC // 2, CH])
        nbB = sb("nbB", [128, NDC // 2, CH])
        zeroB = sb("zeroB", [128, NDC // 2, CH])

        DVE_CHUNK = 73
        GP_CHUNK = 13
        ACT_CHUNK = 5

        def half(buf, h):
            return buf[:, h * HA:(h + 1) * HA, :]

        for b in range(BPC):
            bar = bar_of(e)
            dve0, gp0, act0 = e.s_dve.n, e.s_gp.n, e.s_act.n
            lx0 = [e.LX[0].n, e.LX[1].n]
            st0 = [e.STT[0].n, e.STT[1].n]

            def dvec(ci, k):  return dve0 + 9 + DVE_CHUNK * ci + k
            def gpc(ci, k):   return gp0 + GP_CHUNK * ci + k
            def actc(ci, k):  return act0 + ACT_CHUNK * ci + k

            @block.sync
            def _(sync):
                for s_h, n in bar:
                    if n:
                        sync.wait_ge(s_h.h, n)
                for ci in range(NCHK):
                    if ci >= 2:
                        sync.wait_ge(e.s_dve.h, dvec(ci - 2, DVE_CHUNK))
                        sync.wait_ge(e.s_gp.h, gpc(ci - 2, GP_CHUNK))
                        sync.wait_ge(e.s_act.h, actc(ci - 2, ACT_CHUNK))
                    q = e.LX[ci % 2].h
                    for c in range(NDC):
                        sync.dma_start(kcb[ci % 2][:, c, :],
                                       e.kT[b, c, :, ci * CH:(ci + 1) * CH]
                                       ).then_inc(q, 16)
                    for c in range(NDC):
                        sync.dma_start(vcb[ci % 2][:, c, :],
                                       e.vT[b, c, :, ci * CH:(ci + 1) * CH]
                                       ).then_inc(q, 16)
                    for c in range(NDC):
                        sync.dma_start(rcb[ci % 2][:, c, :],
                                       e.rT[b, c, :, ci * CH:(ci + 1) * CH]
                                       ).then_inc(q, 16)
            e.LX[0].tick(16 * 3 * NDC * (NCHK // 2))
            e.LX[1].tick(16 * 3 * NDC * (NCHK // 2))

            def emit_half_init(eng, h):
                eng.memset(pp[:, h * HA:(h + 1) * HA, 0:1], -1e30)
                eng.memset(aa[:, h * HA:(h + 1) * HA, 0:1], 0.0)
                eng.memset(bb[:, h * HA:(h + 1) * HA, 0:1], 0.0)

            def emit_dve_chunk(vector, ci):
                h = 0
                ppP = pp[:, 0:HA, 0:CH]
                ppC = pp[:, 0:HA, 1:CH + 1]
                aaP = aa[:, 0:HA, 0:CH]
                bbP = bb[:, 0:HA, 0:CH]
                kc, vc, rc = kcb[ci % 2], vcb[ci % 2], rcb[ci % 2]
                kch, vch, rch = half(kc, 0), half(vc, 0), half(rc, 0)
                t1h, t2h, t3h, t4h = half(t1, 0), half(t2, 0), half(t3, 0), half(t4, 0)
                k = 0
                vector.wait_ge(e.LX[ci % 2].h,
                               lx0[ci % 2] + 16 * 3 * NDC * (ci // 2 + 1))
                for c in range(NDC):
                    vector.tensor_tensor_scan(
                        pp[:, c, 1:CH + 1],
                        e.dvecs[:, 0, c:c + 1].broadcast_to([128, CH]),
                        kc[:, c, :], pp[:, c, 0:1],
                        op0=ALU.add, op1=ALU.max)
                    k += 1
                vector.tensor_tensor(t1h, ppP, ppC, op=ALU.subtract); k += 1
                vector.tensor_tensor(
                    t1h, t1h, e.dvecs[:, 0, 0:HA].unsqueeze(2)
                    .broadcast_to([128, HA, CH]), op=ALU.add); k += 1
                vector.wait_ge(e.s_act.h, actc(ci, 1))
                vector.tensor_tensor(t1h, kch, ppC, op=ALU.subtract); k += 1
                vector.wait_ge(e.s_act.h, actc(ci, 2))
                vector.tensor_tensor(t4h, t3h, vch, op=ALU.mult); k += 1
                for c in range(HA):
                    vector.tensor_tensor_scan(
                        aa[:, c, 1:CH + 1], t2[:, c, :], t4[:, c, :],
                        aa[:, c, 0:1], op0=ALU.mult, op1=ALU.add)
                    k += 1
                vector.wait_ge(e.s_gp.h, gpc(ci, 4))
                for c in range(HA, NDC):
                    vector.tensor_tensor_scan(
                        aa[:, c, 1:CH + 1], t2[:, c, :], t4[:, c, :],
                        aa[:, c, 0:1], op0=ALU.mult, op1=ALU.add)
                    k += 1
                for c in range(NDC):
                    vector.tensor_tensor_scan(
                        bb[:, c, 1:CH + 1], t2[:, c, :], t3[:, c, :],
                        bb[:, c, 0:1], op0=ALU.mult, op1=ALU.add)
                    k += 1
                vector.tensor_tensor(t1h, ppP, kch, op=ALU.subtract); k += 1
                vector.tensor_tensor(
                    t1h, t1h, e.dvecs[:, 1, 0:HA].unsqueeze(2)
                    .broadcast_to([128, HA, CH]), op=ALU.add); k += 1
                vector.tensor_scalar(t4h, t1h, 0.0, None, op0=ALU.min); k += 1
                vector.tensor_scalar(t1h, t1h, 0.0, None, op0=ALU.max); k += 1
                vector.wait_ge(e.s_gp.h, gpc(ci, 6))
                vector.tensor_scalar(half(t4, 1), half(t1, 1), 0.0, None,
                                     op0=ALU.min); k += 1
                vector.tensor_scalar(half(t1, 1), half(t1, 1), 0.0, None,
                                     op0=ALU.max); k += 1
                vector.wait_ge(e.s_act.h, actc(ci, 4))
                vector.tensor_tensor(t3h, t4h, aaP, op=ALU.mult); k += 1
                vector.tensor_tensor(vch, t1h, vch, op=ALU.mult); k += 1
                vector.tensor_tensor(t3h, t3h, vch, op=ALU.add); k += 1
                vector.tensor_tensor(t4h, t4h, bbP, op=ALU.mult); k += 1
                vector.tensor_tensor(t4h, t4h, t1h, op=ALU.add); k += 1
                vector.reciprocal(t4h, t4h); k += 1
                vector.wait_ge(e.s_gp.h, gpc(ci, 11))
                vector.reciprocal(half(t4, 1), half(t4, 1)); k += 1
                vector.tensor_tensor(t3h, t3h, t4h, op=ALU.mult); k += 1
                vector.wait_ge(e.s_act.h, actc(ci, 5))
                if ci >= 2:
                    oi = b * NCHK + ci
                    vector.wait_ge(e.STT[oi % 2].h,
                                   st0[oi % 2] + 16 * NDC * (ci // 2))
                vector.tensor_tensor(half(rw[ci % 2], 0), t3h, rch,
                                     op=ALU.mult); k += 1
                vector.tensor_copy(pp[:, 0:HA, 0:1], pp[:, 0:HA, CH:CH + 1]); k += 1
                vector.tensor_copy(aa[:, 0:HA, 0:1], aa[:, 0:HA, CH:CH + 1]); k += 1
                vector.tensor_copy(bb[:, 0:HA, 0:1], bb[:, 0:HA, CH:CH + 1]); k += 1
                vector.wait_ge(e.s_gp.h, gpc(ci, GP_CHUNK))
                vector.tensor_copy(pp[:, HA:NDC, 0:1], pp[:, HA:NDC, CH:CH + 1]); k += 1
                vector.tensor_copy(aa[:, HA:NDC, 0:1], aa[:, HA:NDC, CH:CH + 1]); k += 1
                vector.tensor_copy(bb[:, HA:NDC, 0:1], bb[:, HA:NDC, CH:CH + 1]); k += 1
                assert k == DVE_CHUNK, (k, DVE_CHUNK)

            def emit_gp_chunk(gpx, ci):
                ppP = pp[:, HA:NDC, 0:CH]
                ppC = pp[:, HA:NDC, 1:CH + 1]
                aaP = aa[:, HA:NDC, 0:CH]
                bbP = bb[:, HA:NDC, 0:CH]
                kc, vc, rc = kcb[ci % 2], vcb[ci % 2], rcb[ci % 2]
                kch, vch, rch = half(kc, 1), half(vc, 1), half(rc, 1)
                t1h, t2h, t3h, t4h = half(t1, 1), half(t2, 1), half(t3, 1), half(t4, 1)
                zb = zeroB[:]
                k = 0
                gpx.wait_ge(e.LX[ci % 2].h,
                            lx0[ci % 2] + 16 * 3 * NDC * (ci // 2 + 1))
                gpx.wait_ge(e.s_dve.h, dvec(ci, 16))
                gpx.tensor_tensor(t1h, ppP, ppC, op=ALU.subtract); k += 1
                gpx.tensor_tensor(t1h, t1h, decB[:], op=ALU.add); k += 1
                gpx.wait_ge(e.s_act.h, actc(ci, 1))
                gpx.tensor_tensor(t1h, kch, ppC, op=ALU.subtract); k += 1
                gpx.wait_ge(e.s_act.h, actc(ci, 2))
                gpx.tensor_tensor(t4h, t3h, vch, op=ALU.mult); k += 1
                gpx.wait_ge(e.s_dve.h, dvec(ci, 36))
                gpx.tensor_tensor(t1h, ppP, kch, op=ALU.subtract); k += 1
                gpx.tensor_tensor(t1h, t1h, nbB[:], op=ALU.add); k += 1
                gpx.wait_ge(e.s_act.h, actc(ci, 4))
                gpx.wait_ge(e.s_dve.h, dvec(ci, 58))
                gpx.tensor_tensor(t3h, t4h, aaP, op=ALU.mult); k += 1
                gpx.tensor_tensor(vch, t1h, vch, op=ALU.mult); k += 1
                gpx.tensor_tensor(t3h, t3h, vch, op=ALU.add); k += 1
                gpx.tensor_tensor(t4h, t4h, bbP, op=ALU.mult); k += 1
                gpx.tensor_tensor(t4h, t4h, t1h, op=ALU.add); k += 1
                gpx.wait_ge(e.s_dve.h, dvec(ci, 65))
                gpx.tensor_tensor(t3h, t3h, t4h, op=ALU.mult); k += 1
                gpx.wait_ge(e.s_act.h, actc(ci, 5))
                if ci >= 2:
                    oi = b * NCHK + ci
                    gpx.wait_ge(e.STT[oi % 2].h,
                                st0[oi % 2] + 16 * NDC * (ci // 2))
                gpx.tensor_tensor(half(rw[ci % 2], 1), t3h, rch,
                                  op=ALU.mult); k += 1
                assert k == GP_CHUNK, (k, GP_CHUNK)

            @block.vector
            def _(vector):
                vector = Chained(vector, e.s_dve)
                for s_h, n in bar:
                    if n:
                        vector.wait_ge(s_h.h, n)
                emit_half_init(vector, 0)
                emit_half_init(vector, 1)
                vector.tensor_copy(decB[:], e.dvecs[:, 0, HA:NDC].unsqueeze(2)
                                   .broadcast_to([128, HA, CH]))
                vector.tensor_copy(nbB[:], e.dvecs[:, 1, HA:NDC].unsqueeze(2)
                                   .broadcast_to([128, HA, CH]))
                vector.memset(zeroB[:], 0.0)
                for ci in range(NCHK):
                    emit_dve_chunk(vector, ci)

            @block.gpsimd
            def _(gp):
                gpx = Chained(gp, e.s_gp)
                for s_h, n in bar:
                    if n:
                        gp.wait_ge(s_h.h, n)
                for ci in range(NCHK):
                    emit_gp_chunk(gpx, ci)
                    oi = b * NCHK + ci
                    gp.wait_ge(e.s_gp.h, gpc(ci, GP_CHUNK))
                    gp.wait_ge(e.s_dve.h, dvec(ci, DVE_CHUNK - 6))
                    for c in range(NDC):
                        gp.dma_start(e.rwkvT[b, c, :, ci * CH:(ci + 1) * CH],
                                     rw[ci % 2][:, c, :]).then_inc(e.STT[oi % 2].h, 16)

            e.STT[0].tick(16 * NDC * (NCHK // 2))
            e.STT[1].tick(16 * NDC * (NCHK // 2))

            @block.scalar
            def _(scalar):
                scalar = Chained(scalar, e.s_act)
                for s_h, n in bar:
                    if n:
                        scalar.wait_ge(s_h.h, n)
                for ci in range(NCHK):
                    scalar.wait_ge(e.s_dve.h, dvec(ci, 18))
                    scalar.wait_ge(e.s_gp.h, gpc(ci, 2))
                    scalar.activation(t2[:], t1[:], AF.Exp)
                    scalar.wait_ge(e.s_dve.h, dvec(ci, 19))
                    scalar.wait_ge(e.s_gp.h, gpc(ci, 3))
                    scalar.activation(t3[:], t1[:], AF.Exp)
                    scalar.wait_ge(e.s_dve.h, dvec(ci, 58))
                    scalar.wait_ge(e.s_gp.h, gpc(ci, 6))
                    scalar.activation(t4[:], t4[:], AF.Exp)
                    scalar.activation(t1[:], t1[:], AF.Exp, scale=-1.0)
                    scalar.wait_ge(e.LX[ci % 2].h,
                                   lx0[ci % 2] + 16 * 3 * NDC * (ci // 2 + 1))
                    scalar.activation(rcb[ci % 2][:], rcb[ci % 2][:], AF.Sigmoid)



def emit_wo(e, block):
    nc = e.nc
    with ExitStack() as ctx:
        def sb(name, shape, dt=F32):
            return ctx.enter_context(nc.sbuf_tensor("wo" + name, shape, dt))
        wtmp = [sb(f"wt{i}", [128, D]) for i in range(2)]
        wbf = sb("wbf", [128, D], BF16)
        WT = sb("WT", [128, NDC, NDC, 128], BF16)
        rws = sb("rws", [128, NDC, T], BF16)
        stg = [sb(f"stg{i}", [128, 512]) for i in range(2)]
        psq = ctx.enter_context(nc.psum_tensor("wopsq", [128, 2048], BF16))
        psm = [ctx.enter_context(nc.psum_tensor(f"wopsm{i}", [128, 512], F32))
               for i in range(2)]

        bar = bar_of(e)
        dve0, pe0 = e.s_dve.n, e.s_pe.n
        lw0 = [e.LW[0].n, e.LW[1].n]

        @block.gpsimd
        def _(gp):
            wait_bar(gp, bar)
            for l in range(NDC):
                gp.wait_ge(e.s_dve.h, dve0 + (2 * (l - 2) + 2 if l >= 2 else 0))
                gp.dma_start(wtmp[l % 2][:], e.Wo[l * 128:(l + 1) * 128, :]
                             ).then_inc(e.LW[l % 2].h, 16)
        e.LW[0].tick(16 * 8)
        e.LW[1].tick(16 * 8)

        @block.vector
        def _(vector):
            vector = Chained(vector, e.s_dve)
            wait_bar(vector, bar)
            for l in range(NDC):
                vector.wait_ge(e.LW[l % 2].h, lw0[l % 2] + 16 * (l // 2 + 1))
                vector.wait_ge(e.s_pe.h, pe0 + 16 * l)
                vector.tensor_copy(wbf[:], wtmp[l % 2][:])
                vector.wait_ge(e.s_pe.h, pe0 + 16 * (l + 1))
                vector.tensor_copy(WT[:, :, l, :],
                                   psq[:].rearrange("p (c n) -> p c n", n=128))

        @block.tensor
        def _(tensor):
            wait_bar(tensor, bar)
            for l in range(NDC):
                tensor.wait_ge(e.s_dve.h, dve0 + 2 * l + 1)
                for c in range(NDC):
                    tensor.transpose(psq[:, c * 128:(c + 1) * 128],
                                     wbf[:, c * 128:(c + 1) * 128],
                                     e.idb[:]).then_inc(e.s_pe.h, 1)
        e.s_pe.tick(16 * NDC)

        for b in range(BPC):
            bar = bar_of(e)
            dve0, pe0 = e.s_dve.n, e.s_pe.n
            lx0 = e.LX[0].n
            st0 = [e.STT[0].n, e.STT[1].n]

            @block.sync
            def _(sync):
                for s_h, n in bar:
                    if n:
                        sync.wait_ge(s_h.h, n)
                for c in range(NDC):
                    sync.dma_start(rws[:, c, :], e.rwkvT[b, c, :, :]
                                   ).then_inc(e.LX[0].h, 16)
            e.LX[0].tick(16 * NDC)

            @block.vector
            def _(vector):
                vector = Chained(vector, e.s_dve)
                for s_h, n in bar:
                    if n:
                        vector.wait_ge(s_h.h, n)
                for g in range(64):
                    ts, ot = divmod(g, NDC)
                    vector.wait_ge(e.s_pe.h, pe0 + 16 * (g + 1))
                    if g >= 2:
                        vector.wait_ge(e.STT[g % 2].h, st0[g % 2] + 16 * (g // 2))
                    vector.tensor_scalar(stg[g % 2][:], psm[g % 2][:],
                                         e.vecs[:, 3, ot:ot + 1], None,
                                         op0=ALU.add)

            @block.tensor
            def _(tensor):
                tensor.wait_ge(e.LX[0].h, lx0 + 16 * NDC)
                for g in range(64):
                    ts, ot = divmod(g, NDC)
                    if g >= 2:
                        tensor.wait_ge(e.s_dve.h, dve0 + g - 1)
                    for c in range(NDC):
                        tensor.matmul(psm[g % 2][:], WT[:, c, ot, :],
                                      rws[:, c, ts * 512:(ts + 1) * 512],
                                      start=(c == 0), stop=(c == NDC - 1)
                                      ).then_inc(e.s_pe.h, 1)
            e.s_pe.tick(16 * 64)

            @block.gpsimd
            def _(gp):
                for g in range(64):
                    ts, ot = divmod(g, NDC)
                    gp.wait_ge(e.s_dve.h, dve0 + g + 1)
                    gp.dma_start(e.s1[b, ot, :, ts * 512:(ts + 1) * 512],
                                 stg[g % 2][:]).then_inc(e.STT[g % 2].h, 16)
            e.STT[0].tick(16 * 32)
            e.STT[1].tick(16 * 32)



def emit_dq(e, block):
    nc = e.nc
    F1 = 0x3F000000
    F2 = 0x3F800000
    NTI = 2 * I1 // 128 + D // 128

    def tinfo(i):
        if i < 88:
            return ("gu", 256, 16)
        return ("dn", 704, 44)

    def rounds(nblk):
        out = []
        o = 0
        while o < nblk:
            out.append((o, min(16, nblk - o)))
            o += 16
        return out

    dve0, pe0 = e.s_dve.n, e.s_pe.n
    lw0 = [e.LW[0].n, e.LW[1].n]
    lx0 = e.LX[0].n
    st0 = [e.STT[0].n, e.STT[1].n]
    dve_n = dve0
    pe_n = pe0
    st_n = [st0[0], st0[1]]
    sched = []
    for i in range(NTI):
        kind, W, nblk = tinfo(i)
        s = {}
        if i == 0:
            dve_n += 2
        dve_n += 88
        s["plane_end"] = dve_n
        pe_n += 1
        s["sc_tr"] = pe_n
        dve_n += 1
        s["scb_end"] = dve_n
        dve_n += nblk
        s["mults_end"] = dve_n
        s["evac_end"] = []
        s["pe_round"] = []
        s["st_round"] = []
        for (o, nb) in rounds(nblk):
            pe_n += nb
            s["pe_round"].append(pe_n)
            dve_n += 1
            s["evac_end"].append(dve_n)
            st_n[i % 2] += 16 * nb
            s["st_round"].append(st_n[i % 2])
        sched.append(s)

    bar = bar_of(e)
    with ExitStack() as ctx:
        def sb(name, shape, dt=F32):
            return ctx.enter_context(nc.sbuf_tensor("dq" + name, shape, dt))
        gusf = sb("gusf", [64, 2 * I1])
        dnsf = sb("dnsf", [64, D])
        gush = sb("gush", [64, 2 * I1], F16)
        dnsh = sb("dnsh", [64, D], F16)
        pk = [sb(f"pk{i}", [128, 704], I32) for i in range(2)]
        nib = sb("nib", [128, 704], I32)
        b3s = sb("b3s", [128, 704], I32)
        rr = sb("rr", [128, 704], I32)
        bits = sb("bits", [128, 704], I32)
        mskf = sb("mskf", [128, 704])
        s1i = sb("s1i", [128, 704], I32)
        alt = sb("alt", [128, 704])
        val = sb("val", [128, I1])
        scb = sb("scb", [128, 44])
        vbf = sb("vbf", [128, I1], BF16)
        stg = sb("stg", [128, NDC, 128], BF16)
        psq = ctx.enter_context(nc.psum_tensor("dqpsq", [128, 2048], BF16))
        pss = ctx.enter_context(nc.psum_tensor("dqpss", [128, 64], F32))

        @block.sync
        def _(sync):
            wait_bar(sync, bar)
            sync.dma_start(gush[0:16, :], e.gus[:]).then_inc(e.LX[0].h, 16)
            sync.dma_start(dnsh[0:44, :], e.dns[:]).then_inc(e.LX[0].h, 16)
        e.LX[0].tick(16 * 2)

        @block.gpsimd
        def _(gp):
            wait_bar(gp, bar)
            for i in range(NTI):
                kind, W, nblk = tinfo(i)
                if i >= 2:
                    gp.wait_ge(e.s_dve.h, sched[i - 2]["plane_end"])
                if kind == "gu":
                    gp.dma_start(pk[i % 2][:, 0:256],
                                 e.gup[i * 128:(i + 1) * 128, :]
                                 ).then_inc(e.LW[i % 2].h, 16)
                else:
                    j = i - 88
                    gp.dma_start(pk[i % 2][:],
                                 e.dnp[j * 128:(j + 1) * 128, :]
                                 ).then_inc(e.LW[i % 2].h, 16)
                for r, (o, nb) in enumerate(rounds(nblk)):
                    gp.wait_ge(e.s_dve.h, sched[i]["evac_end"][r])
                    for c2 in range(nb):
                        blk = o + c2
                        if kind == "gu":
                            dst = e.wguT[i, :, blk * 128:(blk + 1) * 128]
                        else:
                            dst = e.wdnT[blk, :, (i - 88) * 128:(i - 87) * 128]
                        gp.dma_start(dst, stg[:, c2, :]).then_inc(e.STT[i % 2].h, 16)
        e.STT[0].tick(st_n[0] - st0[0])
        e.STT[1].tick(st_n[1] - st0[1])
        e.LW[0].tick(16 * 44)
        e.LW[1].tick(16 * 44)

        @block.vector
        def _(vector):
            vector = Chained(vector, e.s_dve)
            wait_bar(vector, bar)
            for i in range(NTI):
                kind, W, nblk = tinfo(i)
                if i == 0:
                    vector.wait_ge(e.LX[0].h, lx0 + 32)
                    vector.tensor_copy(gusf[0:16, :], gush[0:16, :])
                    vector.tensor_copy(dnsf[0:44, :], dnsh[0:44, :])
                vector.wait_ge(e.LW[i % 2].h, lw0[i % 2] + 16 * (i // 2 + 1))
                p = pk[i % 2]
                for j in range(8):
                    vector.tensor_scalar(nib[:, :W], p[:, :W], 4 * j, 15,
                                         op0=ALU.logical_shift_right,
                                         op1=ALU.bitwise_and)
                    vector.tensor_scalar(b3s[:, :W], nib[:, :W], 8, 28,
                                         op0=ALU.bitwise_and,
                                         op1=ALU.logical_shift_left)
                    vector.tensor_scalar(rr[:, :W], nib[:, :W], 7, None,
                                         op0=ALU.bitwise_and)
                    vector.tensor_scalar(bits[:, :W], rr[:, :W], 22, None,
                                         op0=ALU.logical_shift_left)
                    vector.tensor_scalar(bits[:, :W], bits[:, :W], F1, None,
                                         op0=ALU.add)
                    vector.tensor_tensor(bits[:, :W], bits[:, :W], b3s[:, :W],
                                         op=ALU.bitwise_or)
                    vector.tensor_scalar(mskf[:, :W], rr[:, :W], 2, None,
                                         op0=ALU.is_lt)
                    vector.tensor_scalar(s1i[:, :W], b3s[:, :W], F2, None,
                                         op0=ALU.bitwise_or)
                    vector.tensor_tensor(alt[:, :W], bits[:, :W].bitcast(F32),
                                         s1i[:, :W].bitcast(F32), op=ALU.subtract)
                    vector.tensor_tensor(alt[:, :W], alt[:, :W], mskf[:, :W],
                                         op=ALU.mult)
                    vector.tensor_tensor(val[:, j:8 * W:8], alt[:, :W],
                                         bits[:, :W].bitcast(F32), op=ALU.add)
                vector.wait_ge(e.s_pe.h, sched[i]["sc_tr"])
                vector.tensor_copy(scb[:, 0:nblk], pss[:, 0:nblk])
                for g2 in range(nblk):
                    vector.tensor_scalar(vbf[:, g2 * 128:(g2 + 1) * 128],
                                         val[:, g2 * 128:(g2 + 1) * 128],
                                         scb[:, g2:g2 + 1], None, op0=ALU.mult)
                for r, (o, nb) in enumerate(rounds(nblk)):
                    vector.wait_ge(e.s_pe.h, sched[i]["pe_round"][r])
                    if r > 0:
                        vector.wait_ge(e.STT[i % 2].h, sched[i]["st_round"][r - 1])
                    elif i >= 1:
                        vector.wait_ge(e.STT[(i - 1) % 2].h,
                                       sched[i - 1]["st_round"][-1])
                    vector.tensor_copy(stg[:, 0:nb, :],
                                       psq[:, :nb * 128]
                                       .rearrange("p (c n) -> p c n", n=128))

        @block.tensor
        def _(tensor):
            wait_bar(tensor, bar)
            for i in range(NTI):
                kind, W, nblk = tinfo(i)
                tensor.wait_ge(e.s_dve.h,
                               sched[i - 1]["scb_end"] if i else dve0 + 2)
                if kind == "gu":
                    tensor.transpose(pss[:, 0:16],
                                     gusf[0:16, i * 128:(i + 1) * 128],
                                     e.idf[0:16, 0:16]).then_inc(e.s_pe.h, 1)
                else:
                    j = i - 88
                    tensor.transpose(pss[:, 0:44],
                                     dnsf[0:44, j * 128:(j + 1) * 128],
                                     e.idf[0:44, 0:44]).then_inc(e.s_pe.h, 1)
                tensor.wait_ge(e.s_dve.h, sched[i]["mults_end"])
                for r, (o, nb) in enumerate(rounds(nblk)):
                    if r > 0:
                        tensor.wait_ge(e.s_dve.h, sched[i]["evac_end"][r - 1])
                    elif i >= 1:
                        tensor.wait_ge(e.s_dve.h, sched[i - 1]["evac_end"][-1])
                    for c2 in range(nb):
                        tensor.transpose(psq[:, c2 * 128:(c2 + 1) * 128],
                                         vbf[:, (o + c2) * 128:(o + c2 + 1) * 128],
                                         e.idb[:]).then_inc(e.s_pe.h, 1)
        e.s_pe.tick(pe_n - pe0)



def emit_gu(e, block):
    nc = e.nc
    NP = 44
    NU = BPC * 4
    bar = bar_of(e)
    dve0, pe0, act0 = e.s_dve.n, e.s_pe.n, e.s_act.n
    lx0 = [e.LX[0].n, e.LX[1].n]
    st0 = [e.STT[0].n, e.STT[1].n]
    PEP = 32 * NU

    def pe_iu(i, u):   return pe0 + PEP * i + 32 * (u + 1)
    def act_iu(i, u):  return act0 + NU * i + u + 1
    def dve_iu(i, u):  return dve0 + NU * i + u + 1
    def st_iu(i, u):
        gi = i * NU + u
        return st0[gi % 2] + 16 * (gi // 2 + 1)

    def lx_pair(i):
        if i % 2 == 0:
            return e.LX[0], lx0[0] + 16 * BPC * NDC + 16 * 2 * (i // 2 + 1)
        return e.LX[1], lx0[1] + 16 * 2 * (i // 2 + 1)

    with ExitStack() as ctx:
        def sb(name, shape, dt=F32):
            return ctx.enter_context(nc.sbuf_tensor("gu" + name, shape, dt))
        xn1 = sb("xn1", [128, BPC * NDC, T], BF16)
        wg = [sb(f"wg{i}", [128, D], BF16) for i in range(2)]
        wu = [sb(f"wu{i}", [128, D], BF16) for i in range(2)]
        sg = [sb(f"sg{i}", [128, 512]) for i in range(2)]
        h2s = [sb(f"h2s{i}", [128, 512], BF16) for i in range(2)]
        psG = [ctx.enter_context(nc.psum_tensor(f"gupsG{i}", [128, 512], F32))
               for i in range(2)]
        psU = [ctx.enter_context(nc.psum_tensor(f"gupsU{i}", [128, 512], F32))
               for i in range(2)]

        @block.sync
        def _(sync):
            wait_bar(sync, bar)
            for b in range(BPC):
                for c in range(NDC):
                    sync.dma_start(xn1[:, b * NDC + c, :], e.xn1T[b, c, :, :]
                                   ).then_inc(e.LX[0].h, 16)
            for i in range(NP):
                if i >= 2:
                    sync.wait_ge(e.s_pe.h, pe_iu(i - 2, NU - 1))
                q = e.LX[i % 2].h
                sync.dma_start(wg[i % 2][:], e.wguT[i, :, :]).then_inc(q, 16)
                sync.dma_start(wu[i % 2][:], e.wguT[i + 44, :, :]).then_inc(q, 16)
        e.LX[0].tick(16 * BPC * NDC + 16 * 2 * ((NP + 1) // 2))
        e.LX[1].tick(16 * 2 * (NP // 2))

        @block.tensor
        def _(tensor):
            wait_bar(tensor, bar)
            for i in range(NP):
                s_, v_ = lx_pair(i)
                tensor.wait_ge(s_.h, v_)
                for u in range(NU):
                    b, ts = divmod(u, 4)
                    if i * NU + u >= 2:
                        pi, pu = divmod(i * NU + u - 2, NU)
                        tensor.wait_ge(e.s_act.h, act_iu(pi, pu))
                        tensor.wait_ge(e.s_dve.h, dve_iu(pi, pu))
                    for c in range(NDC):
                        tensor.matmul(psG[u % 2][:],
                                      wg[i % 2][:, c * 128:(c + 1) * 128],
                                      xn1[:, b * NDC + c, ts * 512:(ts + 1) * 512],
                                      start=(c == 0), stop=(c == NDC - 1)
                                      ).then_inc(e.s_pe.h, 1)
                    for c in range(NDC):
                        tensor.matmul(psU[u % 2][:],
                                      wu[i % 2][:, c * 128:(c + 1) * 128],
                                      xn1[:, b * NDC + c, ts * 512:(ts + 1) * 512],
                                      start=(c == 0), stop=(c == NDC - 1)
                                      ).then_inc(e.s_pe.h, 1)
        e.s_pe.tick(PEP * NP)

        @block.scalar
        def _(scalar):
            scalar = Chained(scalar, e.s_act)
            wait_bar(scalar, bar)
            for i in range(NP):
                for u in range(NU):
                    scalar.wait_ge(e.s_pe.h, pe0 + PEP * i + 32 * u + 16)
                    if i * NU + u >= 2:
                        pi, pu = divmod(i * NU + u - 2, NU)
                        scalar.wait_ge(e.s_dve.h, dve_iu(pi, pu))
                    scalar.activation(sg[u % 2][:], psG[u % 2][:], AF.Silu)

        @block.vector
        def _(vector):
            vector = Chained(vector, e.s_dve)
            wait_bar(vector, bar)
            for i in range(NP):
                for u in range(NU):
                    gi = i * NU + u
                    vector.wait_ge(e.s_act.h, act_iu(i, u))
                    vector.wait_ge(e.s_pe.h, pe_iu(i, u))
                    if gi >= 2:
                        vector.wait_ge(e.STT[gi % 2].h, st_iu(i, u) - 16)
                    vector.tensor_tensor(h2s[u % 2][:], sg[u % 2][:],
                                         psU[u % 2][:], op=ALU.mult)

        @block.gpsimd
        def _(gp):
            wait_bar(gp, bar)
            for i in range(NP):
                for u in range(NU):
                    b, ts = divmod(u, 4)
                    gi = i * NU + u
                    gp.wait_ge(e.s_dve.h, dve_iu(i, u))
                    gp.dma_start(e.h2[i, :, b * T + ts * 512:b * T + (ts + 1) * 512],
                                 h2s[u % 2][:]).then_inc(e.STT[gi % 2].h, 16)
        e.STT[0].tick(16 * (NP * NU // 2))
        e.STT[1].tick(16 * (NP * NU // 2))



def emit_down(e, block):
    nc = e.nc
    NH2 = 44
    with ExitStack() as ctx:
        def sb(name, shape, dt=F32):
            return ctx.enter_context(nc.sbuf_tensor("dn" + name, shape, dt))
        h2c = sb("h2c", [128, NH2, 512], BF16)
        wq = sb("wq", [128, NH2, 512], BF16)
        ybuf = sb("ybuf", [128, 512])
        ytok = [sb(f"ytok{i}", [128, D]) for i in range(4)]
        junk = sb("junk", [128, D], BF16)
        red = sb("red", [128, 4])
        g1 = sb("g1", [128, D])
        s1c = sb("s1c", [128, NDC, 128])
        s1tok = sb("s1tok", [128, D])
        x1t = sb("x1t", [128, D])
        outt = [sb(f"outt{i}", [128, D]) for i in range(2)]
        psm = [ctx.enter_context(nc.psum_tensor(f"dnpsm{i}", [128, 512], F32))
               for i in range(2)]
        psf = ctx.enter_context(nc.psum_tensor("dnpsf", [128, 512], F32))
        psf2 = ctx.enter_context(nc.psum_tensor("dnpsf2", [128, 2048], F32))

        for b in range(BPC):
            for tsb in range(4):
                gidx = b * 4 + tsb
                bar = bar_of(e)
                dve0, pe0, act0 = e.s_dve.n, e.s_pe.n, e.s_act.n
                lx0 = [e.LX[0].n, e.LX[1].n]
                st0 = [e.STT[0].n, e.STT[1].n]

                def fdve(tt, k):  return dve0 + 80 + 5 * tt + k
                def fact(tt, k):  return act0 + 3 * tt + k
                def fst(tt):
                    return st0[tt % 2] + 16 * (tt // 2 + 1)

                @block.sync
                def _(sync):
                    for s_h, n in bar:
                        if n:
                            sync.wait_ge(s_h.h, n)
                    for c in range(NH2):
                        sync.dma_start(
                            h2c[:, c, :],
                            e.h2[c, :, b * T + tsb * 512:b * T + (tsb + 1) * 512]
                        ).then_inc(e.LX[0].h, 16)
                    for q in range(4):
                        if q > 0:
                            sync.wait_ge(e.s_pe.h, pe0 + 192 * q)
                        for c in range(NH2):
                            sync.dma_start(wq[:, c, :],
                                           e.wdnT[c, :, q * 512:(q + 1) * 512]
                                           ).then_inc(e.LX[1].h, 16)
                    for tt in range(4):
                        t0 = tsb * 512 + tt * 128
                        if tt >= 1:
                            sync.wait_ge(e.s_pe.h, pe0 + 768 + 16 * tt)
                            sync.wait_ge(e.s_dve.h, fdve(tt - 1, 5))
                        sync.dma_start(s1c[:, :, :],
                                       e.s1[b, :, :, t0:t0 + 128]
                                       .transpose([1, 0, 2])).then_inc(e.LX[0].h, 16)
                        sync.dma_start(x1t[:], e.x1[b, t0:t0 + 128, :]
                                       ).then_inc(e.LX[0].h, 16)
                e.LX[0].tick(16 * (NH2 + 8))
                e.LX[1].tick(16 * 4 * NH2)

                @block.tensor
                def _(tensor):
                    for s_h, n in bar:
                        if n:
                            tensor.wait_ge(s_h.h, n)
                    tensor.wait_ge(e.LX[0].h, lx0[0] + 16 * NH2)
                    for q in range(4):
                        tensor.wait_ge(e.LX[1].h, lx0[1] + 16 * NH2 * (q + 1))
                        for dq_ in range(4):
                            g = q * 4 + dq_
                            if g >= 2:
                                tensor.wait_ge(e.s_dve.h, dve0 + 5 * (g - 2) + 1)
                            for c in range(NH2):
                                tensor.matmul(psm[g % 2][:],
                                              wq[:, c, dq_ * 128:(dq_ + 1) * 128],
                                              h2c[:, c, :],
                                              start=(c == 0), stop=(c == NH2 - 1)
                                              ).then_inc(e.s_pe.h, 1)
                            tensor.wait_ge(e.s_dve.h, dve0 + 5 * g + 1)
                            if g >= 1:
                                tensor.wait_ge(e.s_dve.h, dve0 + 5 * g)
                            for tt in range(4):
                                tensor.transpose(psf[:, tt * 128:(tt + 1) * 128],
                                                 ybuf[:, tt * 128:(tt + 1) * 128],
                                                 e.idf[:]).then_inc(e.s_pe.h, 1)
                    for tt in range(4):
                        tensor.wait_ge(e.LX[0].h,
                                       lx0[0] + 16 * (NH2 + 2 * (tt + 1)))
                        if tt >= 1:
                            tensor.wait_ge(e.s_dve.h, fdve(tt - 1, 3))
                        for c in range(NDC):
                            tensor.transpose(psf2[:, c * 128:(c + 1) * 128],
                                             s1c[:, c, :], e.idf[:]
                                             ).then_inc(e.s_pe.h, 1)
                e.s_pe.tick(192 * 4 + 64)

                @block.scalar
                def _(scalar):
                    scalar = Chained(scalar, e.s_act)
                    for s_h, n in bar:
                        if n:
                            scalar.wait_ge(s_h.h, n)
                    for tt in range(4):
                        scalar.wait_ge(e.s_dve.h, dve0 + 80)
                        if tt >= 1:
                            scalar.wait_ge(e.s_dve.h, fdve(tt - 1, 5))
                        scalar.activation(junk[:], ytok[tt][:], AF.Square,
                                          accum_out=red[:, 0:1])
                        scalar.wait_ge(e.s_dve.h, fdve(tt, 1))
                        scalar.activation(red[:, 2:3], red[:, 1:2], AF.Sqrt)
                        scalar.wait_ge(e.s_dve.h, fdve(tt, 2))
                        scalar.activation(g1[:], ytok[tt][:], AF.Silu,
                                          scale=red[:, 3:4])

                @block.vector
                def _(vector):
                    vector = Chained(vector, e.s_dve)
                    for s_h, n in bar:
                        if n:
                            vector.wait_ge(s_h.h, n)
                    for q in range(4):
                        for dq_ in range(4):
                            g = q * 4 + dq_
                            vector.wait_ge(e.s_pe.h, pe0 + 48 * g + 44)
                            vector.tensor_copy(ybuf[:], psm[g % 2][:])
                            vector.wait_ge(e.s_pe.h, pe0 + 48 * (g + 1))
                            for tt in range(4):
                                dc = (q * 4 + dq_) * 128
                                vector.tensor_copy(ytok[tt][:, dc:dc + 128],
                                                   psf[:, tt * 128:(tt + 1) * 128])
                    for tt in range(4):
                        vector.wait_ge(e.s_act.h, fact(tt, 1))
                        vector.tensor_scalar(red[:, 1:2], red[:, 0:1], 1.0 / D,
                                             1e-6, op0=ALU.mult, op1=ALU.add)
                        vector.wait_ge(e.s_act.h, fact(tt, 2))
                        vector.reciprocal(red[:, 3:4], red[:, 2:3])
                        vector.wait_ge(e.s_pe.h, pe0 + 768 + 16 * (tt + 1))
                        vector.tensor_copy(s1tok[:], psf2[:])
                        vector.wait_ge(e.s_act.h, fact(tt, 3))
                        if tt >= 2:
                            vector.wait_ge(e.STT[tt % 2].h, fst(tt - 2))
                        vector.tensor_tensor(outt[tt % 2][:], s1tok[:], g1[:],
                                             op=ALU.mult)
                        vector.tensor_tensor(outt[tt % 2][:], outt[tt % 2][:],
                                             x1t[:], op=ALU.add)

                @block.gpsimd
                def _(gp):
                    for tt in range(4):
                        t0 = tsb * 512 + tt * 128
                        fi = gidx * 4 + tt
                        gp.wait_ge(e.s_dve.h, fdve(tt, 5))
                        gp.dma_start(e.out[b, t0:t0 + 128, :], outt[tt % 2][:]
                                     ).then_inc(e.STT[tt % 2].h, 16)
                e.STT[0].tick(16 * 2)
                e.STT[1].tick(16 * 2)


def build_program(phases=ALL_PHASES, debug_out=()):
    nc = bass.Bass("TRN2", target_bir_lowering=False)
    e = Env()
    e.nc = nc

    def din(name, shape, dt=F32):
        return nc.dram_tensor(name, list(shape), dt, kind="ExternalInput").ap()

    e.x = din("x", [BPC, T, D])
    e.w_kvd = din("w_kvd", [D, KVC])
    e.w_upk = din("w_upk", [KVC, D])
    e.w_upv = din("w_upv", [KVC, D])
    e.w_qd = din("w_qd", [D, QC])
    e.w_qu = din("w_qu", [QC, D])
    e.w_kr = din("w_kr", [D, HD])
    e.w_qr = din("w_qr", [D, HD])
    e.w_out = din("w_out", [D, D])
    e.w_fc = din("w_fc", [FFH, D])
    e.w_proj = din("w_proj", [D, FFH])
    e.Wk = din("Wk", [D, D])
    e.Wv = din("Wv", [D, D])
    e.Wr = din("Wr", [D, D])
    e.Wo = din("Wo", [D, D])
    for v in ("bk", "bv", "br", "bo", "tmk", "tmv", "tmr", "td", "tf"):
        setattr(e, v, din(v, [128, NDC]))
    e.gup = din("gup", [2 * I1, D // 8], I32)
    e.gus = din("gus", [D // 128, 2 * I1], F16)
    e.dnp = din("dnp", [D, I1 // 8], I32)
    e.dns = din("dns", [I1 // 128, D], F16)
    e.idb_d = din("idb", [128, 128], BF16)
    e.idf_d = din("idf", [128, 128], F32)
    e.cos2_d = din("cos2", [128, T])
    e.sin2_d = din("sin2", [128, T])
    e.onescol_d = din("onescol", [128, 1])
    e.onesrow_d = din("onesrow", [1, 128])

    e.out = nc.dram_tensor("out", [BPC, T, D], F32, kind="ExternalOutput").ap()

    def scratch(name, shape, dt=F32):
        if name in debug_out:
            return nc.dram_tensor(name, list(shape), dt, kind="ExternalOutput").ap()
        return nc.dram_tensor(name, list(shape), dt, kind="Internal").ap()

    e.xnT = scratch("xnT", [BPC, NDC, 128, T], BF16)
    e.hmid = scratch("hmid", [FFH // 128, 128, BPC * T], BF16)
    e.g0 = scratch("g0", [BPC, NDC, 128, T])
    e.s0 = scratch("s0", [BPC, NDC, 128, T])
    e.outTd = scratch("outTd", [BPC, HEADS, 128, T], BF16)
    e.x1 = scratch("x1", [BPC, T, D])
    e.xn1T = scratch("xn1T", [BPC, NDC, 128, T], BF16)
    e.kT = scratch("kT", [BPC, NDC, 128, T])
    e.vT = scratch("vT", [BPC, NDC, 128, T])
    e.rT = scratch("rT", [BPC, NDC, 128, T])
    e.rwkvT = scratch("rwkvT", [BPC, NDC, 128, T], BF16)
    e.s1 = scratch("s1", [BPC, NDC, 128, T])
    e.wguT = scratch("wguT", [2 * I1 // 128, 128, D], BF16)
    e.wdnT = scratch("wdnT", [I1 // 128, 128, D], BF16)
    e.h2 = scratch("h2", [I1 // 128, 128, BPC * T], BF16)
    e.tern = scratch("tern", [1, 8])

    with ExitStack() as ctx:
        e.top_ctx = ctx
        e.idb = ctx.enter_context(nc.sbuf_tensor("c_idb", [128, 128], BF16))
        e.idf = ctx.enter_context(nc.sbuf_tensor("c_idf", [128, 128], F32))
        e.onescol = ctx.enter_context(nc.sbuf_tensor("c_onescol", [128, 1], F32))
        e.onesrow = ctx.enter_context(nc.sbuf_tensor("c_onesrow", [1, 128], F32))
        for nm in ("thrfc", "nthrfc", "thrpj", "nthrpj", "amtot"):
            setattr(e, nm, ctx.enter_context(nc.sbuf_tensor("c_" + nm, [128, 1], F32)))
        e.vecs = ctx.enter_context(nc.sbuf_tensor("c_vecs", [128, 9, NDC], F32))
        e.dvecs = ctx.enter_context(nc.sbuf_tensor("c_dvecs", [128, 6, NDC], F32))
        for nm in ("s_pe", "s_dve", "s_act", "s_gp"):
            setattr(e, nm, S(ctx, nc, nm))
        e.LX = [S(ctx, nc, f"lx{i}") for i in range(2)]
        e.LW = [S(ctx, nc, f"lw{i}") for i in range(2)]
        e.STT = [S(ctx, nc, f"st{i}") for i in range(2)]
        e.STX = [S(ctx, nc, f"sx{i}") for i in range(2)]

        block = ctx.enter_context(nc.Block())

        @block.sync
        def _(sync):
            for dst, src in [(e.idb, e.idb_d), (e.idf, e.idf_d),
                             (e.onescol, e.onescol_d), (e.onesrow, e.onesrow_d)]:
                sync.dma_start(dst[:], src[:]).then_inc(e.LX[0].h, 16)
        e.LX[0].tick(16 * 4)

        if "ln0" in phases:
            emit_ln(e, block, e.x, e.xnT, tag="ln0")
        if "tstat" in phases:
            emit_tstat(e, block)
        if "ffn0h" in phases:
            emit_ffn0h(e, block)
        if "ffn0p" in phases:
            emit_ffn0p(e, block)
        if "attn" in phases:
            emit_attn(e, block)
        if "comb0" in phases:
            emit_ln(e, block, e.x, e.xn1T, dst_x_ap=e.x1,
                    s0g0=(e.s0, e.g0), tag="cm0")
        if "kvr" in phases or "wkv" in phases or "wo" in phases:
            emit_l1prep(e, block)
        if "kvr" in phases:
            emit_kvr(e, block)
        if "wkv" in phases:
            emit_wkv(e, block)
        if "wo" in phases:
            emit_wo(e, block)
        if "dq" in phases:
            emit_dq(e, block)
        if "gu" in phases:
            emit_gu(e, block)
        if "down" in phases:
            emit_down(e, block)

        @block.sync
        def _(sync):
            for s in e.STT + e.STX + e.LW + e.LX:
                if s.n:
                    sync.wait_ge(s.h, s.n)
    return nc



def make_consts():
    idb = np.eye(128).astype(ml_dtypes.bfloat16)
    idf = np.eye(128).astype(np.float32)
    j = np.arange(64)
    inv = (10000.0 ** (-(2.0 * j) / 128.0)).astype(np.float64)
    t = np.arange(T, dtype=np.float64)
    ang = np.outer(inv, t)
    cos2 = np.repeat(np.cos(ang), 2, axis=0).astype(np.float32)
    sin = np.sin(ang)
    sin2 = np.empty((128, T), np.float64)
    sin2[0::2] = -sin
    sin2[1::2] = sin
    sin2 = sin2.astype(np.float32)
    onescol = np.ones((128, 1), np.float32)
    onesrow = np.ones((1, 128), np.float32)
    return dict(idb=idb, idf=idf, cos2=cos2, sin2=sin2,
                onescol=onescol, onesrow=onesrow)


def vec_2d(v):
    return np.ascontiguousarray(np.asarray(v).reshape(NDC, 128).T.astype(np.float32))


def make_in_maps(inputs):
    consts = make_consts()
    common = dict(
        w_kvd=inputs["L0_attn_w_kv_down"], w_upk=inputs["L0_attn_w_kv_up_k"],
        w_upv=inputs["L0_attn_w_kv_up_v"], w_qd=inputs["L0_attn_w_q_down"],
        w_qu=inputs["L0_attn_w_q_up"], w_kr=inputs["L0_attn_w_k_rope"],
        w_qr=inputs["L0_attn_w_q_rope"], w_out=inputs["L0_attn_w_out"],
        w_fc=inputs["L0_ffn_w_fc"], w_proj=inputs["L0_ffn_w_proj"],
        Wk=inputs["L1_attn_Wk"], Wv=inputs["L1_attn_Wv"],
        Wr=inputs["L1_attn_Wr"], Wo=inputs["L1_attn_Wo"],
        bk=vec_2d(inputs["L1_attn_bk"]), bv=vec_2d(inputs["L1_attn_bv"]),
        br=vec_2d(inputs["L1_attn_br"]), bo=vec_2d(inputs["L1_attn_bo"]),
        tmk=vec_2d(inputs["L1_attn_tmk"]), tmv=vec_2d(inputs["L1_attn_tmv"]),
        tmr=vec_2d(inputs["L1_attn_tmr"]), td=vec_2d(inputs["L1_attn_time_decay"]),
        tf=vec_2d(inputs["L1_attn_time_first"]),
        gup=inputs["L1_ffn_gate_up_packed"], gus=inputs["L1_ffn_gate_up_scales"],
        dnp=inputs["L1_ffn_down_packed"], dns=inputs["L1_ffn_down_scales"],
        **consts,
    )
    common = {k: np.ascontiguousarray(v) for k, v in common.items()}
    x = np.asarray(inputs["L0_attn_x"])
    in_maps = []
    for c in range(NCORES):
        m = dict(common)
        m["x"] = np.ascontiguousarray(x[c * BPC:(c + 1) * BPC])
        in_maps.append(m)
    return in_maps


def kernel(**inputs):
    in_maps = make_in_maps(inputs)
    nc = build_program()
    res = run_bass_kernel_spmd(nc, in_maps, core_ids=list(range(NCORES)))
    return np.concatenate([np.asarray(r["out"]) for r in res.results], axis=0)
